# version 17
# speedup vs baseline: 1.9310x; 1.0708x over previous
import os
import numpy as np
import ml_dtypes
import concourse.bass as bass
import concourse.mybir as mybir
from concourse.bass_utils import run_bass_kernel_spmd
from contextlib import ExitStack

F32 = mybir.dt.float32
BF16 = mybir.dt.bfloat16
OP = mybir.AluOpType
ACT = mybir.ActivationFunctionType
bfdt = ml_dtypes.bfloat16

D = 2048
T = 2048
NT = 16
IC = 16
H1, KVH1 = 16, 4
H2, DH, CBS = 32, 64, 64
SCALE2 = float(DH ** -0.5)
NEG = -1.0e30
EPS0 = 1e-6
EPSLN = 1e-5
EPSRMS = float(np.finfo(np.float32).eps)
PAIRS = [[0, 1], [2, 3], [4, 5], [6, 7]]

TAPS = [t for t in os.environ.get("SS_TAPS", "").split(",") if t]


def _is_ap(x):
    return hasattr(x, "tensor") and hasattr(x, "ap") and hasattr(x, "dtype")


class EW:

    def __init__(self, K, name, eng):
        self.K, self.name, self.e = K, name, eng
        self.last = None
        if name == "gpsimd":
            self.dma_sem = K.GD[K.dma_rr["gpsimd"] % len(K.GD)]
        elif name == "sync":
            self.dma_sem = K.YD[K.dma_rr["sync"] % len(K.YD)]
        else:
            self.dma_sem = None
        self.entry_count = K.n.get(self.dma_sem, 0)
        self.gated = False
        self.t_drained = False

    def wait_ge(self, sem, val):
        return self.e.wait_ge(sem, val)

    def nop(self):
        return self.e.nop()

    def drain(self):
        self.K.dirty_w[self.name].clear()
        self.K.dirty_r[self.name].clear()
        return self.e.drain()

    def __getattr__(self, op):
        fn = getattr(self.e, op)

        def wrap(*a, **kw):
            K, nm = self.K, self.name
            if op == "dma_start":
                key = self.dma_sem
                if not self.gated:
                    if self.entry_count > 0:
                        self.e.wait_ge(K.sem[key], self.entry_count * 16)
                    self.gated = True
                    K.dma_rr[nm] += 1
                r = fn(*a, **kw)
                r.then_inc(K.sem[key], 16)
                K.n[key] += 1
                return r
            if op == "collective_compute":
                r = fn(*a, **kw)
                r.then_inc(K.sem["c"], 1)
                K.n["c"] += 1
                return r
            outs, ins = [], []
            pos = list(a)
            if pos and _is_ap(pos[0]):
                outs.append(pos[0])
                pos = pos[1:]
            for x in pos:
                if _is_ap(x):
                    ins.append(x)
            for k2, v2 in kw.items():
                if _is_ap(v2):
                    (outs if k2 in ("out", "accum_out") else ins).append(v2)
            in_names = {x.tensor.name for x in ins}
            out_names = {x.tensor.name for x in outs}
            dw, dr = K.dirty_w[nm], K.dirty_r[nm]
            if (in_names & dw) or (out_names & (dw | dr)):
                dw.clear()
                dr.clear()
                mode = self.K.t_drain_mode if nm == "tensor" else "full"
                if (mode == "full" or (mode == "section" and not self.t_drained)):
                    self.e.drain()
                    self.t_drained = True
                elif nm == "tensor" and mode == "none":
                    pass
            dr |= in_names
            dw |= out_names
            r = fn(*a, **kw)
            self.last = r
            return r

        return wrap


class K:
    ENGS = ("tensor", "vector", "scalar", "gpsimd", "sync")

    def __init__(self, nc, block, stack):
        self.nc, self.block = nc, block
        self.sem = {}
        self.n = {}
        for s in ("t", "v", "s", "g", "c",
                  "gd0", "gd1", "gd2", "gd3", "yd0", "yd1"):
            self.sem[s] = stack.enter_context(nc.semaphore("sem_" + s))
            self.n[s] = 0
        self.GD = ("gd0", "gd1", "gd2", "gd3")
        self.YD = ("yd0", "yd1")
        self.dma_rr = {"gpsimd": 0, "sync": 0}
        self.t_drain_mode = "full"
        self.dirty_w = {e: set() for e in self.ENGS}
        self.dirty_r = {e: set() for e in self.ENGS}
        self.esem = {"tensor": "t", "vector": "v", "scalar": "s",
                     "gpsimd": "g", "sync": None}

    def sec(self, engname, body, waits=(), inc=True):
        cap = []
        for w in waits:
            if isinstance(w, str):
                if w == "gd":
                    cap += [(x, self.n[x]) for x in self.GD]
                elif w == "yd":
                    cap += [(x, self.n[x]) for x in self.YD]
                else:
                    cap.append((w, self.n[w]))
            elif w is None:
                pass
            else:
                nm2, c = w
                if c is not None and c > 0:
                    cap.append((nm2, min(c, self.n[nm2])))
        sem = self.sem
        esem = self.esem[engname]

        def f(e):
            w = EW(self, engname, e)
            for nm2, c in cap:
                tgt = c * (16 if nm2[0] in "gy" else 1)
                if tgt > 0:
                    e.wait_ge(sem[nm2], tgt)
            body(w)
            if inc and esem is not None:
                if w.last is not None:
                    w.last.then_inc(sem[esem], 1)
                else:
                    e.nop().then_inc(sem[esem], 1)
                self.n[esem] += 1

        getattr(self.block, engname)(f)
        return self.n[esem] if esem is not None else None


def build_program(taps=()):
    nc = bass.Bass("TRN2", target_bir_lowering=False, debug=False)

    def inp(name, shape, dt=F32):
        return nc.dram_tensor(name, shape, dt, kind="ExternalInput").ap()

    x_in = inp("x_in", [T, D])
    wqT = inp("wqT", [D, D])
    wkT = inp("wkT", [D, 512])
    wvT = inp("wvT", [D, 512])
    woT = inp("woT", [D, D])
    bqc = inp("bqc", [128, IC])
    bk_row = inp("bk_row", [1, 512])
    bv_row = inp("bv_row", [1, 512])
    f0wT = inp("f0wT", [D, D])
    f0scale = inp("f0scale", [1, D])
    f0bias = inp("f0bias", [1, D])
    wqkv = inp("wqkv", [D, 3 * D])
    wcomb = inp("wcomb", [D, 96])
    bcomb_row = inp("bcomb_row", [1, 96])
    wout1 = inp("wout1", [D, D])
    rmswc = inp("rmswc", [128, IC])
    kc_h = inp("kc_h", [H2, 32, 128, 64])
    vc_h = inp("vc_h", [H2, 32, 128, 64])
    kcb_row = inp("kcb_row", [1, H2 * 64])
    vcb_row = inp("vcb_row", [1, H2 * 64])
    kpos_eo = inp("kpos_eo", [H2, 128, 32])
    vpos_eo = inp("vpos_eo", [H2, 128, 32])
    memkT = inp("memkT", [128, 64])
    memv_row = inp("memv_row", [4, H2 * 64])
    pT = inp("pT", [D, 4 * D])
    nT = inp("nT", [D, 4 * D])
    pb_row = inp("pb_row", [1, 4 * D])
    nb_row = inp("nb_row", [1, 4 * D])
    exps_row = inp("exps_row", [1, 4])
    bexps_row = inp("bexps_row", [1, 4])
    maskw_row = inp("maskw_row", [1, 4])
    scale1 = inp("scale1", [1, 1])
    bscale1 = inp("bscale1", [1, 1])
    ident_in = inp("ident_c", [128, 128])
    ones_row = inp("ones_row", [1, 512])
    ones_col = inp("ones_col", [128, 1])
    ones_col8 = inp("ones_col8", [128, 8])
    mloc_cur_in = inp("mloc_cur", [128, 512])
    mloc_prev_in = inp("mloc_prev", [128, 512])
    mloc_prev0_in = inp("mloc_prev0", [128, 512])
    mcomp_in = inp("mcomp", [NT, 68, 512])

    out_res = nc.dram_tensor("out_res", [T, D], F32, kind="ExternalOutput").ap()

    def scr(name, shape, dt):
        return nc.dram_tensor(name, shape, dt).ap()

    R_a = scr("R_a", [T, D], F32)
    R_b = scr("R_b", [T, D], F32)
    R_c = scr("R_c", [T, D], F32)
    XT1_h = scr("XT1_h", [NT, 128, D], BF16)
    XT3_h = scr("XT3_h", [NT, 128, D], BF16)
    Q_h = scr("Q_h", [IC, 128, T], BF16)
    KT_h = scr("KT_h", [IC, 128, T], BF16)
    VTF_h = scr("VTF_h", [IC, 128, T], BF16)
    V_h = scr("V_h", [NT, 128, D], BF16)
    LOC_h = scr("LOC_h", [NT, 128, D], BF16)
    WT_h = scr("WT_h", [IC, 128, D], BF16)
    cin_kv = scr("cin_kv", [128, 516], F32)
    cout_kv = scr("cout_kv", [128, 516], F32)
    cin_bd = scr("cin_bd", [128, 4096], BF16)
    cout_bd = scr("cout_bd", [256, 4096], BF16)
    cin_ck = scr("cin_ck", [32, 4096], BF16)
    cout_ck = scr("cout_ck", [64, 4096], BF16)

    tap_aps = {}

    def tapout(name, shape, dt=F32):
        if name in taps:
            tap_aps[name] = nc.dram_tensor(
                "tap_" + name, shape, dt, kind="ExternalOutput").ap()
            return tap_aps[name]
        return None

    with ExitStack() as ST:
        k = K(nc, ST.enter_context(nc.Block()), ST)

        def sb(name, shape, dt=F32, st=None, side=None):
            return (st or ST).enter_context(nc.sbuf_tensor(name, shape, dt, side=side))

        def psum(name, shape, dt=F32, st=None):
            return (st or ST).enter_context(nc.psum_tensor(name, shape, dt))

        ident = sb("ident", [128, 128], BF16)
        ones_bf = sb("ones_bf", [1, 512], BF16)
        onescol_bf = sb("onescol_bf", [128, 1], BF16)
        onescol8 = sb("onescol8", [128, 8], BF16)
        ones_f = sb("ones_f", [1, 512], F32)
        bn6 = sb("bn6", [128, 24])
        mv = sb("mv", [128, 2])
        c_inv = sb("c_inv", [128, 1])
        c_tmp = sb("c_tmp", [128, 1])
        c_nmi = sb("c_nmi", [128, 1])
        kv_bf = sb("kv_bf", [128, 512], BF16)
        ksum_bf = sb("ksum_bf", [128, 4], BF16)

        k.sec("gpsimd", lambda g: (
            g.dma_start(out=ident[:, :], in_=ident_in),
            g.dma_start(out=ones_bf[:, :], in_=ones_row),
            g.dma_start(out=onescol_bf[:, :], in_=ones_col),
            g.dma_start(out=onescol8[:, :], in_=ones_col8)))
        k.sec("sync", lambda y: y.dma_start(out=ones_f[:, :], in_=ones_row))

        def emit_stats(v, src, eps, mean):
            for i in range(4):
                v.bn_stats(bn6[:, 6 * i:6 * (i + 1)], src[:, 512 * i:512 * (i + 1)])
            v.bn_aggr(mv[:, :], bn6[:, :])
            if mean:
                v.tensor_scalar(c_tmp[:, :], mv[:, 1:2], 1.0, eps,
                                op0=OP.mult, op1=OP.add)
            else:
                v.tensor_tensor(c_tmp[:, :], mv[:, 0:1], mv[:, 0:1], op=OP.mult)
                v.tensor_tensor(c_tmp[:, :], c_tmp[:, :], mv[:, 1:2], op=OP.add)
                v.tensor_scalar(c_tmp[:, :], c_tmp[:, :], 1.0, eps,
                                op0=OP.mult, op1=OP.add)

        def emit_inv(v, mean):
            v.reciprocal(c_inv[:, :], c_tmp[:, :])
            if mean:
                v.tensor_scalar(c_nmi[:, :], mv[:, 0:1], c_inv[:, 0:1], -1.0,
                                op0=OP.mult, op1=OP.mult)
            else:
                v.memset(c_nmi[:, :], 0.0)

        def norm_cols(waits1, src_sb, eps, mean):
            k.sec("vector", lambda v: emit_stats(v, src_sb, eps, mean), waits=waits1)
            k.sec("scalar", lambda s: s.activation(c_tmp[:, :], c_tmp[:, :], ACT.Sqrt),
                  waits=("v",))
            k.sec("vector", lambda v: emit_inv(v, mean), waits=("s",))

        def transpose_tile(bufs, src_bf, dst_fn, twaits, vwaits=("t",)):
            k.sec("tensor", lambda t: [
                t.transpose(bufs[ic // 8][:, (ic % 8) * 128:(ic % 8 + 1) * 128],
                            src_bf[:, ic * 128:(ic + 1) * 128], ident[:, :])
                for ic in range(IC)], waits=twaits)
            k.sec("vector", lambda v: [dst_fn(v, hb, bufs[hb][:, :]) for hb in range(2)],
                  waits=vwaits)



        def fence():
            cnt = dict(k.n)
            for e in ("tensor", "vector", "scalar", "gpsimd", "sync"):
                own = k.esem[e]
                ws = tuple(x for x in ("t", "v", "s", "gd", "yd", "c")
                           if x != own)
                k.sec(e, lambda w: w.drain() if w.name in
                      ("tensor", "vector", "scalar") else w.nop(), waits=ws)

        def dsec(engname, body, waits=()):
            rr = k.dma_rr[engname]
            names = k.GD if engname == "gpsimd" else k.YD
            nm2 = names[rr % len(names)]
            k.sec(engname, body, waits=waits)
            return (nm2, k.n[nm2])

        def tap_dram(tpap, dram_ap, nrows, ncols, dt):
            fence()
            with ExitStack() as stt:
                stage = sb("tapstage_%d" % k.n["gd"], [128, ncols], dt, st=stt)
                nt2 = (nrows + 127) // 128
                for r in range(nt2):
                    h2 = min(128, nrows - r * 128)
                    k.sec("gpsimd", lambda g, r=r, h2=h2: g.dma_start(
                        out=stage[0:h2, :], in_=dram_ap[r * 128:r * 128 + h2, :]),
                        waits=("gd",))
                    k.sec("gpsimd", lambda g, r=r, h2=h2: g.dma_start(
                        out=tpap[r * 128:r * 128 + h2, :], in_=stage[0:h2, :]),
                        waits=("gd",))
        s_xt0 = ExitStack()
        s_att = ExitStack()
        s_xt2 = ExitStack()
        s_cw = ExitStack()
        s_wt = ExitStack()

        fence()
        xt0 = sb("xt0", [128, NT * D], BF16, st=s_xt0)
        with ExitStack() as st1:
            xin_b = [sb("xin%d" % i, [128, D], st=st1) for i in range(2)]
            xhat1b = sb("xhat_p1", [128, D], BF16, st=st1)
            ptb1 = [psum("pt1%d" % i, [128, 1024], BF16, st=st1) for i in range(2)]
            for tc in range(NT):
                bi = tc % 2
                k.sec("sync", lambda y, tc=tc, bi=bi: y.dma_start(
                    out=xin_b[bi][:, :], in_=x_in[tc * 128:(tc + 1) * 128, :]),
                    waits=("s",))
                norm_cols(("yd",), xin_b[bi], EPSLN, True)
                k.sec("scalar", lambda s, bi=bi: s.activation(
                    xhat1b[:, :], xin_b[bi][:, :], ACT.Identity,
                    bias=c_nmi[:, 0:1], scale=c_inv[:, 0:1]), waits=("v",))
                transpose_tile(ptb1, xhat1b,
                               lambda v, hb, ap, tc=tc: v.tensor_copy(
                                   xt0[:, tc * D + hb * 1024: tc * D + (hb + 1) * 1024],
                                   ap), ("s",))
        tp = tapout("xt0", [128, NT * D], BF16)
        if tp is not None:
            k.sec("gpsimd", lambda g: g.dma_start(out=tp, in_=xt0[:, :]), waits=("v",))

        fence()
        with ExitStack() as st2:
            wk_sb = sb("wk_sb", [128, IC * 512], BF16, st=st2)
            wv_sb = sb("wv_sb", [128, IC * 512], BF16, st=st2)
            bkr = sb("bkr", [1, 512], BF16, st=st2)
            bvr = sb("bvr", [1, 512], BF16, st=st2)
            sk_sb = sb("sk_sb", [128, NT * 512], BF16, st=st2)
            v_sb = sb("v_sb", [128, NT * 512], BF16, st=st2)
            r_bf = sb("r_bf2", [128, 512], BF16, st=st2)
            mel = sb("mel2", [128, 512], st=st2)
            ee = sb("ee2", [128, 512], st=st2)
            ps_k = psum("ps_k", [128, 512], st=st2)
            ps_v = psum("ps_v", [128, 512], st=st2)

            k.sec("gpsimd", lambda g: (
                g.dma_start(out=wk_sb[:, :].rearrange("p (ic o) -> p ic o", ic=IC),
                            in_=wkT.rearrange("(ic p) o -> p ic o", p=128)),
                g.dma_start(out=wv_sb[:, :].rearrange("p (ic o) -> p ic o", ic=IC),
                            in_=wvT.rearrange("(ic p) o -> p ic o", p=128)),
                g.dma_start(out=bkr[:, :], in_=bk_row),
                g.dma_start(out=bvr[:, :], in_=bv_row)))
            for tc in range(NT):
                k.sec("tensor", lambda t, tc=tc: (
                    [t.matmul(ps_k[:, :],
                              xt0[:, tc * D + ic * 128: tc * D + (ic + 1) * 128],
                              wk_sb[:, ic * 512:(ic + 1) * 512],
                              start=(ic == 0), stop=False) for ic in range(IC)],
                    t.matmul(ps_k[:, :], ones_bf[0:1, 0:128], bkr[0:1, :],
                             start=False, stop=True),
                    [t.matmul(ps_v[:, :],
                              xt0[:, tc * D + ic * 128: tc * D + (ic + 1) * 128],
                              wv_sb[:, ic * 512:(ic + 1) * 512],
                              start=(ic == 0), stop=False) for ic in range(IC)],
                    t.matmul(ps_v[:, :], ones_bf[0:1, 0:128], bvr[0:1, :],
                             start=False, stop=True)),
                    waits=("gd", "v", "s"))
                k.sec("vector", lambda v, tc=tc: (
                    v.tensor_scalar(r_bf[:, :], ps_k[:, :], 0.0, None, op0=OP.max),
                    v.scalar_tensor_tensor(mel[:, :], ps_k[:, :], 1.0, r_bf[:, :],
                                           op0=OP.mult, op1=OP.subtract),
                    v.tensor_copy(v_sb[:, tc * 512:(tc + 1) * 512], ps_v[:, :])),
                    waits=("t",))
                k.sec("scalar", lambda s: s.activation(ee[:, :], mel[:, :], ACT.Exp),
                      waits=("v",))
                k.sec("vector", lambda v, tc=tc: v.tensor_tensor(
                    sk_sb[:, tc * 512:(tc + 1) * 512], ee[:, :], r_bf[:, :], op=OP.add),
                    waits=("s",))

            ps_kv = psum("ps_kv", [128, 512], st=st2)
            ps_ks = psum("ps_ks", [128, 4], st=st2)
            kvred = sb("kvred", [128, 516], st=st2)

            def p2_kv(t):
                for g in range(KVH1):
                    for tc in range(NT):
                        t.matmul(ps_kv[:, g * 128:(g + 1) * 128],
                                 sk_sb[:, tc * 512 + g * 128: tc * 512 + (g + 1) * 128],
                                 v_sb[:, tc * 512 + g * 128: tc * 512 + (g + 1) * 128],
                                 start=(tc == 0), stop=(tc == NT - 1))
                for g in range(KVH1):
                    for tc in range(NT):
                        t.matmul(ps_ks[:, g:g + 1],
                                 sk_sb[:, tc * 512 + g * 128: tc * 512 + (g + 1) * 128],
                                 onescol_bf[:, :], start=(tc == 0), stop=(tc == NT - 1))
            k.sec("tensor", p2_kv, waits=("v",))
            k.sec("vector", lambda v: (
                v.tensor_copy(kvred[:, 0:512], ps_kv[:, :]),
                v.tensor_copy(kvred[:, 512:516], ps_ks[:, :])), waits=("t",))
            k.sec("gpsimd", lambda g: g.dma_start(out=cin_kv, in_=kvred[:, :]),
                  waits=("v",))
            k.sec("gpsimd", lambda g: g.collective_compute(
                "AllReduce", OP.add, replica_groups=PAIRS,
                ins=[cin_kv.opt()], outs=[cout_kv.opt()]), waits=("gd",))
            k.sec("gpsimd", lambda g: g.dma_start(out=kvred[:, :], in_=cout_kv),
                  waits=("c", "v"))
            k.sec("vector", lambda v: (
                v.tensor_copy(kv_bf[:, :], kvred[:, 0:512]),
                v.tensor_copy(ksum_bf[:, :], kvred[:, 512:516])), waits=("gd",))
            tp = tapout("kvred", [128, 516])
            if tp is not None:
                k.sec("gpsimd", lambda g: g.dma_start(out=tp, in_=kvred[:, :]),
                      waits=("v",))

        k.t_drain_mode = "none"
        fence()
        attnT = sb("attnT", [128, NT * D], BF16, st=s_att, side="right")
        with ExitStack() as st3:
            wq_sb = [sb("wq_sb%d" % i, [128, IC * 512], BF16, st=st3) for i in range(2)]
            bqc_sb = sb("bqc_sb", [128, IC], st=st3)
            sq_sb = [sb("sq_sb%d" % i, [128, D], BF16, st=st3) for i in range(2)]
            sqs_sb = [sb("sqs_sb%d" % i, [128, 512], BF16, st=st3) for i in range(2)]
            r_bf3 = [sb("r_bf3%d" % i, [128, 512], BF16, st=st3) for i in range(2)]
            mel3 = [sb("mel3%d" % i, [128, 512], st=st3) for i in range(2)]
            ee3 = [sb("ee3%d" % i, [128, 512], st=st3) for i in range(2)]
            nrow_tmp = sb("nrow_tmp", [1, 512], st=st3)
            nrow_bf = [sb("nrow_bf%d" % i, [1, D], BF16, st=st3) for i in range(2)]
            ps_q = [psum("ps_q%d" % i, [128, 512], st=st3) for i in range(2)]
            ps_n = [psum("ps_n%d" % i, [1, 512], st=st3) for i in range(2)]
            ps_b = [psum("ps_b%d" % i, [128, 512], st=st3) for i in range(2)]
            ps_a = [psum("ps_a%d" % i, [128, 512], st=st3) for i in range(2)]

            bqdma = dsec("sync", lambda y: y.dma_start(out=bqc_sb[:, :], in_=bqc))
            gq = {}
            tq = {}
            ve1 = {}
            sexp = {}
            ve2 = {}
            tn = {}
            vn = {}
            vr = {}
            tb_ = {}
            vsc = {}
            ta = {}
            vev = {}
            gq[0] = dsec("gpsimd", lambda g: g.dma_start(
                out=wq_sb[0][:, :].rearrange("p (ic o) -> p ic o", ic=IC),
                in_=wqT.rearrange("(ic p) o -> p ic o", p=128)[:, :, 0:512]),
                waits=("t", "v"))
            for h in range(H1):
                g2 = h // 4
                hh = h % 4
                hp = h % 2
                wbuf = wq_sb[g2 % 2]
                if hh == 0 and g2 + 1 < 4:
                    gq[g2 + 1] = dsec("gpsimd", lambda g, g2=g2: g.dma_start(
                        out=wq_sb[(g2 + 1) % 2][:, :].rearrange(
                            "p (ic o) -> p ic o", ic=IC),
                        in_=wqT.rearrange("(ic p) o -> p ic o", p=128)[
                            :, :, (g2 + 1) * 512:(g2 + 2) * 512]),
                        waits=[("t", tq.get((h - 1, 3)))])
                for tb in range(4):
                    st_i = 4 * h + tb
                    p = st_i % 2
                    tq[(h, tb)] = k.sec("tensor", lambda t, hh=hh, tb=tb, p=p,
                                        wbuf=wbuf: [
                        t.matmul(ps_q[p][:, :],
                                 wbuf[:, ic * 512 + hh * 128:
                                      ic * 512 + (hh + 1) * 128],
                                 xt0[:, :].rearrange("p (tc c) -> p tc c", tc=NT)[
                                     :, 4 * tb:4 * tb + 4, ic * 128:(ic + 1) * 128],
                                 start=(ic == 0), stop=(ic == IC - 1))
                        for ic in range(IC)],
                        waits=[gq[g2], ("v", ve1.get(st_i - 2))])
                    ve1[st_i] = k.sec("vector", lambda v, h=h, p=p: (
                        v.tensor_scalar(r_bf3[p][:, :], ps_q[p][:, :],
                                        bqc_sb[:, h:h + 1], 0.0,
                                        op0=OP.add, op1=OP.max),
                        v.scalar_tensor_tensor(mel3[p][:, :], ps_q[p][:, :],
                                               bqc_sb[:, h:h + 1], r_bf3[p][:, :],
                                               op0=OP.add, op1=OP.subtract)),
                        waits=[("t", tq[(h, tb)]), ("s", sexp.get(st_i - 2)),
                               bqdma])
                    sexp[st_i] = k.sec("scalar", lambda s, p=p: s.activation(
                        ee3[p][:, :], mel3[p][:, :], ACT.Exp),
                        waits=[("v", ve1[st_i])])
                    ve2[st_i] = k.sec("vector", lambda v, h=h, tb=tb, p=p:
                                      v.tensor_tensor(
                        sq_sb[h % 2][:, tb * 512:(tb + 1) * 512], ee3[p][:, :],
                        r_bf3[p][:, :], op=OP.add),
                        waits=[("s", sexp[st_i])])
                    tn[(h, tb)] = k.sec("tensor", lambda t, g2=g2, h=h, tb=tb, p=p:
                                        t.matmul(
                        ps_n[p][:, :], ksum_bf[:, g2:g2 + 1],
                        sq_sb[h % 2][:, tb * 512:(tb + 1) * 512],
                        start=True, stop=True),
                        waits=[("v", ve2[st_i]), ("v", vn.get(st_i - 2))])
                    def _vn(v, h=h, tb=tb, p=p):
                        v.tensor_scalar(nrow_tmp[0:1, :], ps_n[p][0:1, :],
                                        1.0, EPS0, op0=OP.mult, op1=OP.add)
                        with nc.allow_low_precision(
                                reason="bf16 recip feeds bf16 matmul operand"):
                            v.reciprocal(
                                nrow_bf[h % 2][0:1, tb * 512:(tb + 1) * 512],
                                nrow_tmp[0:1, :])
                    vn[st_i] = k.sec("vector", _vn,
                                     waits=[("t", tn[(h, tb)]),
                                            ("t", tb_.get((h - 1, tb)))])

                for tb in range(4):
                    st_i = 4 * h + tb
                    p = st_i % 2
                    tb_[(h, tb)] = k.sec("tensor", lambda t, h=h, tb=tb, p=p:
                                         t.matmul(
                        ps_b[p][:, :], ones_bf[0:1, 0:128],
                        nrow_bf[h % 2][0:1, tb * 512:(tb + 1) * 512],
                        start=True, stop=True),
                        waits=[("v", vn[st_i]), ("v", vsc.get(st_i - 2))])
                    vsc[st_i] = k.sec("vector", lambda v, h=h, tb=tb, p=p:
                                      v.tensor_tensor(
                        sqs_sb[p][:, :], sq_sb[h % 2][:, tb * 512:(tb + 1) * 512],
                        ps_b[p][:, :], op=OP.mult),
                        waits=[("t", tb_[(h, tb)])])
                    ta[(h, tb)] = k.sec("tensor", lambda t, g2=g2, p=p: t.matmul(
                        ps_a[p][:, :], kv_bf[:, g2 * 128:(g2 + 1) * 128],
                        sqs_sb[p][:, :], start=True, stop=True),
                        waits=[("v", vsc[st_i]), ("v", vev.get(st_i - 2))])
                    vev[st_i] = k.sec("vector", lambda v, h=h, tb=tb, p=p:
                                      v.tensor_copy(
                        attnT[:, h * D + tb * 512: h * D + (tb + 1) * 512],
                        ps_a[p][:, :]), waits=[("t", ta[(h, tb)])])
        s_xt0.close()
        tp = tapout("attnT", [128, NT * D], BF16)
        if tp is not None:
            k.sec("gpsimd", lambda g: g.dma_start(out=tp, in_=attnT[:, :]), waits=("v",))

        def boundary_chain(bufs, src_psums, res_src, res_dst,
                           extra_scale=None, extra_bias=None):
            ao, rn, yb = bufs["ao"], bufs["rn"], bufs["yb"]
            k.sec("sync", lambda y: y.dma_start(out=rn[1][:, :], in_=res_src),
                  waits=("v",))
            if extra_scale is None:
                k.sec("vector", lambda v: [
                    v.tensor_copy(ao[:, oc * 512:(oc + 1) * 512], src_psums[oc])
                    for oc in range(4)], waits=("t",))
            else:
                k.sec("vector", lambda v: [
                    (v.tensor_tensor(ao[:, oc * 512:(oc + 1) * 512], src_psums[oc],
                                     extra_scale[:, oc * 512:(oc + 1) * 512],
                                     op=OP.mult),
                     v.tensor_tensor(ao[:, oc * 512:(oc + 1) * 512],
                                     ao[:, oc * 512:(oc + 1) * 512],
                                     extra_bias[:, oc * 512:(oc + 1) * 512],
                                     op=OP.add))
                    for oc in range(4)], waits=("t",))
            norm_cols((), ao, EPSLN, True)
            k.sec("scalar", lambda s: s.activation(
                yb[:, :], ao[:, :], ACT.Identity, bias=c_nmi[:, 0:1], scale=c_inv[:, 0:1]),
                waits=("v",))
            k.sec("vector", lambda v: v.tensor_tensor(
                rn[0][:, :], yb[:, :], rn[1][:, :], op=OP.add), waits=("s", "yd"))
            k.sec("gpsimd", lambda g: g.dma_start(out=res_dst, in_=rn[0][:, :]),
                  waits=("v",))

        def next_norm_apply(bufs, eps2, mean2):
            rn, xh = bufs["rn"], bufs["xh"]
            norm_cols((), rn[0], eps2, mean2)
            k.sec("scalar", lambda s: s.activation(
                xh[:, :], rn[0][:, :], ACT.Identity, bias=c_nmi[:, 0:1],
                scale=c_inv[:, 0:1]), waits=("v",))

        def mk_bufs(st, tag):
            return {
                "ao": sb("ao_" + tag, [128, D], st=st),
                "rn": (sb("rn0_" + tag, [128, D], st=st),
                       sb("rn1_" + tag, [128, D], st=st)),
                "yb": sb("yb_" + tag, [128, D], st=st),
                "xh": sb("xh_" + tag, [128, D], BF16, st=st),
                "pt": [psum("pt_" + tag + str(i), [128, 1024], BF16, st=st)
                       for i in range(2)],
            }

        def out_phase(st, tag, mm_emit, res_src_fn, res_dst_fn,
                      extra_scale=None, extra_bias=None,
                      xt_write=None, xt_dma_fn=None, eps2=EPSLN, mean2=True,
                      prefetch_emit=None, mm_extra_waits=lambda tc: []):
            psb = [psum(tag + "_w%d" % i2, [128, 512], st=st) for i2 in range(4)]
            ptb = [psum(tag + "_pt%d" % i2, [128, 1024], BF16, st=st) for i2 in range(2)]
            ao = sb(tag + "_ao", [128, D], st=st)
            rn1 = [sb(tag + "_rn1%d" % i2, [128, D], st=st) for i2 in range(2)]
            rn0 = [sb(tag + "_rn0%d" % i2, [128, D], st=st) for i2 in range(2)]
            yb = sb(tag + "_yb", [128, D], st=st)
            xh = [sb(tag + "_xh%d" % i2, [128, D], BF16, st=st) for i2 in range(2)]
            ev = {}
            tcnt = {}
            ap2 = {}
            addv = {}
            rd = {}
            gdres = {}
            trc = {}
            xtev = {}

            xtgd = {}

            def emit_transpose(tcx):
                if xt_write is None or tcx < 0:
                    return
                trc[tcx] = k.sec("tensor", lambda t, tcx=tcx: [
                    t.transpose(ptb[ic // 8][:, (ic % 8) * 128:(ic % 8 + 1) * 128],
                                xh[tcx % 2][:, ic * 128:(ic + 1) * 128], ident[:, :])
                    for ic in range(IC)],
                    waits=[("s", ap2[tcx]), ("v", xtev.get(tcx - 1))])
                xtev[tcx] = k.sec("vector", lambda v, tcx=tcx: [
                    xt_write(v, tcx, hb, ptb[hb][:, :]) for hb in range(2)],
                    waits=[("t", trc[tcx]), xtgd.get(tcx - 2)])
                if xt_dma_fn is not None:
                    xtgd[tcx] = dsec("gpsimd", lambda g, tcx=tcx: xt_dma_fn(g, tcx),
                                     waits=[("v", xtev[tcx])])

            for tc in range(NT):
                if prefetch_emit is not None:
                    prefetch_emit(tc, tcnt)
                rd[tc] = dsec("sync", lambda y, tc=tc: y.dma_start(
                    out=rn1[tc % 2][:, :], in_=res_src_fn(tc)),
                    waits=[("v", addv.get(tc - 2))])
                for oc in range(4):
                    tcnt[(tc, oc)] = k.sec(
                        "tensor", lambda t, tc=tc, oc=oc: mm_emit(t, tc, oc, psb[oc]),
                        waits=[("v", ev.get((tc - 1, oc)))] + mm_extra_waits(tc))
                for oc in range(4):
                    if extra_scale is None:
                        ev[(tc, oc)] = k.sec("vector", lambda v, oc=oc: v.tensor_copy(
                            ao[:, oc * 512:(oc + 1) * 512], psb[oc][:, :]),
                            waits=[("t", tcnt[(tc, oc)])])
                    else:
                        ev[(tc, oc)] = k.sec("vector", lambda v, oc=oc: (
                            v.tensor_tensor(ao[:, oc * 512:(oc + 1) * 512],
                                            psb[oc][:, :],
                                            extra_scale[:, oc * 512:(oc + 1) * 512],
                                            op=OP.mult),
                            v.tensor_tensor(ao[:, oc * 512:(oc + 1) * 512],
                                            ao[:, oc * 512:(oc + 1) * 512],
                                            extra_bias[:, oc * 512:(oc + 1) * 512],
                                            op=OP.add)),
                            waits=[("t", tcnt[(tc, oc)])])
                k.sec("vector", lambda v: emit_stats(v, ao, EPSLN, True),
                      waits=("s",))
                sq1 = k.sec("scalar", lambda s: s.activation(c_tmp[:, :], c_tmp[:, :],
                                                             ACT.Sqrt), waits=("v",))
                iv1 = k.sec("vector", lambda v: emit_inv(v, True),
                            waits=[("s", sq1)])
                apl = k.sec("scalar", lambda s: s.activation(
                    yb[:, :], ao[:, :], ACT.Identity, bias=c_nmi[:, 0:1],
                    scale=c_inv[:, 0:1]), waits=[("v", iv1)])
                addv[tc] = k.sec("vector", lambda v, tc=tc: v.tensor_tensor(
                    rn0[tc % 2][:, :], yb[:, :], rn1[tc % 2][:, :], op=OP.add),
                    waits=[("s", apl), rd[tc], gdres.get(tc - 2)])
                gdres[tc] = dsec("gpsimd", lambda g, tc=tc: g.dma_start(
                    out=res_dst_fn(tc), in_=rn0[tc % 2][:, :]),
                    waits=[("v", addv[tc])])
                if xt_write is not None:
                    k.sec("vector", lambda v, tc=tc: emit_stats(
                        v, rn0[tc % 2], eps2, mean2), waits=("s",))
                    sq2 = k.sec("scalar", lambda s: s.activation(
                        c_tmp[:, :], c_tmp[:, :], ACT.Sqrt), waits=("v",))
                    iv2 = k.sec("vector", lambda v: emit_inv(v, mean2),
                                waits=[("s", sq2)])
                    ap2[tc] = k.sec("scalar", lambda s, tc=tc: s.activation(
                        xh[tc % 2][:, :], rn0[tc % 2][:, :], ACT.Identity,
                        bias=c_nmi[:, 0:1], scale=c_inv[:, 0:1]),
                        waits=[("v", iv2)])
                emit_transpose(tc - 1)
            emit_transpose(NT - 1)
            return gdres, xtev

        k.t_drain_mode = "none"
        fence()
        with ExitStack() as st4:
            wo_sb = sb("wo_sb", [128, IC * D], BF16, st=st4)
            xtst = [sb("xtst4%d" % i, [128, D], BF16, st=st4) for i in range(2)]
            gwo = dsec("gpsimd", lambda g: g.dma_start(
                out=wo_sb[:, :].rearrange("p (ec o) -> p ec o", ec=IC),
                in_=woT.rearrange("(ec p) o -> p ec o", p=128)), waits=("v",))
            xt1dma = {}

            def mm4(t, tc, oc, bank):
                for ec in range(IC):
                    t.matmul(bank[:, :],
                             attnT[:, ec * D + tc * 128: ec * D + (tc + 1) * 128],
                             wo_sb[:, ec * D + oc * 512: ec * D + (oc + 1) * 512],
                             start=(ec == 0), stop=(ec == IC - 1))

            def xtw4(v, tcx, hb, ap):
                v.tensor_copy(xtst[tcx % 2][:, hb * 1024:(hb + 1) * 1024], ap)

            out_phase(
                st4, "p4", mm4,
                lambda tc: x_in[tc * 128:(tc + 1) * 128, :],
                lambda tc: R_a[tc * 128:(tc + 1) * 128, :],
                xt_write=xtw4,
                xt_dma_fn=lambda g, tcx: g.dma_start(out=XT1_h[tcx],
                                                     in_=xtst[tcx % 2][:, :]),
                mm_extra_waits=lambda tc: [gwo] if tc == 0 else [])
        s_att.close()
        tp = tapout("r_a", [T, D])
        if tp is not None:
            tap_dram(tp, R_a, T, D, F32)

        fence()
        xt2 = sb("xt2", [128, NT * D], BF16, st=s_xt2)
        with ExitStack() as st5:
            fw_sb = sb("fw_sb", [128, IC * D], BF16, st=st5)
            scl_sb = sb("scl_sb", [128, D], BF16, st=st5)
            bias_sb = sb("bias_sb", [128, D], BF16, st=st5)
            xt1t = [sb("xt1t%d" % i, [128, D], BF16, st=st5) for i in range(2)]
            ps_bc = psum("ps_bc", [128, 512], st=st5)

            gfw = dsec("gpsimd", lambda g: g.dma_start(
                out=fw_sb[:, :].rearrange("p (ic o) -> p ic o", ic=IC),
                in_=f0wT.rearrange("(ic p) o -> p ic o", p=128)), waits=("v", "t"))
            with ExitStack() as st5b:
                srow = sb("srow", [1, D], st=st5b)
                brow = sb("brow", [1, D], st=st5b)
                k.sec("sync", lambda y: (y.dma_start(out=srow[:, :], in_=f0scale),
                                         y.dma_start(out=brow[:, :], in_=f0bias)))
                for row, dst in [(srow, scl_sb), (brow, bias_sb)]:
                    for tb in range(4):
                        k.sec("tensor", lambda t, row=row, tb=tb: t.matmul(
                            ps_bc[:, :], ones_f[0:1, 0:128],
                            row[0:1, tb * 512:(tb + 1) * 512],
                            start=True, stop=True), waits=("yd", "v"))
                        k.sec("vector", lambda v, dst=dst, tb=tb: v.tensor_copy(
                            dst[:, tb * 512:(tb + 1) * 512], ps_bc[:, :]),
                            waits=("t",))
            fence()

            xt1dma = {}
            xt1dma[0] = dsec("sync", lambda y: y.dma_start(
                out=xt1t[0][:, :], in_=XT1_h[0]), waits=("gd",))

            def pre5(tc, tcnt):
                if tc + 1 < NT:
                    xt1dma[tc + 1] = dsec("sync", lambda y, tc=tc: y.dma_start(
                        out=xt1t[(tc + 1) % 2][:, :], in_=XT1_h[tc + 1]),
                        waits=[("t", tcnt.get((tc - 1, 3)))])

            def mm5(t, tc, oc, bank):
                for ic in range(IC):
                    t.matmul(bank[:, :],
                             xt1t[tc % 2][:, ic * 128:(ic + 1) * 128],
                             fw_sb[:, ic * D + oc * 512: ic * D + (oc + 1) * 512],
                             start=(ic == 0), stop=(ic == IC - 1))

            def xtw5(v, tcx, hb, ap):
                v.tensor_copy(xt2[:, tcx * D + hb * 1024: tcx * D + (hb + 1) * 1024],
                              ap)

            out_phase(
                st5, "p5", mm5,
                lambda tc: R_a[tc * 128:(tc + 1) * 128, :],
                lambda tc: R_b[tc * 128:(tc + 1) * 128, :],
                extra_scale=scl_sb, extra_bias=bias_sb,
                xt_write=xtw5, eps2=EPSRMS, mean2=False,
                prefetch_emit=pre5,
                mm_extra_waits=lambda tc: [xt1dma[tc]] +
                    ([gfw] if tc == 0 else []))
        tp = tapout("r_b", [T, D])
        if tp is not None:
            tap_dram(tp, R_b, T, D, F32)
        tp = tapout("xt2", [128, NT * D], BF16)
        if tp is not None:
            k.sec("gpsimd", lambda g: g.dma_start(out=tp, in_=xt2[:, :]), waits=("v",))

        fence()
        cw_sb = sb("cw_sb", [128, NT * 96], st=s_cw, side="right")
        with ExitStack() as st6:
            wsec = sb("wsec", [128, IC * 1024], BF16, st=st6)
            rms_sb = sb("rms_sb", [128, IC], st=st6)
            qstage = [sb("qstage%d" % i, [128, 512], BF16, st=st6) for i in range(2)]
            vstage = [sb("vstage%d" % i, [128, 1024], BF16, st=st6) for i in range(2)]
            vtst = [sb("vtst%d" % i, [128, 1024], BF16, st=st6) for i in range(2)]
            ps_p = [psum("ps_p6%d" % i, [128, 512], st=st6) for i in range(2)]
            ps_v4 = [psum("ps_v4%d" % i, [128, 512], st=st6) for i in range(4)]
            ptv = [psum("ptv%d" % i, [128, 1024], BF16, st=st6) for i in range(2)]

            k.sec("sync", lambda y: y.dma_start(out=rms_sb[:, :], in_=rmswc))
            ev6 = {}
            gv6 = {}
            gqk = {}
            tv6 = {}
            vtev = {}
            vtgd = {}
            step = 0
            vstep = 0
            for osec in range(6):
                gw = k.sec("gpsimd", lambda g, osec=osec: g.dma_start(
                    out=wsec[:, :].rearrange("p (ic o) -> p ic o", ic=IC),
                    in_=wqkv.rearrange("(ic p) o -> p ic o", p=128)[
                        :, :, osec * 1024:(osec + 1) * 1024]), waits=("t",))
                fold = k.sec("vector", lambda v: [
                    v.tensor_scalar_mul(wsec[:, ic * 1024:(ic + 1) * 1024],
                                        wsec[:, ic * 1024:(ic + 1) * 1024],
                                        rms_sb[:, ic:ic + 1]) for ic in range(IC)],
                    waits=("gd", "yd"))
                if osec < 4:
                    for oc8 in range(8):
                        oc = osec * 8 + oc8
                        dst_h = Q_h if oc < 16 else KT_h
                        oc_l = oc % 16
                        for tb in range(4):
                            par = step % 2
                            tcnt6 = k.sec("tensor", lambda t, oc8=oc8, tb=tb,
                                          par=par: [
                                t.matmul(ps_p[par][:, :],
                                         wsec[:, ic * 1024 + oc8 * 128:
                                              ic * 1024 + (oc8 + 1) * 128],
                                         xt2[:, :].rearrange(
                                             "p (tc c) -> p tc c", tc=NT)[
                                             :, 4 * tb:4 * tb + 4,
                                             ic * 128:(ic + 1) * 128],
                                         start=(ic == 0), stop=(ic == IC - 1))
                                for ic in range(IC)],
                                waits=[("v", fold), ("v", ev6.get(step - 2))])
                            ev6[step] = k.sec(
                                "vector", lambda v, par=par: v.tensor_copy(
                                    qstage[par][:, :], ps_p[par][:, :]),
                                waits=[("t", tcnt6), gqk.get(step - 2)])
                            gqk[step] = dsec(
                                "gpsimd", lambda g, dst_h=dst_h, oc_l=oc_l,
                                tb=tb, par=par: g.dma_start(
                                    out=dst_h[oc_l][:, tb * 512:(tb + 1) * 512],
                                    in_=qstage[par][:, :]),
                                waits=[("v", ev6[step])])
                            step += 1
                else:
                    for tc in range(NT):
                        par = vstep % 2
                        tvc = k.sec("tensor", lambda t, tc=tc, par=par: [
                            t.matmul(ps_v4[par * 2 + oh][:, :],
                                     xt2[:, tc * D + ic * 128:
                                         tc * D + (ic + 1) * 128],
                                     wsec[:, ic * 1024 + oh * 512:
                                          ic * 1024 + (oh + 1) * 512],
                                     start=(ic == 0), stop=(ic == IC - 1))
                            for oh in range(2) for ic in range(IC)],
                            waits=[("v", fold), ("v", tv6.get(vstep - 2))])
                        tv6[vstep] = k.sec("vector", lambda v, par=par: [
                            v.tensor_copy(vstage[par][:, oh * 512:(oh + 1) * 512],
                                          ps_v4[par * 2 + oh][:, :])
                            for oh in range(2)],
                            waits=[("t", tvc), vtgd.get(vstep - 2),
                                   gv6.get(vstep - 2)])
                        gv6[vstep] = dsec("gpsimd", lambda g, tc=tc, osec=osec,
                                          par=par: g.dma_start(
                            out=V_h[tc][:, (osec - 4) * 1024:(osec - 3) * 1024],
                            in_=vstage[par][:, :]), waits=[("v", tv6[vstep])])
                        trv = k.sec("tensor", lambda t, par=par: [
                            t.transpose(ptv[par][:, j * 128:(j + 1) * 128],
                                        vstage[par][:, j * 128:(j + 1) * 128],
                                        ident[:, :])
                            for j in range(8)],
                            waits=[("v", tv6[vstep]), ("v", vtev.get(vstep - 2))])
                        vtev[vstep] = k.sec(
                            "vector", lambda v, par=par: v.tensor_copy(
                                vtst[par][:, :], ptv[par][:, :]),
                            waits=[("t", trv), vtgd.get(vstep - 1)])
                        vtgd[vstep] = dsec(
                            "gpsimd", lambda g, tc=tc, osec=osec, par=par:
                            g.dma_start(
                                out=VTF_h[(osec - 4) * 8:(osec - 4) * 8 + 8, :,
                                          tc * 128:(tc + 1) * 128]
                                .transpose([1, 0, 2]),
                                in_=vtst[par][:, :].rearrange(
                                    "p (j t) -> p j t", j=8)),
                            waits=[("v", vtev[vstep])])
                        vstep += 1
            wcb = sb("wcb", [128, IC * 96], BF16, st=st6)
            bcb = sb("bcb", [1, 96], BF16, st=st6)
            k.sec("gpsimd", lambda g: (
                g.dma_start(out=wcb[:, :].rearrange("p (ic o) -> p ic o", ic=IC),
                            in_=wcomb.rearrange("(ic p) o -> p ic o", p=128)),
                g.dma_start(out=bcb[:, :], in_=bcomb_row)), waits=("t",))
            fw6 = k.sec("vector", lambda v: [
                v.tensor_scalar_mul(wcb[:, ic * 96:(ic + 1) * 96],
                                    wcb[:, ic * 96:(ic + 1) * 96],
                                    rms_sb[:, ic:ic + 1]) for ic in range(IC)],
                waits=("gd", "v", "s"))
            scw = {}
            for tc in range(NT):
                pc = ps_p[tc % 2][:, 0:96]
                tcw = k.sec("tensor", lambda t, tc=tc, pc=pc: (
                    [t.matmul(pc,
                              xt2[:, tc * D + ic * 128: tc * D + (ic + 1) * 128],
                              wcb[:, ic * 96:(ic + 1) * 96],
                              start=(ic == 0), stop=False) for ic in range(IC)],
                    t.matmul(pc, ones_bf[0:1, 0:128], bcb[0:1, :],
                             start=False, stop=True)),
                    waits=[("v", fw6), ("s", scw.get(tc - 2))])
                scw[tc] = k.sec("scalar", lambda s, tc=tc, pc=pc: s.activation(
                    cw_sb[:, tc * 96:(tc + 1) * 96], pc, ACT.Sigmoid),
                    waits=[("t", tcw)])
        s_xt2.close()
        tp = tapout("cw", [128, NT * 96])
        if tp is not None:
            k.sec("gpsimd", lambda g: g.dma_start(out=tp, in_=cw_sb[:, :]),
                  waits=("s",))

        fence()
        with ExitStack() as st7:
            bstg = sb("bstg", [128, D], BF16, st=st7)
            bstg2 = sb("bstg2", [128, D], BF16, st=st7)
            k.sec("gpsimd", lambda g: (
                g.dma_start(out=bstg[:, :].rearrange("p (a t) -> p a t", a=IC),
                            in_=KT_h[:, :, T - 128:T].transpose([1, 0, 2])),
                g.dma_start(out=bstg2[:, :], in_=V_h[NT - 1])),
                waits=("gd", "s"))
            k.sec("gpsimd", lambda g: (
                g.dma_start(out=cin_bd[:, 0:2048], in_=bstg2[:, :]),
                g.dma_start(out=cin_bd[:, 2048:4096], in_=bstg[:, :])),
                waits=("gd",))
            k.sec("gpsimd", lambda g: g.collective_compute(
                "AllGather", OP.bypass, replica_groups=PAIRS,
                ins=[cin_bd.opt()], outs=[cout_bd.opt()]), waits=("gd",))

        fence()
        with ExitStack() as st8:
            qsb_ = [sb("qsb%d" % i, [128, IC * 512], BF16, st=st8) for i in range(2)]
            ksb_ = [sb("ksb%d" % i, [128, IC * 512], BF16, st=st8) for i in range(2)]
            kpb_ = [sb("kpb%d" % i, [128, IC * 128], BF16, st=st8) for i in range(2)]
            vwin_ = [sb("vwin%d" % i, [128, 5 * D], BF16, st=st8) for i in range(2)]
            ec_sb = [sb("ec_sb%d" % i, [128, 512], BF16, st=st8) for i in range(2)]
            ep_sb = [sb("ep_sb%d" % i, [128, 512], BF16, st=st8) for i in range(2)]

            loc4 = [sb("loc4%d" % i, [128, 4 * D], BF16, st=st8) for i in range(2)]
            rc4 = [sb("rc4%d" % i, [128, 4], st=st8) for i in range(2)]
            mlc = sb("mlc", [128, 512], BF16, st=st8)
            mlp = sb("mlp", [128, 512], BF16, st=st8)
            mlp0 = sb("mlp0", [128, 512], BF16, st=st8)
            ps_sc = [psum("ps_sc%d" % i, [128, 512], st=st8) for i in range(2)]
            ps_sp = [psum("ps_sp%d" % i, [128, 512], st=st8) for i in range(2)]
            ps_od = [psum("ps_od%d" % i, [128, 260], st=st8) for i in range(2)]

            mldma = dsec("gpsimd", lambda g: (
                g.dma_start(out=mlc[:, :], in_=mloc_cur_in),
                g.dma_start(out=mlp[:, :], in_=mloc_prev_in),
                g.dma_start(out=mlp0[:, :], in_=mloc_prev0_in)))

            ts8 = {}
            vm1 = {}
            se1 = {}
            vm2 = {}
            se2 = {}
            to8 = {}
            vfin = {}
            gld = {}
            locdma = {}
            def emit_gld(sbk):
                sp = sbk % 2
                gld[sbk] = dsec("gpsimd", lambda g, sbk=sbk, sp=sp: (
                    g.dma_start(out=qsb_[sp][:, :].rearrange("p (a t) -> p a t", a=IC),
                                in_=Q_h[:, :, sbk * 512:(sbk + 1) * 512]
                                .transpose([1, 0, 2])),
                    g.dma_start(out=ksb_[sp][:, :].rearrange("p (a t) -> p a t", a=IC),
                                in_=KT_h[:, :, sbk * 512:(sbk + 1) * 512]
                                .transpose([1, 0, 2])),
                    (g.dma_start(out=kpb_[sp][:, :].rearrange("p (a t) -> p a t", a=IC),
                                 in_=KT_h[:, :, sbk * 512 - 128: sbk * 512]
                                 .transpose([1, 0, 2]))
                     if sbk > 0 else
                     g.dma_start(out=kpb_[sp][:, :], in_=cout_bd[0:128, 2048:4096])),
                    [g.dma_start(out=vwin_[sp][:, (w + 1) * D:(w + 2) * D],
                                 in_=V_h[sbk * 4 + w]) for w in range(4)],
                    (g.dma_start(out=vwin_[sp][:, 0:D], in_=V_h[sbk * 4 - 1])
                     if sbk > 0 else
                     g.dma_start(out=vwin_[sp][:, 0:D], in_=cout_bd[0:128, 0:2048]))),
                    waits=[("t", ts8.get((sbk - 2, H2 - 1))), ("c", 2),
                           ("v", vfin.get((sbk - 2, H2 - 1)))])
            emit_gld(0)
            for sbk in range(4):
                if sbk + 1 < 4:
                    emit_gld(sbk + 1)
                qsb = qsb_[sbk % 2]
                ksb = ksb_[sbk % 2]
                kpb = kpb_[sbk % 2]
                vwin = vwin_[sbk % 2]
                for h in range(H2):
                    hp, oc = h % 2, h // 2
                    p = h % 2
                    ts8[(sbk, h)] = k.sec("tensor", lambda t, hp=hp, oc=oc, p=p, qsb=qsb, ksb=ksb, kpb=kpb: [
                        (t.matmul(ps_sc[p][:, b * 128:(b + 1) * 128],
                                  ksb[hp * 64:hp * 64 + 64,
                                      oc * 512 + b * 128: oc * 512 + (b + 1) * 128],
                                  qsb[hp * 64:hp * 64 + 64,
                                      oc * 512 + b * 128: oc * 512 + (b + 1) * 128],
                                  start=True, stop=True),
                         t.matmul(ps_sp[p][:, b * 128:(b + 1) * 128],
                                  (kpb[hp * 64:hp * 64 + 64, oc * 128:(oc + 1) * 128]
                                   if b == 0 else
                                   ksb[hp * 64:hp * 64 + 64,
                                       oc * 512 + (b - 1) * 128: oc * 512 + b * 128]),
                                  qsb[hp * 64:hp * 64 + 64,
                                      oc * 512 + b * 128: oc * 512 + (b + 1) * 128],
                                  start=True, stop=True)) for b in range(4)],
                        waits=[gld[sbk], ("s", se1.get((sbk, h - 2))),
                               ("s", se2.get((sbk, h - 2)))])
                    se1[(sbk, h)] = k.sec("scalar", lambda s, p=p: s.activation(
                        ec_sb[p][:, :], ps_sc[p][:, :], ACT.Exp, scale=SCALE2),
                        waits=[("t", ts8[(sbk, h)]), ("v", vm1.get((sbk, h - 2))),
                               ("t", to8.get((sbk, h - 2)))])
                    se2[(sbk, h)] = k.sec("scalar", lambda s, p=p: s.activation(
                        ep_sb[p][:, :], ps_sp[p][:, :], ACT.Exp, scale=SCALE2),
                        waits=[("v", vm1.get((sbk, h - 2)))])
                    vm1[(sbk, h)] = k.sec("vector", lambda v, sbk=sbk, p=p: (
                        v.tensor_tensor(ec_sb[p][:, :], ec_sb[p][:, :], mlc[:, :],
                                        op=OP.mult),
                        v.tensor_tensor(ep_sb[p][:, :], ep_sb[p][:, :],
                                        mlp0[:, :] if sbk == 0 else mlp[:, :],
                                        op=OP.mult)),
                        waits=[("s", se2[(sbk, h)]), mldma])
                    to8[(sbk, h)] = k.sec("tensor", lambda t, h=h, p=p, vwin=vwin: [
                        (t.matmul(ps_od[p][:, b * 64:(b + 1) * 64],
                                  ec_sb[p][:, b * 128:(b + 1) * 128],
                                  vwin[:, (b + 1) * D + h * 64:
                                       (b + 1) * D + (h + 1) * 64],
                                  start=True, stop=False),
                         t.matmul(ps_od[p][:, b * 64:(b + 1) * 64],
                                  ep_sb[p][:, b * 128:(b + 1) * 128],
                                  vwin[:, b * D + h * 64: b * D + (h + 1) * 64],
                                  start=False, stop=True),
                         t.matmul(ps_od[p][:, 256 + b: 256 + b + 1],
                                  ec_sb[p][:, b * 128:(b + 1) * 128],
                                  onescol_bf[:, :], start=True, stop=False),
                         t.matmul(ps_od[p][:, 256 + b: 256 + b + 1],
                                  ep_sb[p][:, b * 128:(b + 1) * 128],
                                  onescol_bf[:, :], start=False, stop=True))
                        for b in range(4)],
                        waits=[("v", vm1[(sbk, h)]), ("v", vfin.get((sbk, h - 2)))])
                    vfin[(sbk, h)] = k.sec("vector", lambda v, sbk=sbk, h=h, p=p: (
                        v.reciprocal(rc4[p][:, :], ps_od[p][:, 256:260]),
                        [v.tensor_scalar_mul(
                            loc4[sbk % 2][:, b * D + h * 64: b * D + (h + 1) * 64],
                            ps_od[p][:, b * 64:(b + 1) * 64], rc4[p][:, b:b + 1])
                         for b in range(4)]),
                        waits=[("t", to8[(sbk, h)]),
                               locdma.get(sbk - 2)])
                locdma[sbk] = dsec("gpsimd", lambda g, sbk=sbk: [
                    g.dma_start(out=LOC_h[sbk * 4 + b],
                                in_=loc4[sbk % 2][:, b * D:(b + 1) * D])
                    for b in range(4)], waits=[("v", vfin[(sbk, H2 - 1)])])

        tp = tapout("loc", [NT * 128, D], BF16)
        if tp is not None:
            tap_dram(tp, LOC_h.rearrange("a p d -> (a p) d"), NT * 128, D, BF16)

        fence()
        brow1 = sb("brow1", [1, D], BF16, st=s_wt)
        st11 = ExitStack()
        with ExitStack() as st11a:
            rwa = sb("rwa", [1, 4 * D], st=st11a)
            rwb = sb("rwb", [1, 4 * D], st=st11a)
            bacc = [sb("bacc%d" % i, [1, D], st=st11a) for i in range(2)]
            bscl11 = sb("bscl11", [1, 1], st=st11a)
            bx0 = sb("bx0", [1, 4], st=st11a)
            k.sec("sync", lambda y: (y.dma_start(out=rwa[:, :], in_=pb_row),
                                     y.dma_start(out=rwb[:, :], in_=nb_row),
                                     y.dma_start(out=bx0[:, :], in_=bexps_row),
                                     y.dma_start(out=bscl11[:, :], in_=bscale1)),
                  waits=("v", "s"))

            def bias_chain(v):
                v.tensor_tensor(rwa[:, :], rwa[:, :], rwb[:, :], op=OP.subtract)
                for n in range(4):
                    if n == 0:
                        v.tensor_scalar_mul(bacc[0][:, :], rwa[0:1, 0:D],
                                            bx0[0:1, 0:1])
                    else:
                        v.scalar_tensor_tensor(bacc[n % 2][:, :],
                                               rwa[0:1, n * D:(n + 1) * D],
                                               bx0[0:1, n:n + 1],
                                               bacc[(n + 1) % 2][:, :],
                                               op0=OP.mult, op1=OP.add)
                v.tensor_scalar(bacc[0][:, :], bacc[1][:, :], -1.0, 1.0,
                                op0=OP.max, op1=OP.min)
                v.tensor_scalar_mul(bacc[0][:, :], bacc[0][:, :], 15.0)
                MAGIC = 8388608.0
                v.tensor_scalar(bacc[1][:, :], bacc[0][:, :], 0.0, MAGIC,
                                op0=OP.max, op1=OP.add)
                v.tensor_scalar(bacc[1][:, :], bacc[1][:, :], 1.0, -MAGIC,
                                op0=OP.mult, op1=OP.add)
                v.tensor_scalar(rwa[0:1, 0:D], bacc[0][:, :], -1.0, 0.0,
                                op0=OP.mult, op1=OP.max)
                v.tensor_scalar(rwa[0:1, 0:D], rwa[0:1, 0:D], 1.0, MAGIC,
                                op0=OP.mult, op1=OP.add)
                v.tensor_scalar(rwa[0:1, 0:D], rwa[0:1, 0:D], 1.0, -MAGIC,
                                op0=OP.mult, op1=OP.add)
                v.tensor_tensor(bacc[1][:, :], bacc[1][:, :], rwa[0:1, 0:D],
                                op=OP.subtract)
                v.tensor_scalar_mul(bacc[1][:, :], bacc[1][:, :], 1.0 / 15.0)
                v.tensor_scalar_mul(bacc[1][:, :], bacc[1][:, :], bscl11[0:1, 0:1])
                v.tensor_copy(brow1[:, :], bacc[1][:, :])
            k.sec("vector", bias_chain, waits=("yd",))
        fence()
        pbuf = sb("pbuf", [128, 4096], st=st11)
        nbuf = sb("nbuf", [128, 4096], st=st11)
        sps = sb("sps", [128, 4096], st=st11)
        sns = sb("sns", [128, 4096], st=st11)
        acc = [sb("acc%d" % i, [128, D], st=st11) for i in range(2)]
        wtst = [sb("wtst%d" % i, [128, D], BF16, st=st11) for i in range(2)]
        enm = sb("enm", [1, 4], st=st11)
        enmb = sb("enmb", [128, 4], st=st11)
        enmbn = sb("enmbn", [128, 4], st=st11)
        srow1 = sb("srow1", [1, 4], st=st11)
        scl11 = sb("scl11", [1, 1], st=st11)
        bx = sb("bx", [1, 4], st=st11)
        ps_e = st11.enter_context(nc.psum_tensor("ps_e", [128, 4], F32))

        k.sec("sync", lambda y: (
            y.dma_start(out=srow1[:, :], in_=maskw_row),
            y.dma_start(out=bx[:, :], in_=exps_row),
            y.dma_start(out=scl11[:, :], in_=scale1)), waits=("v", "s"))
        k.sec("scalar", lambda s: s.activation(enm[:, :], srow1[:, :], ACT.Sigmoid),
              waits=("yd",))
        k.sec("vector", lambda v: (
            v.tensor_tensor(enm[:, :], enm[:, :], bx[:, :], op=OP.mult),
            v.tensor_scalar_mul(enm[:, :], enm[:, :], scl11[0:1, 0:1])),
            waits=("s",))
        k.sec("tensor", lambda t: t.matmul(ps_e[:, :], ones_f[0:1, 0:128],
                                           enm[0:1, :], start=True, stop=True),
              waits=("v",))
        k.sec("vector", lambda v: (
            v.tensor_copy(enmb[:, :], ps_e[:, :]),
            v.tensor_scalar_mul(enmbn[:, :], ps_e[:, :], -1.0)), waits=("t",))

        prep_state = {"sig": {}, "mac": {}, "wdma": {}, "ld": {}}

        def prep_step_fns():
            for ic in range(IC):
                for half in range(2):
                    def step(ic=ic, half=half):
                        prep_state["ld"][(ic, half)] = dsec("gpsimd", lambda g: (
                            g.dma_start(out=pbuf[:, :],
                                        in_=pT[ic * 128:(ic + 1) * 128,
                                               half * 4096:(half + 1) * 4096]),
                            g.dma_start(out=nbuf[:, :],
                                        in_=nT[ic * 128:(ic + 1) * 128,
                                               half * 4096:(half + 1) * 4096])),
                            waits=[("s", prep_state["sig"].get(
                                (ic, half - 1),
                                prep_state["sig"].get((ic - 1, 1))))])
                        prep_state["sig"][(ic, half)] = k.sec(
                            "scalar", lambda s: (
                                s.activation(sps[:, :], pbuf[:, :], ACT.Sigmoid),
                                s.activation(sns[:, :], nbuf[:, :], ACT.Sigmoid)),
                            waits=[prep_state["ld"][(ic, half)],
                                   ("v", prep_state["mac"].get(
                                       (ic, half - 1),
                                       prep_state["mac"].get((ic - 1, 1))))])

                        def macs(v, half=half):
                            o = half * 4
                            for nn2 in range(2):
                                n_g = half * 2 + nn2
                                if o == 0:
                                    v.tensor_scalar_mul(acc[0][:, :],
                                                        sps[:, nn2 * D:(nn2 + 1) * D],
                                                        enmb[:, n_g:n_g + 1])
                                else:
                                    v.scalar_tensor_tensor(
                                        acc[o % 2][:, :],
                                        sps[:, nn2 * D:(nn2 + 1) * D],
                                        enmb[:, n_g:n_g + 1], acc[(o + 1) % 2][:, :],
                                        op0=OP.mult, op1=OP.add)
                                o += 1
                                v.scalar_tensor_tensor(
                                    acc[o % 2][:, :],
                                    sns[:, nn2 * D:(nn2 + 1) * D],
                                    enmbn[:, n_g:n_g + 1], acc[(o + 1) % 2][:, :],
                                    op0=OP.mult, op1=OP.add)
                                o += 1
                            if half == 1:
                                v.tensor_copy(wtst[ic % 2][:, :], acc[1][:, :])
                        prep_state["mac"][(ic, half)] = k.sec(
                            "vector", macs,
                            waits=[("s", prep_state["sig"][(ic, half)])] +
                                  ([prep_state["wdma"].get(ic - 2)]
                                   if half == 1 else []))
                    yield step

                def store(ic=ic):
                    prep_state["wdma"][ic] = dsec("gpsimd", lambda g: g.dma_start(
                        out=WT_h[ic], in_=wtst[ic % 2][:, :]),
                        waits=[("v", prep_state["mac"][(ic, 1)])])
                yield store

        prep_iter = prep_step_fns()

        fence()
        with ExitStack() as st9:
            src128 = [sb("src128%d" % i, [128, D], BF16, st=st9) for i in range(2)]
            eo = [sb("eo9%d" % i, [128, 1024], BF16, st=st9) for i in range(2)]
            pos_sb = [sb("pos_sb%d" % i, [128, 32], st=st9) for i in range(2)]
            wc_sb9 = [sb("wc_sb9%d" % i, [128, 32 * 64], BF16, st=st9)
                      for i in range(2)]
            ckown = sb("ckown", [32, D], BF16, st=st9)
            cvown = sb("cvown", [32, D], BF16, st=st9)
            cbk = sb("cbk", [1, H2 * 64], BF16, st=st9)
            cbv = sb("cbv", [1, H2 * 64], BF16, st=st9)
            ps_ck = [psum("ps_ck9%d" % i, [32, 64], st=st9) for i in range(2)]

            cbdma = dsec("gpsimd", lambda g: (
                g.dma_start(out=cbk[:, :], in_=kcb_row),
                g.dma_start(out=cbv[:, :], in_=vcb_row)))
            items = [(h, kind) for h in range(H2) for kind in range(2)]

            def ld9(i):
                h, kind = items[i]
                src_h = KT_h if kind == 0 else VTF_h
                pos_in = kpos_eo if kind == 0 else vpos_eo
                wc_in = kc_h if kind == 0 else vc_h
                p = i % 2
                return dsec("gpsimd", lambda g: (
                    g.dma_start(out=src128[p][0:64, :],
                                in_=src_h[h // 2][(h % 2) * 64:(h % 2) * 64 + 64, :]),
                    g.dma_start(out=src128[p][64:128, :],
                                in_=src_h[h // 2][(h % 2) * 64:(h % 2) * 64 + 64, :]),
                    g.dma_start(out=pos_sb[p][:, :], in_=pos_in[h]),
                    g.dma_start(out=wc_sb9[p][:, :].rearrange(
                        "p (a o) -> p a o", a=32), in_=wc_in[h].transpose([1, 0, 2]))),
                    waits=[("v", veo.get(i - 2)), ("t", tck.get(i - 2))])

            veo = {}
            tck = {}
            vck = {}
            gld9 = {0: None}
            gld9[0] = ld9(0)
            for i, (h, kind) in enumerate(items):
                nxt = next(prep_iter, None)
                if nxt is not None:
                    nxt()
                if i + 1 < len(items):
                    gld9[i + 1] = ld9(i + 1)
                p = i % 2
                own = ckown if kind == 0 else cvown
                brow_sb = cbk if kind == 0 else cbv
                veo[i] = k.sec("vector", lambda v, p=p: [
                    v.scalar_tensor_tensor(
                        eo[p][64 * dd:64 * (dd + 1), :],
                        src128[p][64 * dd:64 * (dd + 1), dd::2], 1.0,
                        pos_sb[p][64 * dd:64 * (dd + 1), :].rearrange(
                            "p (one m) -> p one m", one=1)
                        .broadcast_to([64, 32, 32]),
                        op0=OP.mult, op1=OP.add) for dd in range(2)],
                    waits=[gld9[i], ("t", tck.get(i - 2))])
                tck[i] = k.sec("tensor", lambda t, h=h, brow_sb=brow_sb, p=p: (
                    [t.matmul(ps_ck[p][:, :], eo[p][:, tau::32],
                              wc_sb9[p][:, tau * 64:(tau + 1) * 64],
                              start=(tau == 0), stop=False) for tau in range(32)],
                    t.matmul(ps_ck[p][:, :], ones_bf[0:1, 0:32],
                             brow_sb[0:1, h * 64:(h + 1) * 64],
                             start=False, stop=True)),
                    waits=[("v", veo[i]), ("v", vck.get(i - 2)), cbdma])
                vck[i] = k.sec("vector", lambda v, h=h, own=own, p=p: v.tensor_copy(
                    own[:, h * 64:(h + 1) * 64], ps_ck[p][:, :]),
                    waits=[("t", tck[i])])
            k.sec("gpsimd", lambda g: (
                g.dma_start(out=cin_ck[:, 0:2048], in_=ckown[:, :]),
                g.dma_start(out=cin_ck[:, 2048:4096], in_=cvown[:, :])),
                waits=("v",))
            k.sec("gpsimd", lambda g: g.collective_compute(
                "AllGather", OP.bypass, replica_groups=PAIRS,
                ins=[cin_ck.opt()], outs=[cout_ck.opt()]), waits=("gd",))
        for nxt in prep_iter:
            nxt()
        st11.close()
        tp = tapout("ckg", [64, 4096], BF16)
        if tp is not None:
            tap_dram(tp, cout_ck, 64, 4096, BF16)

        k.t_drain_mode = "full"
        fence()
        with ExitStack() as st10:
            wo1_sb = sb("wo1_sb", [128, IC * D], BF16, st=st10)
            ckg = sb("ckg", [128, D], BF16, st=st10)
            cva = sb("cva", [68, D], BF16, st=st10)
            ckT = sb("ckT", [128, 16 * 128], BF16, st=st10)
            mk_sb = sb("mk_sb", [128, 64], BF16, st=st10)
            qtile = [sb("qtile%d" % i, [128, IC * 128], BF16, st=st10)
                     for i in range(2)]
            ltile = [sb("ltile%d" % i, [128, D], BF16, st=st10) for i in range(2)]
            e_c = [sb("e_c10%d" % i, [68, 512], BF16, st=st10) for i in range(2)]

            mcp = sb("mcp", [68, NT * 512], BF16, st=st10)
            s1c = sb("s1c", [128, 32], st=st10)
            s2c = sb("s2c", [128, 32], st=st10)
            rc32 = sb("rc32", [128, 32], st=st10)
            tmps = sb("tmps", [128, 512], st=st10)
            combf = sb("combf", [128, D], BF16, st=st10)
            comb_bf = sb("comb_bf", [128, D], BF16, st=st10)
            combT = sb("combT", [128, D], BF16, st=st10)
            ao10 = sb("ao10", [128, D], st=st10)
            rn110 = [sb("rn110%d" % i, [128, D], st=st10) for i in range(2)]
            rn010 = [sb("rn010%d" % i, [128, D], st=st10) for i in range(2)]
            yb10 = sb("yb10", [128, D], st=st10)
            xh10 = [sb("xh10%d" % i, [128, D], BF16, st=st10) for i in range(2)]
            xtst10 = [sb("xtst10%d" % i, [128, D], BF16, st=st10) for i in range(2)]
            ps_cs = [psum("ps_cs%d" % i, [128, 512], st=st10) for i in range(2)]
            ps_co = [psum("ps_co%d" % i, [128, 512], st=st10) for i in range(4)]
            ps_dn = psum("ps_dn", [128, 256], st=st10)
            pt10 = psum("ptx10", [128, 1024], BF16, st=st10)

            gw10 = dsec("gpsimd", lambda g: (
                g.dma_start(out=wo1_sb[:, :].rearrange("p (ec o) -> p ec o", ec=IC),
                            in_=wout1.rearrange("(ec p) o -> p ec o", p=128)),
                g.dma_start(out=ckg[0:64, :], in_=cout_ck[:, 0:2048]),
                g.dma_start(out=ckg[64:128, :], in_=cout_ck[:, 0:2048]),
                g.dma_start(out=cva[4:68, :], in_=cout_ck[:, 2048:4096]),
                g.dma_start(out=cva[0:4, :], in_=memv_row),
                g.dma_start(out=mk_sb[:, :], in_=memkT),
                g.dma_start(out=mcp[:, :].rearrange("p (a q) -> p a q", a=NT),
                            in_=mcomp_in.transpose([1, 0, 2]))), waits=("c", "v"))
            tck10 = {}
            vck10 = {}
            zck = k.sec("vector", lambda v: v.memset(ckT[:, :], 0.0), waits=("t", "s"))
            for hh in range(16):
                tck10[hh] = k.sec("tensor", lambda t, hh=hh: [
                    t.transpose(pt10[par * 64:(par + 1) * 64, 0:64],
                                ckg[par * 64:par * 64 + 64,
                                    (2 * hh + par) * 64:(2 * hh + par + 1) * 64],
                                ident[par * 64:par * 64 + 64,
                                      par * 64:par * 64 + 64])
                    for par in range(2)],
                    waits=[gw10, ("v", vck10.get(hh - 1))])
                vck10[hh] = k.sec("vector", lambda v, hh=hh: (
                    v.tensor_copy(ckT[0:64, hh * 128 + 4: hh * 128 + 68],
                                  pt10[0:64, 0:64]),
                    v.tensor_copy(ckT[64:128, hh * 128 + 4: hh * 128 + 68],
                                  pt10[64:128, 0:64]),
                    v.tensor_copy(ckT[0:64, hh * 128: hh * 128 + 4],
                                  mk_sb[0:64, hh * 4: hh * 4 + 4]),
                    v.tensor_copy(ckT[64:128, hh * 128: hh * 128 + 4],
                                  mk_sb[64:128, hh * 4: hh * 4 + 4])),
                    waits=[("t", tck10[hh])])

            yq = {}
            tcs = {}
            vmask = {}
            sexp10 = {}
            tout = {}
            vrc = {}
            vcomb = {}
            trc10 = {}
            trev10 = {}
            wout_t = {}
            ev10 = {}
            ap210 = {}
            addv10 = {}
            rd10 = {}
            gdres10 = {}
            xtev10 = {}
            xtgd10 = {}

            yq[0] = dsec("sync", lambda y: (
                y.dma_start(out=qtile[0][:, :].rearrange("p (a t) -> p a t", a=IC),
                            in_=Q_h[:, :, 0:128].transpose([1, 0, 2])),
                y.dma_start(out=ltile[0][:, :], in_=LOC_h[0])),
                waits=[("v", vck10[15])])

            def emit_xt10(tcx):
                if tcx < 0:
                    return
                for half in range(2):
                    tx = k.sec("tensor", lambda t, tcx=tcx, half=half: [
                        t.transpose(pt10[:, j * 128:(j + 1) * 128],
                                    xh10[tcx % 2][:, half * 1024 + j * 128:
                                                  half * 1024 + (j + 1) * 128],
                                    ident[:, :]) for j in range(8)],
                        waits=[("s", ap210[tcx])] +
                              ([("v", xtev10[(tcx, 0)])] if half else
                               [("v", xtev10.get((tcx - 1, 1)))]))
                    xtev10[(tcx, half)] = k.sec(
                        "vector", lambda v, tcx=tcx, half=half: v.tensor_copy(
                            xtst10[tcx % 2][:, half * 1024:(half + 1) * 1024],
                            pt10[:, :]),
                        waits=[("t", tx), xtgd10.get(tcx - 2)])
                xtgd10[tcx] = dsec("gpsimd", lambda g, tcx=tcx: g.dma_start(
                    out=XT3_h[tcx], in_=xtst10[tcx % 2][:, :]),
                    waits=[("v", xtev10[(tcx, 1)])])

            for tc in range(NT):
                if tc + 1 < NT:
                    yq[tc + 1] = dsec("sync", lambda y, tc=tc: (
                        y.dma_start(out=qtile[(tc + 1) % 2][:, :].rearrange(
                            "p (a t) -> p a t", a=IC),
                            in_=Q_h[:, :, (tc + 1) * 128:(tc + 2) * 128]
                            .transpose([1, 0, 2])),
                        y.dma_start(out=ltile[(tc + 1) % 2][:, :],
                                    in_=LOC_h[tc + 1])),
                        waits=[("t", tcs.get((tc - 1, 7))),
                               ("v", vrc.get(tc - 1))])
                for hb4 in range(8):
                    p = hb4 % 2
                    tcs[(tc, hb4)] = k.sec("tensor", lambda t, tc=tc, hb4=hb4, p=p: [
                        t.matmul(ps_cs[p][:, i4 * 128:(i4 + 1) * 128],
                                 ckT[((hb4 * 4 + i4) % 2) * 64:
                                     ((hb4 * 4 + i4) % 2) * 64 + 64,
                                     ((hb4 * 4 + i4) // 2) * 128:
                                     ((hb4 * 4 + i4) // 2) * 128 + 128],
                                 qtile[tc % 2][((hb4 * 4 + i4) % 2) * 64:
                                               ((hb4 * 4 + i4) % 2) * 64 + 64,
                                               ((hb4 * 4 + i4) // 2) * 128:
                                               ((hb4 * 4 + i4) // 2) * 128 + 128],
                                 start=True, stop=True) for i4 in range(4)],
                        waits=[yq[tc], ("s", sexp10.get((tc, hb4 - 2)))])
                    sexp10[(tc, hb4)] = k.sec("scalar", lambda s, p=p: s.activation(
                        e_c[p][:, :], ps_cs[p][0:68, :], ACT.Exp, scale=SCALE2),
                        waits=[("t", tcs[(tc, hb4)]),
                               ("v", vmask.get((tc, hb4 - 2))),
                               ("t", tout.get((tc, hb4 - 2)))])
                    vmask[(tc, hb4)] = k.sec("vector", lambda v, tc=tc, p=p:
                                             v.tensor_tensor(
                        e_c[p][:, :], e_c[p][:, :],
                        mcp[:, tc * 512:(tc + 1) * 512], op=OP.mult),
                        waits=[("s", sexp10[(tc, hb4)])])
                    tout[(tc, hb4)] = k.sec("tensor", lambda t, hb4=hb4, p=p: [
                        (t.matmul(ps_co[(hb4 * 4 + i4) // 8][
                            :, ((hb4 * 4 + i4) % 8) * 64:
                            ((hb4 * 4 + i4) % 8 + 1) * 64],
                            e_c[p][:, i4 * 128:(i4 + 1) * 128],
                            cva[:, (hb4 * 4 + i4) * 64:(hb4 * 4 + i4 + 1) * 64],
                            start=True, stop=True),
                         t.matmul(ps_dn[:, (hb4 * 4 + i4) * 8:
                                        (hb4 * 4 + i4) * 8 + 8],
                                  e_c[p][:, i4 * 128:(i4 + 1) * 128],
                                  onescol8[0:68, :], start=True, stop=True))
                        for i4 in range(4)],
                        waits=[("v", vmask[(tc, hb4)]),
                               ("v", ev10.get((tc - 1, (hb4 * 4) // 8))),
                               ("v", vrc.get(tc - 1))])
                vrc[tc] = k.sec("vector", lambda v, tc=tc: (
                    v.reciprocal(rc32[:, :], ps_dn[:, 0:256:8]),
                    v.tensor_tensor(s1c[:, :],
                                    cw_sb[:, tc * 96 + 0: (tc + 1) * 96: 3],
                                    cw_sb[:, tc * 96 + 1: (tc + 1) * 96: 3],
                                    op=OP.add),
                    v.tensor_tensor(s2c[:, :],
                                    cw_sb[:, tc * 96 + 2: (tc + 1) * 96: 3],
                                    rc32[:, :], op=OP.mult),
                    v.tensor_tensor(combf[:, :], ltile[tc % 2][:, :],
                                    s1c[:, :].rearrange("p (m one) -> p m one",
                                                        one=1)
                                    .broadcast_to([128, 32, 64]), op=OP.mult)),
                    waits=[("t", tout[(tc, 7)])])
                for oc in range(4):
                    vcomb[(tc, oc)] = k.sec("vector", lambda v, oc=oc: (
                        v.tensor_tensor(tmps[:, :], ps_co[oc][:, :],
                                        s2c[:, oc * 8:(oc + 1) * 8].rearrange(
                                            "p (m one) -> p m one", one=1)
                                        .broadcast_to([128, 8, 64]), op=OP.mult),
                        v.tensor_tensor(comb_bf[:, oc * 512:(oc + 1) * 512],
                                        combf[:, oc * 512:(oc + 1) * 512],
                                        tmps[:, :], op=OP.add)))
                for half in range(2):
                    trc10[(tc, half)] = k.sec("tensor", lambda t, half=half: [
                        t.transpose(pt10[:, j * 128:(j + 1) * 128],
                                    comb_bf[:, half * 1024 + j * 128:
                                            half * 1024 + (j + 1) * 128],
                                    ident[:, :]) for j in range(8)],
                        waits=[("v", vcomb[(tc, 3)] if half == 0
                                else trev10[(tc, 0)])])
                    trev10[(tc, half)] = k.sec("vector", lambda v, half=half:
                                               v.tensor_copy(
                        combT[:, half * 1024:(half + 1) * 1024], pt10[:, :]),
                        waits=[("t", trc10[(tc, half)])])
                for oc in range(4):
                    wout_t[(tc, oc)] = k.sec("tensor", lambda t, oc=oc: [
                        t.matmul(ps_co[oc][:, :], combT[:, ec * 128:(ec + 1) * 128],
                                 wo1_sb[:, ec * D + oc * 512: ec * D + (oc + 1) * 512],
                                 start=(ec == 0), stop=(ec == IC - 1))
                        for ec in range(IC)],
                        waits=[("v", trev10[(tc, 1)])])
                rd10[tc] = dsec("sync", lambda y, tc=tc: y.dma_start(
                    out=rn110[tc % 2][:, :], in_=R_b[tc * 128:(tc + 1) * 128, :]),
                    waits=[("v", addv10.get(tc - 2))])
                for oc in range(4):
                    ev10[(tc, oc)] = k.sec("vector", lambda v, oc=oc: v.tensor_copy(
                        ao10[:, oc * 512:(oc + 1) * 512], ps_co[oc][:, :]),
                        waits=[("t", wout_t[(tc, oc)])])
                k.sec("vector", lambda v: emit_stats(v, ao10, EPSLN, True),
                      waits=("s",))
                sq110 = k.sec("scalar", lambda s: s.activation(
                    c_tmp[:, :], c_tmp[:, :], ACT.Sqrt), waits=("v",))
                iv110 = k.sec("vector", lambda v: emit_inv(v, True),
                              waits=[("s", sq110)])
                apl10 = k.sec("scalar", lambda s: s.activation(
                    yb10[:, :], ao10[:, :], ACT.Identity, bias=c_nmi[:, 0:1],
                    scale=c_inv[:, 0:1]), waits=[("v", iv110)])
                addv10[tc] = k.sec("vector", lambda v, tc=tc: v.tensor_tensor(
                    rn010[tc % 2][:, :], yb10[:, :], rn110[tc % 2][:, :], op=OP.add),
                    waits=[("s", apl10), rd10[tc],
                           gdres10.get(tc - 2)])
                gdres10[tc] = dsec("gpsimd", lambda g, tc=tc: g.dma_start(
                    out=R_c[tc * 128:(tc + 1) * 128, :], in_=rn010[tc % 2][:, :]),
                    waits=[("v", addv10[tc])])
                k.sec("vector", lambda v, tc=tc: emit_stats(
                    v, rn010[tc % 2], EPSLN, True), waits=("s",))
                sq210 = k.sec("scalar", lambda s: s.activation(
                    c_tmp[:, :], c_tmp[:, :], ACT.Sqrt), waits=("v",))
                iv210 = k.sec("vector", lambda v: emit_inv(v, True),
                              waits=[("s", sq210)])
                ap210[tc] = k.sec("scalar", lambda s, tc=tc: s.activation(
                    xh10[tc % 2][:, :], rn010[tc % 2][:, :], ACT.Identity,
                    bias=c_nmi[:, 0:1], scale=c_inv[:, 0:1]),
                    waits=[("v", iv210)])
                emit_xt10(tc - 1)
            emit_xt10(NT - 1)
        tp = tapout("r_c", [T, D])
        if tp is not None:
            tap_dram(tp, R_c, T, D, F32)

        k.t_drain_mode = "none"
        fence()
        with ExitStack() as st12:
            wt12 = sb("wt12", [128, IC * D], BF16, st=st12)
            xt3t = [sb("xt3t%d" % i, [128, D], BF16, st=st12) for i in range(2)]
            gwt = dsec("gpsimd", lambda g: g.dma_start(
                out=wt12[:, :].rearrange("p (a o) -> p a o", a=IC),
                in_=WT_h.transpose([1, 0, 2])), waits=("v", "t"))
            xt3dma = {}
            xt3dma[0] = dsec("sync", lambda y: y.dma_start(
                out=xt3t[0][:, :], in_=XT3_h[0]), waits=("gd", "v"))

            def pre12(tc, tcnt):
                if tc + 1 < NT:
                    xt3dma[tc + 1] = dsec("sync", lambda y, tc=tc: y.dma_start(
                        out=xt3t[(tc + 1) % 2][:, :], in_=XT3_h[tc + 1]),
                        waits=[("t", tcnt.get((tc - 1, 3)))])

            def mm12(t, tc, oc, bank):
                for ic in range(IC):
                    t.matmul(bank[:, :],
                             xt3t[tc % 2][:, ic * 128:(ic + 1) * 128],
                             wt12[:, ic * D + oc * 512: ic * D + (oc + 1) * 512],
                             start=(ic == 0), stop=False)
                t.matmul(bank[:, :], ones_bf[0:1, 0:128],
                         brow1[0:1, oc * 512:(oc + 1) * 512],
                         start=False, stop=True)

            out_phase(
                st12, "p12", mm12,
                lambda tc: R_c[tc * 128:(tc + 1) * 128, :],
                lambda tc: out_res[tc * 128:(tc + 1) * 128, :],
                prefetch_emit=pre12,
                mm_extra_waits=lambda tc: [xt3dma[tc]] +
                    ([gwt] if tc == 0 else []))

        fins = [(nm2, k.n[nm2]) for nm2 in (k.GD + k.YD)]
        k.sec("gpsimd", lambda g: [g.wait_ge(k.sem[nm2], c * 16)
                                   for nm2, c in fins if c > 0], inc=False)
        s_wt.close()

    return nc, tap_aps


def _prep_shared(i):
    f32 = np.float32
    sh = {}
    sh["wqT"] = np.ascontiguousarray(np.asarray(i["L0_attn_Wq"], f32).T)
    sh["wkT"] = np.ascontiguousarray(np.asarray(i["L0_attn_Wk"], f32).T)
    sh["wvT"] = np.ascontiguousarray(np.asarray(i["L0_attn_Wv"], f32).T)
    sh["woT"] = np.ascontiguousarray(np.asarray(i["L0_attn_Wo"], f32).T)
    sh["bqc"] = np.ascontiguousarray(np.asarray(i["L0_attn_bq"], f32).reshape(IC, 128).T)
    sh["bk_row"] = np.asarray(i["L0_attn_bk"], f32).reshape(1, 512)
    sh["bv_row"] = np.asarray(i["L0_attn_bv"], f32).reshape(1, 512)
    sh["f0wT"] = np.ascontiguousarray(np.asarray(i["L0_ffn_weight"], f32).T)
    sh["f0scale"] = np.asarray(i["L0_ffn_scale_weight"], f32).reshape(1, D)
    sh["f0bias"] = np.asarray(i["L0_ffn_bias"], f32).reshape(1, D)
    sh["wqkv"] = np.ascontiguousarray(np.asarray(i["L1_attn_w_qkv"], f32))
    sh["wcomb"] = np.ascontiguousarray(np.asarray(i["L1_attn_w_comb"], f32))
    sh["bcomb_row"] = np.asarray(i["L1_attn_b_comb"], f32).reshape(1, 96)
    sh["wout1"] = np.ascontiguousarray(np.asarray(i["L1_attn_w_out"], f32))
    sh["rmswc"] = np.ascontiguousarray(np.asarray(i["L1_attn_rms_w"], f32).reshape(IC, 128).T)
    kc = np.asarray(i["L1_attn_kc_w"], f32).transpose(0, 3, 2, 1)
    sh["kc_h"] = np.ascontiguousarray(kc.reshape(H2, 32, 128, 64))
    vc = np.asarray(i["L1_attn_vc_w"], f32).transpose(0, 3, 2, 1)
    sh["vc_h"] = np.ascontiguousarray(vc.reshape(H2, 32, 128, 64))
    sh["kcb_row"] = np.ascontiguousarray(np.asarray(i["L1_attn_kc_b"], f32).reshape(1, H2 * 64))
    sh["vcb_row"] = np.ascontiguousarray(np.asarray(i["L1_attn_vc_b"], f32).reshape(1, H2 * 64))
    kp = np.asarray(i["L1_attn_k_pos"], f32).reshape(H2, 32, 2, 64)
    sh["kpos_eo"] = np.ascontiguousarray(kp.transpose(0, 2, 3, 1).reshape(H2, 128, 32))
    vp = np.asarray(i["L1_attn_v_pos"], f32).reshape(H2, 32, 2, 64)
    sh["vpos_eo"] = np.ascontiguousarray(vp.transpose(0, 2, 3, 1).reshape(H2, 128, 32))
    mk = np.asarray(i["L1_attn_mem_kv"], f32)[0]
    mkT = np.zeros((128, 64), f32)
    for h in range(H2):
        mkT[(h % 2) * 64:(h % 2) * 64 + 64,
            (h // 2) * 4:(h // 2) * 4 + 4] = mk[h].T
    sh["memkT"] = mkT
    sh["memv_row"] = np.ascontiguousarray(
        np.asarray(i["L1_attn_mem_kv"], f32)[1].transpose(1, 0, 2).reshape(4, H2 * 64))
    sh["pT"] = np.ascontiguousarray(
        np.asarray(i["L1_ffn_pweight"], f32).transpose(1, 2, 0).reshape(D, 4 * D))
    sh["nT"] = np.ascontiguousarray(
        np.asarray(i["L1_ffn_nweight"], f32).transpose(1, 2, 0).reshape(D, 4 * D))
    sh["pb_row"] = np.ascontiguousarray(np.asarray(i["L1_ffn_pbias"], f32).T.reshape(1, 4 * D))
    sh["nb_row"] = np.ascontiguousarray(np.asarray(i["L1_ffn_nbias"], f32).T.reshape(1, 4 * D))
    sh["exps_row"] = np.asarray(i["L1_ffn_exps"], f32).reshape(1, 4)
    sh["bexps_row"] = np.asarray(i["L1_ffn_bexps"], f32).reshape(1, 4)
    sh["maskw_row"] = np.asarray(i["L1_ffn_mask_weight"], f32).reshape(1, 4)
    sh["scale1"] = np.asarray(i["L1_ffn_scale"], f32).reshape(1, 1)
    sh["bscale1"] = np.asarray(i["L1_ffn_biasscale"], f32).reshape(1, 1)
    sh["ident_c"] = np.eye(128, dtype=f32)
    sh["ones_row"] = np.ones((1, 512), f32)
    sh["ones_col"] = np.ones((128, 1), f32)
    sh["ones_col8"] = np.ones((128, 8), f32)
    jj = np.arange(128)[:, None]
    qq = np.arange(128)[None, :]
    cur = np.where(jj <= qq, 1.0, 0.0).astype(f32)
    prv = np.where(qq <= jj, 1.0, 0.0).astype(f32)
    sh["mloc_cur"] = np.ascontiguousarray(np.tile(cur, (1, 4)))
    sh["mloc_prev"] = np.ascontiguousarray(np.tile(prv, (1, 4)))
    return sh


def _prep_core(i, sh, b, h):
    d = dict(sh)
    x = np.asarray(i["L0_attn_hidden_states"], np.float32)[b, h * T:(h + 1) * T, :]
    d["x_in"] = np.ascontiguousarray(x)
    prv = sh["mloc_prev"][:, 0:128]
    first = np.zeros((128, 128), np.float32) if h == 0 else prv
    d["mloc_prev0"] = np.ascontiguousarray(
        np.concatenate([first, np.tile(prv, (1, 3))], axis=1))
    mc = np.zeros((NT, 68, 128), np.float32)
    for tc in range(NT):
        gq = h * T + tc * 128 + np.arange(128)
        gc = gq // CBS
        jcol = np.arange(68)
        allowed = (jcol[:, None] >= 4) & ((jcol[:, None] - 4) >= gc[None, :])
        mc[tc] = np.where(allowed, 1.0, 0.0)
    d["mcomp"] = np.ascontiguousarray(np.tile(mc, (1, 1, 4)))
    return d


_CACHED = {}


def kernel(**inputs):
    taps = tuple(TAPS)
    key = ("prog", taps)
    if key not in _CACHED:
        _CACHED[key] = build_program(taps)
    nc, tap_aps = _CACHED[key]
    sh = _prep_shared(inputs)
    in_maps = [_prep_core(inputs, sh, c // 2, c % 2) for c in range(8)]
    res = run_bass_kernel_spmd(nc, in_maps, core_ids=list(range(8)))
    out = np.zeros((4, 2 * T, D), np.float32)
    for c in range(8):
        out[c // 2, (c % 2) * T:(c % 2 + 1) * T, :] = res.results[c]["out_res"]
    if taps:
        kernel.tap_results = [{n: r["tap_" + n] for n in taps}
                              for r in res.results]
    return out


# version 18
# speedup vs baseline: 2.0238x; 1.0480x over previous
import os
import numpy as np
import ml_dtypes
import concourse.bass as bass
import concourse.mybir as mybir
from concourse.bass_utils import run_bass_kernel_spmd
from contextlib import ExitStack

F32 = mybir.dt.float32
BF16 = mybir.dt.bfloat16
OP = mybir.AluOpType
ACT = mybir.ActivationFunctionType
bfdt = ml_dtypes.bfloat16

D = 2048
T = 2048
NT = 16
IC = 16
H1, KVH1 = 16, 4
H2, DH, CBS = 32, 64, 64
SCALE2 = float(DH ** -0.5)
NEG = -1.0e30
EPS0 = 1e-6
EPSLN = 1e-5
EPSRMS = float(np.finfo(np.float32).eps)
PAIRS = [[0, 1], [2, 3], [4, 5], [6, 7]]

TAPS = [t for t in os.environ.get("SS_TAPS", "").split(",") if t]


def _is_ap(x):
    return hasattr(x, "tensor") and hasattr(x, "ap") and hasattr(x, "dtype")


class EW:

    def __init__(self, K, name, eng):
        self.K, self.name, self.e = K, name, eng
        self.last = None
        if name == "gpsimd":
            self.dma_sem = K.GD[K.dma_rr["gpsimd"] % len(K.GD)]
        elif name == "sync":
            self.dma_sem = K.YD[K.dma_rr["sync"] % len(K.YD)]
        else:
            self.dma_sem = None
        self.entry_count = K.n.get(self.dma_sem, 0)
        self.gated = False
        self.t_drained = False

    def wait_ge(self, sem, val):
        return self.e.wait_ge(sem, val)

    def nop(self):
        return self.e.nop()

    def drain(self):
        self.K.dirty_w[self.name].clear()
        self.K.dirty_r[self.name].clear()
        return self.e.drain()

    def __getattr__(self, op):
        fn = getattr(self.e, op)

        def wrap(*a, **kw):
            K, nm = self.K, self.name
            if op == "dma_start":
                key = self.dma_sem
                if not self.gated:
                    if self.entry_count > 0:
                        self.e.wait_ge(K.sem[key], self.entry_count * 16)
                    self.gated = True
                    K.dma_rr[nm] += 1
                r = fn(*a, **kw)
                r.then_inc(K.sem[key], 16)
                K.n[key] += 1
                return r
            if op == "collective_compute":
                r = fn(*a, **kw)
                r.then_inc(K.sem["c"], 1)
                K.n["c"] += 1
                return r
            outs, ins = [], []
            pos = list(a)
            if pos and _is_ap(pos[0]):
                outs.append(pos[0])
                pos = pos[1:]
            for x in pos:
                if _is_ap(x):
                    ins.append(x)
            for k2, v2 in kw.items():
                if _is_ap(v2):
                    (outs if k2 in ("out", "accum_out") else ins).append(v2)
            in_names = {x.tensor.name for x in ins}
            out_names = {x.tensor.name for x in outs}
            dw, dr = K.dirty_w[nm], K.dirty_r[nm]
            if (in_names & dw) or (out_names & (dw | dr)):
                dw.clear()
                dr.clear()
                mode = self.K.t_drain_mode if nm == "tensor" else "full"
                if (mode == "full" or (mode == "section" and not self.t_drained)):
                    self.e.drain()
                    self.t_drained = True
                elif nm == "tensor" and mode == "none":
                    pass
            dr |= in_names
            dw |= out_names
            r = fn(*a, **kw)
            self.last = r
            return r

        return wrap


class K:
    ENGS = ("tensor", "vector", "scalar", "gpsimd", "sync")

    def __init__(self, nc, block, stack):
        self.nc, self.block = nc, block
        self.sem = {}
        self.n = {}
        for s in ("t", "v", "s", "g", "c",
                  "gd0", "gd1", "gd2", "gd3", "yd0", "yd1"):
            self.sem[s] = stack.enter_context(nc.semaphore("sem_" + s))
            self.n[s] = 0
        self.GD = ("gd0", "gd1", "gd2", "gd3")
        self.YD = ("yd0", "yd1")
        self.dma_rr = {"gpsimd": 0, "sync": 0}
        self.t_drain_mode = "full"
        self.dirty_w = {e: set() for e in self.ENGS}
        self.dirty_r = {e: set() for e in self.ENGS}
        self.esem = {"tensor": "t", "vector": "v", "scalar": "s",
                     "gpsimd": "g", "sync": None}

    def sec(self, engname, body, waits=(), inc=True):
        cap = []
        for w in waits:
            if isinstance(w, str):
                if w == "gd":
                    cap += [(x, self.n[x]) for x in self.GD]
                elif w == "yd":
                    cap += [(x, self.n[x]) for x in self.YD]
                else:
                    cap.append((w, self.n[w]))
            elif w is None:
                pass
            else:
                nm2, c = w
                if c is not None and c > 0:
                    cap.append((nm2, min(c, self.n[nm2])))
        sem = self.sem
        esem = self.esem[engname]

        def f(e):
            w = EW(self, engname, e)
            for nm2, c in cap:
                tgt = c * (16 if nm2[0] in "gy" else 1)
                if tgt > 0:
                    e.wait_ge(sem[nm2], tgt)
            body(w)
            if inc and esem is not None:
                if w.last is not None:
                    w.last.then_inc(sem[esem], 1)
                else:
                    e.nop().then_inc(sem[esem], 1)
                self.n[esem] += 1

        getattr(self.block, engname)(f)
        return self.n[esem] if esem is not None else None


def build_program(taps=()):
    nc = bass.Bass("TRN2", target_bir_lowering=False, debug=False)

    def inp(name, shape, dt=F32):
        return nc.dram_tensor(name, shape, dt, kind="ExternalInput").ap()

    x_in = inp("x_in", [T, D])
    wqT = inp("wqT", [D, D])
    wkT = inp("wkT", [D, 512])
    wvT = inp("wvT", [D, 512])
    woT = inp("woT", [D, D])
    bqc = inp("bqc", [128, IC])
    bk_row = inp("bk_row", [1, 512])
    bv_row = inp("bv_row", [1, 512])
    f0wT = inp("f0wT", [D, D])
    f0scale = inp("f0scale", [1, D])
    f0bias = inp("f0bias", [1, D])
    wqkv = inp("wqkv", [D, 3 * D])
    wcomb = inp("wcomb", [D, 96])
    bcomb_row = inp("bcomb_row", [1, 96])
    wout1 = inp("wout1", [D, D])
    rmswc = inp("rmswc", [128, IC])
    kc_h = inp("kc_h", [H2, 32, 128, 64])
    vc_h = inp("vc_h", [H2, 32, 128, 64])
    kcb_row = inp("kcb_row", [1, H2 * 64])
    vcb_row = inp("vcb_row", [1, H2 * 64])
    kpos_eo = inp("kpos_eo", [H2, 128, 32])
    vpos_eo = inp("vpos_eo", [H2, 128, 32])
    memkT = inp("memkT", [128, 64])
    memv_row = inp("memv_row", [4, H2 * 64])
    pT = inp("pT", [256, 4 * D])
    nT = inp("nT", [256, 4 * D])
    pb_row = inp("pb_row", [1, 4 * D])
    nb_row = inp("nb_row", [1, 4 * D])
    exps_row = inp("exps_row", [1, 4])
    bexps_row = inp("bexps_row", [1, 4])
    maskw_row = inp("maskw_row", [1, 4])
    scale1 = inp("scale1", [1, 1])
    bscale1 = inp("bscale1", [1, 1])
    ident_in = inp("ident_c", [128, 128])
    ones_row = inp("ones_row", [1, 512])
    ones_col = inp("ones_col", [128, 1])
    ones_col8 = inp("ones_col8", [128, 8])
    mloc_cur_in = inp("mloc_cur", [128, 512])
    mloc_prev_in = inp("mloc_prev", [128, 512])
    mloc_prev0_in = inp("mloc_prev0", [128, 512])
    mcomp_in = inp("mcomp", [NT, 68, 512])

    out_res = nc.dram_tensor("out_res", [T, D], F32, kind="ExternalOutput").ap()

    def scr(name, shape, dt):
        return nc.dram_tensor(name, shape, dt).ap()

    R_a = scr("R_a", [T, D], F32)
    R_b = scr("R_b", [T, D], F32)
    R_c = scr("R_c", [T, D], F32)
    XT1_h = scr("XT1_h", [NT, 128, D], BF16)
    XT3_h = scr("XT3_h", [NT, 128, D], BF16)
    Q_h = scr("Q_h", [IC, 128, T], BF16)
    KT_h = scr("KT_h", [IC, 128, T], BF16)
    VTF_h = scr("VTF_h", [IC, 128, T], BF16)
    V_h = scr("V_h", [NT, 128, D], BF16)
    LOC_h = scr("LOC_h", [NT, 128, D], BF16)
    WT_h = scr("WT_h", [IC, 128, D], BF16)
    cin_wt = scr("cin_wt", [256, D], BF16)
    cout_wt = scr("cout_wt", [2048, D], BF16)
    cin_kv = scr("cin_kv", [128, 516], F32)
    cout_kv = scr("cout_kv", [128, 516], F32)
    cin_bd = scr("cin_bd", [128, 4096], BF16)
    cout_bd = scr("cout_bd", [256, 4096], BF16)
    cin_ck = scr("cin_ck", [32, 4096], BF16)
    cout_ck = scr("cout_ck", [64, 4096], BF16)

    tap_aps = {}

    def tapout(name, shape, dt=F32):
        if name in taps:
            tap_aps[name] = nc.dram_tensor(
                "tap_" + name, shape, dt, kind="ExternalOutput").ap()
            return tap_aps[name]
        return None

    with ExitStack() as ST:
        k = K(nc, ST.enter_context(nc.Block()), ST)

        def sb(name, shape, dt=F32, st=None, side=None):
            return (st or ST).enter_context(nc.sbuf_tensor(name, shape, dt, side=side))

        def psum(name, shape, dt=F32, st=None):
            return (st or ST).enter_context(nc.psum_tensor(name, shape, dt))

        ident = sb("ident", [128, 128], BF16)
        ones_bf = sb("ones_bf", [1, 512], BF16)
        onescol_bf = sb("onescol_bf", [128, 1], BF16)
        onescol8 = sb("onescol8", [128, 8], BF16)
        ones_f = sb("ones_f", [1, 512], F32)
        bn6 = sb("bn6", [128, 24])
        mv = sb("mv", [128, 2])
        c_inv = sb("c_inv", [128, 1])
        c_tmp = sb("c_tmp", [128, 1])
        c_nmi = sb("c_nmi", [128, 1])
        kv_bf = sb("kv_bf", [128, 512], BF16)
        ksum_bf = sb("ksum_bf", [128, 4], BF16)

        k.sec("gpsimd", lambda g: (
            g.dma_start(out=ident[:, :], in_=ident_in),
            g.dma_start(out=ones_bf[:, :], in_=ones_row),
            g.dma_start(out=onescol_bf[:, :], in_=ones_col),
            g.dma_start(out=onescol8[:, :], in_=ones_col8)))
        k.sec("sync", lambda y: y.dma_start(out=ones_f[:, :], in_=ones_row))

        def emit_stats(v, src, eps, mean):
            for i in range(4):
                v.bn_stats(bn6[:, 6 * i:6 * (i + 1)], src[:, 512 * i:512 * (i + 1)])
            v.bn_aggr(mv[:, :], bn6[:, :])
            if mean:
                v.tensor_scalar(c_tmp[:, :], mv[:, 1:2], 1.0, eps,
                                op0=OP.mult, op1=OP.add)
            else:
                v.tensor_tensor(c_tmp[:, :], mv[:, 0:1], mv[:, 0:1], op=OP.mult)
                v.tensor_tensor(c_tmp[:, :], c_tmp[:, :], mv[:, 1:2], op=OP.add)
                v.tensor_scalar(c_tmp[:, :], c_tmp[:, :], 1.0, eps,
                                op0=OP.mult, op1=OP.add)

        def emit_inv(v, mean):
            v.reciprocal(c_inv[:, :], c_tmp[:, :])
            if mean:
                v.tensor_scalar(c_nmi[:, :], mv[:, 0:1], c_inv[:, 0:1], -1.0,
                                op0=OP.mult, op1=OP.mult)
            else:
                v.memset(c_nmi[:, :], 0.0)

        def norm_cols(waits1, src_sb, eps, mean):
            k.sec("vector", lambda v: emit_stats(v, src_sb, eps, mean), waits=waits1)
            k.sec("scalar", lambda s: s.activation(c_tmp[:, :], c_tmp[:, :], ACT.Sqrt),
                  waits=("v",))
            k.sec("vector", lambda v: emit_inv(v, mean), waits=("s",))

        def transpose_tile(bufs, src_bf, dst_fn, twaits, vwaits=("t",)):
            k.sec("tensor", lambda t: [
                t.transpose(bufs[ic // 8][:, (ic % 8) * 128:(ic % 8 + 1) * 128],
                            src_bf[:, ic * 128:(ic + 1) * 128], ident[:, :])
                for ic in range(IC)], waits=twaits)
            k.sec("vector", lambda v: [dst_fn(v, hb, bufs[hb][:, :]) for hb in range(2)],
                  waits=vwaits)



        def fence():
            cnt = dict(k.n)
            for e in ("tensor", "vector", "scalar", "gpsimd", "sync"):
                own = k.esem[e]
                ws = tuple(x for x in ("t", "v", "s", "gd", "yd", "c")
                           if x != own)
                k.sec(e, lambda w: w.drain() if w.name in
                      ("tensor", "vector", "scalar") else w.nop(), waits=ws)

        def dsec(engname, body, waits=()):
            rr = k.dma_rr[engname]
            names = k.GD if engname == "gpsimd" else k.YD
            nm2 = names[rr % len(names)]
            k.sec(engname, body, waits=waits)
            return (nm2, k.n[nm2])

        def tap_dram(tpap, dram_ap, nrows, ncols, dt):
            fence()
            with ExitStack() as stt:
                stage = sb("tapstage_%d" % k.n["gd"], [128, ncols], dt, st=stt)
                nt2 = (nrows + 127) // 128
                for r in range(nt2):
                    h2 = min(128, nrows - r * 128)
                    k.sec("gpsimd", lambda g, r=r, h2=h2: g.dma_start(
                        out=stage[0:h2, :], in_=dram_ap[r * 128:r * 128 + h2, :]),
                        waits=("gd",))
                    k.sec("gpsimd", lambda g, r=r, h2=h2: g.dma_start(
                        out=tpap[r * 128:r * 128 + h2, :], in_=stage[0:h2, :]),
                        waits=("gd",))
        s_xt0 = ExitStack()
        s_att = ExitStack()
        s_xt2 = ExitStack()
        s_cw = ExitStack()
        s_wt = ExitStack()

        fence()
        xt0 = sb("xt0", [128, NT * D], BF16, st=s_xt0)
        with ExitStack() as st1:
            xin_b = [sb("xin%d" % i, [128, D], st=st1) for i in range(2)]
            xhat1b = sb("xhat_p1", [128, D], BF16, st=st1)
            ptb1 = [psum("pt1%d" % i, [128, 1024], BF16, st=st1) for i in range(2)]
            for tc in range(NT):
                bi = tc % 2
                k.sec("sync", lambda y, tc=tc, bi=bi: y.dma_start(
                    out=xin_b[bi][:, :], in_=x_in[tc * 128:(tc + 1) * 128, :]),
                    waits=("s",))
                norm_cols(("yd",), xin_b[bi], EPSLN, True)
                k.sec("scalar", lambda s, bi=bi: s.activation(
                    xhat1b[:, :], xin_b[bi][:, :], ACT.Identity,
                    bias=c_nmi[:, 0:1], scale=c_inv[:, 0:1]), waits=("v",))
                transpose_tile(ptb1, xhat1b,
                               lambda v, hb, ap, tc=tc: v.tensor_copy(
                                   xt0[:, tc * D + hb * 1024: tc * D + (hb + 1) * 1024],
                                   ap), ("s",))
        tp = tapout("xt0", [128, NT * D], BF16)
        if tp is not None:
            k.sec("gpsimd", lambda g: g.dma_start(out=tp, in_=xt0[:, :]), waits=("v",))

        fence()
        with ExitStack() as st2:
            wk_sb = sb("wk_sb", [128, IC * 512], BF16, st=st2)
            wv_sb = sb("wv_sb", [128, IC * 512], BF16, st=st2)
            bkr = sb("bkr", [1, 512], BF16, st=st2)
            bvr = sb("bvr", [1, 512], BF16, st=st2)
            sk_sb = sb("sk_sb", [128, NT * 512], BF16, st=st2)
            v_sb = sb("v_sb", [128, NT * 512], BF16, st=st2)
            r_bf = sb("r_bf2", [128, 512], BF16, st=st2)
            mel = sb("mel2", [128, 512], st=st2)
            ee = sb("ee2", [128, 512], st=st2)
            ps_k = psum("ps_k", [128, 512], st=st2)
            ps_v = psum("ps_v", [128, 512], st=st2)

            k.sec("gpsimd", lambda g: (
                g.dma_start(out=wk_sb[:, :].rearrange("p (ic o) -> p ic o", ic=IC),
                            in_=wkT.rearrange("(ic p) o -> p ic o", p=128)),
                g.dma_start(out=wv_sb[:, :].rearrange("p (ic o) -> p ic o", ic=IC),
                            in_=wvT.rearrange("(ic p) o -> p ic o", p=128)),
                g.dma_start(out=bkr[:, :], in_=bk_row),
                g.dma_start(out=bvr[:, :], in_=bv_row)))
            for tc in range(NT):
                k.sec("tensor", lambda t, tc=tc: (
                    [t.matmul(ps_k[:, :],
                              xt0[:, tc * D + ic * 128: tc * D + (ic + 1) * 128],
                              wk_sb[:, ic * 512:(ic + 1) * 512],
                              start=(ic == 0), stop=False) for ic in range(IC)],
                    t.matmul(ps_k[:, :], ones_bf[0:1, 0:128], bkr[0:1, :],
                             start=False, stop=True),
                    [t.matmul(ps_v[:, :],
                              xt0[:, tc * D + ic * 128: tc * D + (ic + 1) * 128],
                              wv_sb[:, ic * 512:(ic + 1) * 512],
                              start=(ic == 0), stop=False) for ic in range(IC)],
                    t.matmul(ps_v[:, :], ones_bf[0:1, 0:128], bvr[0:1, :],
                             start=False, stop=True)),
                    waits=("gd", "v", "s"))
                k.sec("vector", lambda v, tc=tc: (
                    v.tensor_scalar(r_bf[:, :], ps_k[:, :], 0.0, None, op0=OP.max),
                    v.scalar_tensor_tensor(mel[:, :], ps_k[:, :], 1.0, r_bf[:, :],
                                           op0=OP.mult, op1=OP.subtract),
                    v.tensor_copy(v_sb[:, tc * 512:(tc + 1) * 512], ps_v[:, :])),
                    waits=("t",))
                k.sec("scalar", lambda s: s.activation(ee[:, :], mel[:, :], ACT.Exp),
                      waits=("v",))
                k.sec("vector", lambda v, tc=tc: v.tensor_tensor(
                    sk_sb[:, tc * 512:(tc + 1) * 512], ee[:, :], r_bf[:, :], op=OP.add),
                    waits=("s",))

            ps_kv = psum("ps_kv", [128, 512], st=st2)
            ps_ks = psum("ps_ks", [128, 4], st=st2)
            kvred = sb("kvred", [128, 516], st=st2)

            def p2_kv(t):
                for g in range(KVH1):
                    for tc in range(NT):
                        t.matmul(ps_kv[:, g * 128:(g + 1) * 128],
                                 sk_sb[:, tc * 512 + g * 128: tc * 512 + (g + 1) * 128],
                                 v_sb[:, tc * 512 + g * 128: tc * 512 + (g + 1) * 128],
                                 start=(tc == 0), stop=(tc == NT - 1))
                for g in range(KVH1):
                    for tc in range(NT):
                        t.matmul(ps_ks[:, g:g + 1],
                                 sk_sb[:, tc * 512 + g * 128: tc * 512 + (g + 1) * 128],
                                 onescol_bf[:, :], start=(tc == 0), stop=(tc == NT - 1))
            k.sec("tensor", p2_kv, waits=("v",))
            k.sec("vector", lambda v: (
                v.tensor_copy(kvred[:, 0:512], ps_kv[:, :]),
                v.tensor_copy(kvred[:, 512:516], ps_ks[:, :])), waits=("t",))
            k.sec("gpsimd", lambda g: g.dma_start(out=cin_kv, in_=kvred[:, :]),
                  waits=("v",))
            k.sec("gpsimd", lambda g: g.collective_compute(
                "AllReduce", OP.add, replica_groups=PAIRS,
                ins=[cin_kv.opt()], outs=[cout_kv.opt()]), waits=("gd",))
            k.sec("gpsimd", lambda g: g.dma_start(out=kvred[:, :], in_=cout_kv),
                  waits=("c", "v"))
            k.sec("vector", lambda v: (
                v.tensor_copy(kv_bf[:, :], kvred[:, 0:512]),
                v.tensor_copy(ksum_bf[:, :], kvred[:, 512:516])), waits=("gd",))
            tp = tapout("kvred", [128, 516])
            if tp is not None:
                k.sec("gpsimd", lambda g: g.dma_start(out=tp, in_=kvred[:, :]),
                      waits=("v",))

        k.t_drain_mode = "none"
        fence()
        attnT = sb("attnT", [128, NT * D], BF16, st=s_att, side="right")
        with ExitStack() as st3:
            wq_sb = [sb("wq_sb%d" % i, [128, IC * 512], BF16, st=st3) for i in range(2)]
            bqc_sb = sb("bqc_sb", [128, IC], st=st3)
            sq_sb = [sb("sq_sb%d" % i, [128, D], BF16, st=st3) for i in range(2)]
            sqs_sb = [sb("sqs_sb%d" % i, [128, 512], BF16, st=st3) for i in range(2)]
            r_bf3 = [sb("r_bf3%d" % i, [128, 512], BF16, st=st3) for i in range(2)]
            mel3 = [sb("mel3%d" % i, [128, 512], st=st3) for i in range(2)]
            ee3 = [sb("ee3%d" % i, [128, 512], st=st3) for i in range(2)]
            nrow_tmp = sb("nrow_tmp", [1, 512], st=st3)
            nrow_bf = [sb("nrow_bf%d" % i, [1, D], BF16, st=st3) for i in range(2)]
            ps_q = [psum("ps_q%d" % i, [128, 512], st=st3) for i in range(2)]
            ps_n = [psum("ps_n%d" % i, [1, 512], st=st3) for i in range(2)]
            ps_b = [psum("ps_b%d" % i, [128, 512], st=st3) for i in range(2)]
            ps_a = [psum("ps_a%d" % i, [128, 512], st=st3) for i in range(2)]

            bqdma = dsec("sync", lambda y: y.dma_start(out=bqc_sb[:, :], in_=bqc))
            gq = {}
            tq = {}
            ve1 = {}
            sexp = {}
            ve2 = {}
            tn = {}
            vn = {}
            vr = {}
            tb_ = {}
            vsc = {}
            ta = {}
            vev = {}
            gq[0] = dsec("gpsimd", lambda g: g.dma_start(
                out=wq_sb[0][:, :].rearrange("p (ic o) -> p ic o", ic=IC),
                in_=wqT.rearrange("(ic p) o -> p ic o", p=128)[:, :, 0:512]),
                waits=("t", "v"))
            for h in range(H1):
                g2 = h // 4
                hh = h % 4
                hp = h % 2
                wbuf = wq_sb[g2 % 2]
                if hh == 0 and g2 + 1 < 4:
                    gq[g2 + 1] = dsec("gpsimd", lambda g, g2=g2: g.dma_start(
                        out=wq_sb[(g2 + 1) % 2][:, :].rearrange(
                            "p (ic o) -> p ic o", ic=IC),
                        in_=wqT.rearrange("(ic p) o -> p ic o", p=128)[
                            :, :, (g2 + 1) * 512:(g2 + 2) * 512]),
                        waits=[("t", tq.get((h - 1, 3)))])
                for tb in range(4):
                    st_i = 4 * h + tb
                    p = st_i % 2
                    tq[(h, tb)] = k.sec("tensor", lambda t, hh=hh, tb=tb, p=p,
                                        wbuf=wbuf: [
                        t.matmul(ps_q[p][:, :],
                                 wbuf[:, ic * 512 + hh * 128:
                                      ic * 512 + (hh + 1) * 128],
                                 xt0[:, :].rearrange("p (tc c) -> p tc c", tc=NT)[
                                     :, 4 * tb:4 * tb + 4, ic * 128:(ic + 1) * 128],
                                 start=(ic == 0), stop=(ic == IC - 1))
                        for ic in range(IC)],
                        waits=[gq[g2], ("v", ve1.get(st_i - 2))])
                    ve1[st_i] = k.sec("vector", lambda v, h=h, p=p: (
                        v.tensor_scalar(r_bf3[p][:, :], ps_q[p][:, :],
                                        bqc_sb[:, h:h + 1], 0.0,
                                        op0=OP.add, op1=OP.max),
                        v.scalar_tensor_tensor(mel3[p][:, :], ps_q[p][:, :],
                                               bqc_sb[:, h:h + 1], r_bf3[p][:, :],
                                               op0=OP.add, op1=OP.subtract)),
                        waits=[("t", tq[(h, tb)]), ("s", sexp.get(st_i - 2)),
                               bqdma])
                    sexp[st_i] = k.sec("scalar", lambda s, p=p: s.activation(
                        ee3[p][:, :], mel3[p][:, :], ACT.Exp),
                        waits=[("v", ve1[st_i])])
                    ve2[st_i] = k.sec("vector", lambda v, h=h, tb=tb, p=p:
                                      v.tensor_tensor(
                        sq_sb[h % 2][:, tb * 512:(tb + 1) * 512], ee3[p][:, :],
                        r_bf3[p][:, :], op=OP.add),
                        waits=[("s", sexp[st_i])])
                    tn[(h, tb)] = k.sec("tensor", lambda t, g2=g2, h=h, tb=tb, p=p:
                                        t.matmul(
                        ps_n[p][:, :], ksum_bf[:, g2:g2 + 1],
                        sq_sb[h % 2][:, tb * 512:(tb + 1) * 512],
                        start=True, stop=True),
                        waits=[("v", ve2[st_i]), ("v", vn.get(st_i - 2))])
                    def _vn(v, h=h, tb=tb, p=p):
                        v.tensor_scalar(nrow_tmp[0:1, :], ps_n[p][0:1, :],
                                        1.0, EPS0, op0=OP.mult, op1=OP.add)
                        with nc.allow_low_precision(
                                reason="bf16 recip feeds bf16 matmul operand"):
                            v.reciprocal(
                                nrow_bf[h % 2][0:1, tb * 512:(tb + 1) * 512],
                                nrow_tmp[0:1, :])
                    vn[st_i] = k.sec("vector", _vn,
                                     waits=[("t", tn[(h, tb)]),
                                            ("t", tb_.get((h - 1, tb)))])

                for tb in range(4):
                    st_i = 4 * h + tb
                    p = st_i % 2
                    tb_[(h, tb)] = k.sec("tensor", lambda t, h=h, tb=tb, p=p:
                                         t.matmul(
                        ps_b[p][:, :], ones_bf[0:1, 0:128],
                        nrow_bf[h % 2][0:1, tb * 512:(tb + 1) * 512],
                        start=True, stop=True),
                        waits=[("v", vn[st_i]), ("v", vsc.get(st_i - 2))])
                    vsc[st_i] = k.sec("vector", lambda v, h=h, tb=tb, p=p:
                                      v.tensor_tensor(
                        sqs_sb[p][:, :], sq_sb[h % 2][:, tb * 512:(tb + 1) * 512],
                        ps_b[p][:, :], op=OP.mult),
                        waits=[("t", tb_[(h, tb)])])
                    ta[(h, tb)] = k.sec("tensor", lambda t, g2=g2, p=p: t.matmul(
                        ps_a[p][:, :], kv_bf[:, g2 * 128:(g2 + 1) * 128],
                        sqs_sb[p][:, :], start=True, stop=True),
                        waits=[("v", vsc[st_i]), ("v", vev.get(st_i - 2))])
                    vev[st_i] = k.sec("vector", lambda v, h=h, tb=tb, p=p:
                                      v.tensor_copy(
                        attnT[:, h * D + tb * 512: h * D + (tb + 1) * 512],
                        ps_a[p][:, :]), waits=[("t", ta[(h, tb)])])
        s_xt0.close()
        tp = tapout("attnT", [128, NT * D], BF16)
        if tp is not None:
            k.sec("gpsimd", lambda g: g.dma_start(out=tp, in_=attnT[:, :]), waits=("v",))

        def boundary_chain(bufs, src_psums, res_src, res_dst,
                           extra_scale=None, extra_bias=None):
            ao, rn, yb = bufs["ao"], bufs["rn"], bufs["yb"]
            k.sec("sync", lambda y: y.dma_start(out=rn[1][:, :], in_=res_src),
                  waits=("v",))
            if extra_scale is None:
                k.sec("vector", lambda v: [
                    v.tensor_copy(ao[:, oc * 512:(oc + 1) * 512], src_psums[oc])
                    for oc in range(4)], waits=("t",))
            else:
                k.sec("vector", lambda v: [
                    (v.tensor_tensor(ao[:, oc * 512:(oc + 1) * 512], src_psums[oc],
                                     extra_scale[:, oc * 512:(oc + 1) * 512],
                                     op=OP.mult),
                     v.tensor_tensor(ao[:, oc * 512:(oc + 1) * 512],
                                     ao[:, oc * 512:(oc + 1) * 512],
                                     extra_bias[:, oc * 512:(oc + 1) * 512],
                                     op=OP.add))
                    for oc in range(4)], waits=("t",))
            norm_cols((), ao, EPSLN, True)
            k.sec("scalar", lambda s: s.activation(
                yb[:, :], ao[:, :], ACT.Identity, bias=c_nmi[:, 0:1], scale=c_inv[:, 0:1]),
                waits=("v",))
            k.sec("vector", lambda v: v.tensor_tensor(
                rn[0][:, :], yb[:, :], rn[1][:, :], op=OP.add), waits=("s", "yd"))
            k.sec("gpsimd", lambda g: g.dma_start(out=res_dst, in_=rn[0][:, :]),
                  waits=("v",))

        def next_norm_apply(bufs, eps2, mean2):
            rn, xh = bufs["rn"], bufs["xh"]
            norm_cols((), rn[0], eps2, mean2)
            k.sec("scalar", lambda s: s.activation(
                xh[:, :], rn[0][:, :], ACT.Identity, bias=c_nmi[:, 0:1],
                scale=c_inv[:, 0:1]), waits=("v",))

        def mk_bufs(st, tag):
            return {
                "ao": sb("ao_" + tag, [128, D], st=st),
                "rn": (sb("rn0_" + tag, [128, D], st=st),
                       sb("rn1_" + tag, [128, D], st=st)),
                "yb": sb("yb_" + tag, [128, D], st=st),
                "xh": sb("xh_" + tag, [128, D], BF16, st=st),
                "pt": [psum("pt_" + tag + str(i), [128, 1024], BF16, st=st)
                       for i in range(2)],
            }

        def out_phase(st, tag, mm_emit, res_src_fn, res_dst_fn,
                      extra_scale=None, extra_bias=None,
                      xt_write=None, xt_dma_fn=None, eps2=EPSLN, mean2=True,
                      prefetch_emit=None, mm_extra_waits=lambda tc: []):
            psb = [psum(tag + "_w%d" % i2, [128, 512], st=st) for i2 in range(4)]
            ptb = [psum(tag + "_pt%d" % i2, [128, 1024], BF16, st=st) for i2 in range(2)]
            ao = sb(tag + "_ao", [128, D], st=st)
            rn1 = [sb(tag + "_rn1%d" % i2, [128, D], st=st) for i2 in range(2)]
            rn0 = [sb(tag + "_rn0%d" % i2, [128, D], st=st) for i2 in range(2)]
            yb = sb(tag + "_yb", [128, D], st=st)
            xh = [sb(tag + "_xh%d" % i2, [128, D], BF16, st=st) for i2 in range(2)]
            ev = {}
            tcnt = {}
            ap2 = {}
            addv = {}
            rd = {}
            gdres = {}
            trc = {}
            xtev = {}

            xtgd = {}

            def emit_transpose(tcx):
                if xt_write is None or tcx < 0:
                    return
                trc[tcx] = k.sec("tensor", lambda t, tcx=tcx: [
                    t.transpose(ptb[ic // 8][:, (ic % 8) * 128:(ic % 8 + 1) * 128],
                                xh[tcx % 2][:, ic * 128:(ic + 1) * 128], ident[:, :])
                    for ic in range(IC)],
                    waits=[("s", ap2[tcx]), ("v", xtev.get(tcx - 1))])
                xtev[tcx] = k.sec("vector", lambda v, tcx=tcx: [
                    xt_write(v, tcx, hb, ptb[hb][:, :]) for hb in range(2)],
                    waits=[("t", trc[tcx]), xtgd.get(tcx - 2)])
                if xt_dma_fn is not None:
                    xtgd[tcx] = dsec("gpsimd", lambda g, tcx=tcx: xt_dma_fn(g, tcx),
                                     waits=[("v", xtev[tcx])])

            for tc in range(NT):
                if prefetch_emit is not None:
                    prefetch_emit(tc, tcnt)
                rd[tc] = dsec("sync", lambda y, tc=tc: y.dma_start(
                    out=rn1[tc % 2][:, :], in_=res_src_fn(tc)),
                    waits=[("v", addv.get(tc - 2))])
                for oc in range(4):
                    tcnt[(tc, oc)] = k.sec(
                        "tensor", lambda t, tc=tc, oc=oc: mm_emit(t, tc, oc, psb[oc]),
                        waits=[("v", ev.get((tc - 1, oc)))] + mm_extra_waits(tc))
                for oc in range(4):
                    if extra_scale is None:
                        ev[(tc, oc)] = k.sec("vector", lambda v, oc=oc: v.tensor_copy(
                            ao[:, oc * 512:(oc + 1) * 512], psb[oc][:, :]),
                            waits=[("t", tcnt[(tc, oc)])])
                    else:
                        ev[(tc, oc)] = k.sec("vector", lambda v, oc=oc: (
                            v.tensor_tensor(ao[:, oc * 512:(oc + 1) * 512],
                                            psb[oc][:, :],
                                            extra_scale[:, oc * 512:(oc + 1) * 512],
                                            op=OP.mult),
                            v.tensor_tensor(ao[:, oc * 512:(oc + 1) * 512],
                                            ao[:, oc * 512:(oc + 1) * 512],
                                            extra_bias[:, oc * 512:(oc + 1) * 512],
                                            op=OP.add)),
                            waits=[("t", tcnt[(tc, oc)])])
                k.sec("vector", lambda v: emit_stats(v, ao, EPSLN, True),
                      waits=("s",))
                sq1 = k.sec("scalar", lambda s: s.activation(c_tmp[:, :], c_tmp[:, :],
                                                             ACT.Sqrt), waits=("v",))
                iv1 = k.sec("vector", lambda v: emit_inv(v, True),
                            waits=[("s", sq1)])
                apl = k.sec("scalar", lambda s: s.activation(
                    yb[:, :], ao[:, :], ACT.Identity, bias=c_nmi[:, 0:1],
                    scale=c_inv[:, 0:1]), waits=[("v", iv1)])
                addv[tc] = k.sec("vector", lambda v, tc=tc: v.tensor_tensor(
                    rn0[tc % 2][:, :], yb[:, :], rn1[tc % 2][:, :], op=OP.add),
                    waits=[("s", apl), rd[tc], gdres.get(tc - 2)])
                gdres[tc] = dsec("gpsimd", lambda g, tc=tc: g.dma_start(
                    out=res_dst_fn(tc), in_=rn0[tc % 2][:, :]),
                    waits=[("v", addv[tc])])
                if xt_write is not None:
                    k.sec("vector", lambda v, tc=tc: emit_stats(
                        v, rn0[tc % 2], eps2, mean2), waits=("s",))
                    sq2 = k.sec("scalar", lambda s: s.activation(
                        c_tmp[:, :], c_tmp[:, :], ACT.Sqrt), waits=("v",))
                    iv2 = k.sec("vector", lambda v: emit_inv(v, mean2),
                                waits=[("s", sq2)])
                    ap2[tc] = k.sec("scalar", lambda s, tc=tc: s.activation(
                        xh[tc % 2][:, :], rn0[tc % 2][:, :], ACT.Identity,
                        bias=c_nmi[:, 0:1], scale=c_inv[:, 0:1]),
                        waits=[("v", iv2)])
                emit_transpose(tc - 1)
            emit_transpose(NT - 1)
            return gdres, xtev

        k.t_drain_mode = "none"
        fence()
        with ExitStack() as st4:
            wo_sb = sb("wo_sb", [128, IC * D], BF16, st=st4)
            xtst = [sb("xtst4%d" % i, [128, D], BF16, st=st4) for i in range(2)]
            gwo = dsec("gpsimd", lambda g: g.dma_start(
                out=wo_sb[:, :].rearrange("p (ec o) -> p ec o", ec=IC),
                in_=woT.rearrange("(ec p) o -> p ec o", p=128)), waits=("v",))
            xt1dma = {}

            def mm4(t, tc, oc, bank):
                for ec in range(IC):
                    t.matmul(bank[:, :],
                             attnT[:, ec * D + tc * 128: ec * D + (tc + 1) * 128],
                             wo_sb[:, ec * D + oc * 512: ec * D + (oc + 1) * 512],
                             start=(ec == 0), stop=(ec == IC - 1))

            def xtw4(v, tcx, hb, ap):
                v.tensor_copy(xtst[tcx % 2][:, hb * 1024:(hb + 1) * 1024], ap)

            out_phase(
                st4, "p4", mm4,
                lambda tc: x_in[tc * 128:(tc + 1) * 128, :],
                lambda tc: R_a[tc * 128:(tc + 1) * 128, :],
                xt_write=xtw4,
                xt_dma_fn=lambda g, tcx: g.dma_start(out=XT1_h[tcx],
                                                     in_=xtst[tcx % 2][:, :]),
                mm_extra_waits=lambda tc: [gwo] if tc == 0 else [])
        s_att.close()
        tp = tapout("r_a", [T, D])
        if tp is not None:
            tap_dram(tp, R_a, T, D, F32)

        fence()
        xt2 = sb("xt2", [128, NT * D], BF16, st=s_xt2)
        with ExitStack() as st5:
            fw_sb = sb("fw_sb", [128, IC * D], BF16, st=st5)
            scl_sb = sb("scl_sb", [128, D], BF16, st=st5)
            bias_sb = sb("bias_sb", [128, D], BF16, st=st5)
            xt1t = [sb("xt1t%d" % i, [128, D], BF16, st=st5) for i in range(2)]
            ps_bc = psum("ps_bc", [128, 512], st=st5)

            gfw = dsec("gpsimd", lambda g: g.dma_start(
                out=fw_sb[:, :].rearrange("p (ic o) -> p ic o", ic=IC),
                in_=f0wT.rearrange("(ic p) o -> p ic o", p=128)), waits=("v", "t"))
            with ExitStack() as st5b:
                srow = sb("srow", [1, D], st=st5b)
                brow = sb("brow", [1, D], st=st5b)
                k.sec("sync", lambda y: (y.dma_start(out=srow[:, :], in_=f0scale),
                                         y.dma_start(out=brow[:, :], in_=f0bias)))
                for row, dst in [(srow, scl_sb), (brow, bias_sb)]:
                    for tb in range(4):
                        k.sec("tensor", lambda t, row=row, tb=tb: t.matmul(
                            ps_bc[:, :], ones_f[0:1, 0:128],
                            row[0:1, tb * 512:(tb + 1) * 512],
                            start=True, stop=True), waits=("yd", "v"))
                        k.sec("vector", lambda v, dst=dst, tb=tb: v.tensor_copy(
                            dst[:, tb * 512:(tb + 1) * 512], ps_bc[:, :]),
                            waits=("t",))
            fence()

            xt1dma = {}
            xt1dma[0] = dsec("sync", lambda y: y.dma_start(
                out=xt1t[0][:, :], in_=XT1_h[0]), waits=("gd",))

            def pre5(tc, tcnt):
                if tc + 1 < NT:
                    xt1dma[tc + 1] = dsec("sync", lambda y, tc=tc: y.dma_start(
                        out=xt1t[(tc + 1) % 2][:, :], in_=XT1_h[tc + 1]),
                        waits=[("t", tcnt.get((tc - 1, 3)))])

            def mm5(t, tc, oc, bank):
                for ic in range(IC):
                    t.matmul(bank[:, :],
                             xt1t[tc % 2][:, ic * 128:(ic + 1) * 128],
                             fw_sb[:, ic * D + oc * 512: ic * D + (oc + 1) * 512],
                             start=(ic == 0), stop=(ic == IC - 1))

            def xtw5(v, tcx, hb, ap):
                v.tensor_copy(xt2[:, tcx * D + hb * 1024: tcx * D + (hb + 1) * 1024],
                              ap)

            out_phase(
                st5, "p5", mm5,
                lambda tc: R_a[tc * 128:(tc + 1) * 128, :],
                lambda tc: R_b[tc * 128:(tc + 1) * 128, :],
                extra_scale=scl_sb, extra_bias=bias_sb,
                xt_write=xtw5, eps2=EPSRMS, mean2=False,
                prefetch_emit=pre5,
                mm_extra_waits=lambda tc: [xt1dma[tc]] +
                    ([gfw] if tc == 0 else []))
        tp = tapout("r_b", [T, D])
        if tp is not None:
            tap_dram(tp, R_b, T, D, F32)
        tp = tapout("xt2", [128, NT * D], BF16)
        if tp is not None:
            k.sec("gpsimd", lambda g: g.dma_start(out=tp, in_=xt2[:, :]), waits=("v",))

        fence()
        cw_sb = sb("cw_sb", [128, NT * 96], st=s_cw, side="right")
        with ExitStack() as st6:
            wsec = sb("wsec", [128, IC * 1024], BF16, st=st6)
            rms_sb = sb("rms_sb", [128, IC], st=st6)
            qstage = [sb("qstage%d" % i, [128, 512], BF16, st=st6) for i in range(2)]
            vstage = [sb("vstage%d" % i, [128, 1024], BF16, st=st6) for i in range(2)]
            vtst = [sb("vtst%d" % i, [128, 1024], BF16, st=st6) for i in range(2)]
            ps_p = [psum("ps_p6%d" % i, [128, 512], st=st6) for i in range(2)]
            ps_v4 = [psum("ps_v4%d" % i, [128, 512], st=st6) for i in range(4)]
            ptv = [psum("ptv%d" % i, [128, 1024], BF16, st=st6) for i in range(2)]

            k.sec("sync", lambda y: y.dma_start(out=rms_sb[:, :], in_=rmswc))
            ev6 = {}
            gv6 = {}
            gqk = {}
            tv6 = {}
            vtev = {}
            vtgd = {}
            step = 0
            vstep = 0
            for osec in range(6):
                gw = k.sec("gpsimd", lambda g, osec=osec: g.dma_start(
                    out=wsec[:, :].rearrange("p (ic o) -> p ic o", ic=IC),
                    in_=wqkv.rearrange("(ic p) o -> p ic o", p=128)[
                        :, :, osec * 1024:(osec + 1) * 1024]), waits=("t",))
                fold = k.sec("vector", lambda v: [
                    v.tensor_scalar_mul(wsec[:, ic * 1024:(ic + 1) * 1024],
                                        wsec[:, ic * 1024:(ic + 1) * 1024],
                                        rms_sb[:, ic:ic + 1]) for ic in range(IC)],
                    waits=("gd", "yd"))
                if osec < 4:
                    for oc8 in range(8):
                        oc = osec * 8 + oc8
                        dst_h = Q_h if oc < 16 else KT_h
                        oc_l = oc % 16
                        for tb in range(4):
                            par = step % 2
                            tcnt6 = k.sec("tensor", lambda t, oc8=oc8, tb=tb,
                                          par=par: [
                                t.matmul(ps_p[par][:, :],
                                         wsec[:, ic * 1024 + oc8 * 128:
                                              ic * 1024 + (oc8 + 1) * 128],
                                         xt2[:, :].rearrange(
                                             "p (tc c) -> p tc c", tc=NT)[
                                             :, 4 * tb:4 * tb + 4,
                                             ic * 128:(ic + 1) * 128],
                                         start=(ic == 0), stop=(ic == IC - 1))
                                for ic in range(IC)],
                                waits=[("v", fold), ("v", ev6.get(step - 2))])
                            ev6[step] = k.sec(
                                "vector", lambda v, par=par: v.tensor_copy(
                                    qstage[par][:, :], ps_p[par][:, :]),
                                waits=[("t", tcnt6), gqk.get(step - 2)])
                            gqk[step] = dsec(
                                "gpsimd", lambda g, dst_h=dst_h, oc_l=oc_l,
                                tb=tb, par=par: g.dma_start(
                                    out=dst_h[oc_l][:, tb * 512:(tb + 1) * 512],
                                    in_=qstage[par][:, :]),
                                waits=[("v", ev6[step])])
                            step += 1
                else:
                    for tc in range(NT):
                        par = vstep % 2
                        tvc = k.sec("tensor", lambda t, tc=tc, par=par: [
                            t.matmul(ps_v4[par * 2 + oh][:, :],
                                     xt2[:, tc * D + ic * 128:
                                         tc * D + (ic + 1) * 128],
                                     wsec[:, ic * 1024 + oh * 512:
                                          ic * 1024 + (oh + 1) * 512],
                                     start=(ic == 0), stop=(ic == IC - 1))
                            for oh in range(2) for ic in range(IC)],
                            waits=[("v", fold), ("v", tv6.get(vstep - 2))])
                        tv6[vstep] = k.sec("vector", lambda v, par=par: [
                            v.tensor_copy(vstage[par][:, oh * 512:(oh + 1) * 512],
                                          ps_v4[par * 2 + oh][:, :])
                            for oh in range(2)],
                            waits=[("t", tvc), vtgd.get(vstep - 2),
                                   gv6.get(vstep - 2)])
                        gv6[vstep] = dsec("gpsimd", lambda g, tc=tc, osec=osec,
                                          par=par: g.dma_start(
                            out=V_h[tc][:, (osec - 4) * 1024:(osec - 3) * 1024],
                            in_=vstage[par][:, :]), waits=[("v", tv6[vstep])])
                        trv = k.sec("tensor", lambda t, par=par: [
                            t.transpose(ptv[par][:, j * 128:(j + 1) * 128],
                                        vstage[par][:, j * 128:(j + 1) * 128],
                                        ident[:, :])
                            for j in range(8)],
                            waits=[("v", tv6[vstep]), ("v", vtev.get(vstep - 2))])
                        vtev[vstep] = k.sec(
                            "vector", lambda v, par=par: v.tensor_copy(
                                vtst[par][:, :], ptv[par][:, :]),
                            waits=[("t", trv), vtgd.get(vstep - 1)])
                        vtgd[vstep] = dsec(
                            "gpsimd", lambda g, tc=tc, osec=osec, par=par:
                            g.dma_start(
                                out=VTF_h[(osec - 4) * 8:(osec - 4) * 8 + 8, :,
                                          tc * 128:(tc + 1) * 128]
                                .transpose([1, 0, 2]),
                                in_=vtst[par][:, :].rearrange(
                                    "p (j t) -> p j t", j=8)),
                            waits=[("v", vtev[vstep])])
                        vstep += 1
            wcb = sb("wcb", [128, IC * 96], BF16, st=st6)
            bcb = sb("bcb", [1, 96], BF16, st=st6)
            k.sec("gpsimd", lambda g: (
                g.dma_start(out=wcb[:, :].rearrange("p (ic o) -> p ic o", ic=IC),
                            in_=wcomb.rearrange("(ic p) o -> p ic o", p=128)),
                g.dma_start(out=bcb[:, :], in_=bcomb_row)), waits=("t",))
            fw6 = k.sec("vector", lambda v: [
                v.tensor_scalar_mul(wcb[:, ic * 96:(ic + 1) * 96],
                                    wcb[:, ic * 96:(ic + 1) * 96],
                                    rms_sb[:, ic:ic + 1]) for ic in range(IC)],
                waits=("gd", "v", "s"))
            scw = {}
            for tc in range(NT):
                pc = ps_p[tc % 2][:, 0:96]
                tcw = k.sec("tensor", lambda t, tc=tc, pc=pc: (
                    [t.matmul(pc,
                              xt2[:, tc * D + ic * 128: tc * D + (ic + 1) * 128],
                              wcb[:, ic * 96:(ic + 1) * 96],
                              start=(ic == 0), stop=False) for ic in range(IC)],
                    t.matmul(pc, ones_bf[0:1, 0:128], bcb[0:1, :],
                             start=False, stop=True)),
                    waits=[("v", fw6), ("s", scw.get(tc - 2))])
                scw[tc] = k.sec("scalar", lambda s, tc=tc, pc=pc: s.activation(
                    cw_sb[:, tc * 96:(tc + 1) * 96], pc, ACT.Sigmoid),
                    waits=[("t", tcw)])
        s_xt2.close()
        tp = tapout("cw", [128, NT * 96])
        if tp is not None:
            k.sec("gpsimd", lambda g: g.dma_start(out=tp, in_=cw_sb[:, :]),
                  waits=("s",))

        fence()
        with ExitStack() as st7:
            bstg = sb("bstg", [128, D], BF16, st=st7)
            bstg2 = sb("bstg2", [128, D], BF16, st=st7)
            k.sec("gpsimd", lambda g: (
                g.dma_start(out=bstg[:, :].rearrange("p (a t) -> p a t", a=IC),
                            in_=KT_h[:, :, T - 128:T].transpose([1, 0, 2])),
                g.dma_start(out=bstg2[:, :], in_=V_h[NT - 1])),
                waits=("gd", "s"))
            k.sec("gpsimd", lambda g: (
                g.dma_start(out=cin_bd[:, 0:2048], in_=bstg2[:, :]),
                g.dma_start(out=cin_bd[:, 2048:4096], in_=bstg[:, :])),
                waits=("gd",))
            k.sec("gpsimd", lambda g: g.collective_compute(
                "AllGather", OP.bypass, replica_groups=PAIRS,
                ins=[cin_bd.opt()], outs=[cout_bd.opt()]), waits=("gd",))

        fence()
        with ExitStack() as st8:
            qsb_ = [sb("qsb%d" % i, [128, IC * 512], BF16, st=st8) for i in range(2)]
            ksb_ = [sb("ksb%d" % i, [128, IC * 512], BF16, st=st8) for i in range(2)]
            kpb_ = [sb("kpb%d" % i, [128, IC * 128], BF16, st=st8) for i in range(2)]
            vwin_ = [sb("vwin%d" % i, [128, 5 * D], BF16, st=st8) for i in range(2)]
            ec_sb = [sb("ec_sb%d" % i, [128, 512], BF16, st=st8) for i in range(2)]
            ep_sb = [sb("ep_sb%d" % i, [128, 512], BF16, st=st8) for i in range(2)]

            loc4 = [sb("loc4%d" % i, [128, 4 * D], BF16, st=st8) for i in range(2)]
            rc4 = [sb("rc4%d" % i, [128, 4], st=st8) for i in range(2)]
            mlc = sb("mlc", [128, 512], BF16, st=st8)
            mlp = sb("mlp", [128, 512], BF16, st=st8)
            mlp0 = sb("mlp0", [128, 512], BF16, st=st8)
            ps_sc = [psum("ps_sc%d" % i, [128, 512], st=st8) for i in range(2)]
            ps_sp = [psum("ps_sp%d" % i, [128, 512], st=st8) for i in range(2)]
            ps_od = [psum("ps_od%d" % i, [128, 260], st=st8) for i in range(2)]

            mldma = dsec("gpsimd", lambda g: (
                g.dma_start(out=mlc[:, :], in_=mloc_cur_in),
                g.dma_start(out=mlp[:, :], in_=mloc_prev_in),
                g.dma_start(out=mlp0[:, :], in_=mloc_prev0_in)))

            ts8 = {}
            vm1 = {}
            se1 = {}
            vm2 = {}
            se2 = {}
            to8 = {}
            vfin = {}
            gld = {}
            locdma = {}
            def emit_gld(sbk):
                sp = sbk % 2
                gld[sbk] = dsec("gpsimd", lambda g, sbk=sbk, sp=sp: (
                    g.dma_start(out=qsb_[sp][:, :].rearrange("p (a t) -> p a t", a=IC),
                                in_=Q_h[:, :, sbk * 512:(sbk + 1) * 512]
                                .transpose([1, 0, 2])),
                    g.dma_start(out=ksb_[sp][:, :].rearrange("p (a t) -> p a t", a=IC),
                                in_=KT_h[:, :, sbk * 512:(sbk + 1) * 512]
                                .transpose([1, 0, 2])),
                    (g.dma_start(out=kpb_[sp][:, :].rearrange("p (a t) -> p a t", a=IC),
                                 in_=KT_h[:, :, sbk * 512 - 128: sbk * 512]
                                 .transpose([1, 0, 2]))
                     if sbk > 0 else
                     g.dma_start(out=kpb_[sp][:, :], in_=cout_bd[0:128, 2048:4096])),
                    [g.dma_start(out=vwin_[sp][:, (w + 1) * D:(w + 2) * D],
                                 in_=V_h[sbk * 4 + w]) for w in range(4)],
                    (g.dma_start(out=vwin_[sp][:, 0:D], in_=V_h[sbk * 4 - 1])
                     if sbk > 0 else
                     g.dma_start(out=vwin_[sp][:, 0:D], in_=cout_bd[0:128, 0:2048]))),
                    waits=[("t", ts8.get((sbk - 2, H2 - 1))), ("c", 2),
                           ("v", vfin.get((sbk - 2, H2 - 1)))])
            emit_gld(0)
            for sbk in range(4):
                if sbk + 1 < 4:
                    emit_gld(sbk + 1)
                qsb = qsb_[sbk % 2]
                ksb = ksb_[sbk % 2]
                kpb = kpb_[sbk % 2]
                vwin = vwin_[sbk % 2]
                for h in range(H2):
                    hp, oc = h % 2, h // 2
                    p = h % 2
                    ts8[(sbk, h)] = k.sec("tensor", lambda t, hp=hp, oc=oc, p=p, qsb=qsb, ksb=ksb, kpb=kpb: [
                        (t.matmul(ps_sc[p][:, b * 128:(b + 1) * 128],
                                  ksb[hp * 64:hp * 64 + 64,
                                      oc * 512 + b * 128: oc * 512 + (b + 1) * 128],
                                  qsb[hp * 64:hp * 64 + 64,
                                      oc * 512 + b * 128: oc * 512 + (b + 1) * 128],
                                  start=True, stop=True),
                         t.matmul(ps_sp[p][:, b * 128:(b + 1) * 128],
                                  (kpb[hp * 64:hp * 64 + 64, oc * 128:(oc + 1) * 128]
                                   if b == 0 else
                                   ksb[hp * 64:hp * 64 + 64,
                                       oc * 512 + (b - 1) * 128: oc * 512 + b * 128]),
                                  qsb[hp * 64:hp * 64 + 64,
                                      oc * 512 + b * 128: oc * 512 + (b + 1) * 128],
                                  start=True, stop=True)) for b in range(4)],
                        waits=[gld[sbk], ("s", se1.get((sbk, h - 2))),
                               ("s", se2.get((sbk, h - 2)))])
                    se1[(sbk, h)] = k.sec("scalar", lambda s, p=p: s.activation(
                        ec_sb[p][:, :], ps_sc[p][:, :], ACT.Exp, scale=SCALE2),
                        waits=[("t", ts8[(sbk, h)]), ("v", vm1.get((sbk, h - 2))),
                               ("t", to8.get((sbk, h - 2)))])
                    se2[(sbk, h)] = k.sec("scalar", lambda s, p=p: s.activation(
                        ep_sb[p][:, :], ps_sp[p][:, :], ACT.Exp, scale=SCALE2),
                        waits=[("v", vm1.get((sbk, h - 2)))])
                    vm1[(sbk, h)] = k.sec("vector", lambda v, sbk=sbk, p=p: (
                        v.tensor_tensor(ec_sb[p][:, :], ec_sb[p][:, :], mlc[:, :],
                                        op=OP.mult),
                        v.tensor_tensor(ep_sb[p][:, :], ep_sb[p][:, :],
                                        mlp0[:, :] if sbk == 0 else mlp[:, :],
                                        op=OP.mult)),
                        waits=[("s", se2[(sbk, h)]), mldma])
                    to8[(sbk, h)] = k.sec("tensor", lambda t, h=h, p=p, vwin=vwin: [
                        (t.matmul(ps_od[p][:, b * 64:(b + 1) * 64],
                                  ec_sb[p][:, b * 128:(b + 1) * 128],
                                  vwin[:, (b + 1) * D + h * 64:
                                       (b + 1) * D + (h + 1) * 64],
                                  start=True, stop=False),
                         t.matmul(ps_od[p][:, b * 64:(b + 1) * 64],
                                  ep_sb[p][:, b * 128:(b + 1) * 128],
                                  vwin[:, b * D + h * 64: b * D + (h + 1) * 64],
                                  start=False, stop=True),
                         t.matmul(ps_od[p][:, 256 + b: 256 + b + 1],
                                  ec_sb[p][:, b * 128:(b + 1) * 128],
                                  onescol_bf[:, :], start=True, stop=False),
                         t.matmul(ps_od[p][:, 256 + b: 256 + b + 1],
                                  ep_sb[p][:, b * 128:(b + 1) * 128],
                                  onescol_bf[:, :], start=False, stop=True))
                        for b in range(4)],
                        waits=[("v", vm1[(sbk, h)]), ("v", vfin.get((sbk, h - 2)))])
                    vfin[(sbk, h)] = k.sec("vector", lambda v, sbk=sbk, h=h, p=p: (
                        v.reciprocal(rc4[p][:, :], ps_od[p][:, 256:260]),
                        [v.tensor_scalar_mul(
                            loc4[sbk % 2][:, b * D + h * 64: b * D + (h + 1) * 64],
                            ps_od[p][:, b * 64:(b + 1) * 64], rc4[p][:, b:b + 1])
                         for b in range(4)]),
                        waits=[("t", to8[(sbk, h)]),
                               locdma.get(sbk - 2)])
                locdma[sbk] = dsec("gpsimd", lambda g, sbk=sbk: [
                    g.dma_start(out=LOC_h[sbk * 4 + b],
                                in_=loc4[sbk % 2][:, b * D:(b + 1) * D])
                    for b in range(4)], waits=[("v", vfin[(sbk, H2 - 1)])])

        tp = tapout("loc", [NT * 128, D], BF16)
        if tp is not None:
            tap_dram(tp, LOC_h.rearrange("a p d -> (a p) d"), NT * 128, D, BF16)

        fence()
        brow1 = sb("brow1", [1, D], BF16, st=s_wt)
        st11 = ExitStack()
        with ExitStack() as st11a:
            rwa = sb("rwa", [1, 4 * D], st=st11a)
            rwb = sb("rwb", [1, 4 * D], st=st11a)
            bacc = [sb("bacc%d" % i, [1, D], st=st11a) for i in range(2)]
            bscl11 = sb("bscl11", [1, 1], st=st11a)
            bx0 = sb("bx0", [1, 4], st=st11a)
            k.sec("sync", lambda y: (y.dma_start(out=rwa[:, :], in_=pb_row),
                                     y.dma_start(out=rwb[:, :], in_=nb_row),
                                     y.dma_start(out=bx0[:, :], in_=bexps_row),
                                     y.dma_start(out=bscl11[:, :], in_=bscale1)),
                  waits=("v", "s"))

            def bias_chain(v):
                v.tensor_tensor(rwa[:, :], rwa[:, :], rwb[:, :], op=OP.subtract)
                for n in range(4):
                    if n == 0:
                        v.tensor_scalar_mul(bacc[0][:, :], rwa[0:1, 0:D],
                                            bx0[0:1, 0:1])
                    else:
                        v.scalar_tensor_tensor(bacc[n % 2][:, :],
                                               rwa[0:1, n * D:(n + 1) * D],
                                               bx0[0:1, n:n + 1],
                                               bacc[(n + 1) % 2][:, :],
                                               op0=OP.mult, op1=OP.add)
                v.tensor_scalar(bacc[0][:, :], bacc[1][:, :], -1.0, 1.0,
                                op0=OP.max, op1=OP.min)
                v.tensor_scalar_mul(bacc[0][:, :], bacc[0][:, :], 15.0)
                MAGIC = 8388608.0
                v.tensor_scalar(bacc[1][:, :], bacc[0][:, :], 0.0, MAGIC,
                                op0=OP.max, op1=OP.add)
                v.tensor_scalar(bacc[1][:, :], bacc[1][:, :], 1.0, -MAGIC,
                                op0=OP.mult, op1=OP.add)
                v.tensor_scalar(rwa[0:1, 0:D], bacc[0][:, :], -1.0, 0.0,
                                op0=OP.mult, op1=OP.max)
                v.tensor_scalar(rwa[0:1, 0:D], rwa[0:1, 0:D], 1.0, MAGIC,
                                op0=OP.mult, op1=OP.add)
                v.tensor_scalar(rwa[0:1, 0:D], rwa[0:1, 0:D], 1.0, -MAGIC,
                                op0=OP.mult, op1=OP.add)
                v.tensor_tensor(bacc[1][:, :], bacc[1][:, :], rwa[0:1, 0:D],
                                op=OP.subtract)
                v.tensor_scalar_mul(bacc[1][:, :], bacc[1][:, :], 1.0 / 15.0)
                v.tensor_scalar_mul(bacc[1][:, :], bacc[1][:, :], bscl11[0:1, 0:1])
                v.tensor_copy(brow1[:, :], bacc[1][:, :])
            k.sec("vector", bias_chain, waits=("yd",))
        fence()
        pbuf = sb("pbuf", [128, 4096], st=st11)
        nbuf = sb("nbuf", [128, 4096], st=st11)
        sps = sb("sps", [128, 4096], st=st11)
        sns = sb("sns", [128, 4096], st=st11)
        acc = [sb("acc%d" % i, [128, D], st=st11) for i in range(2)]
        wtst = [sb("wtst%d" % i, [128, D], BF16, st=st11) for i in range(2)]
        enm = sb("enm", [1, 4], st=st11)
        enmb = sb("enmb", [128, 4], st=st11)
        enmbn = sb("enmbn", [128, 4], st=st11)
        srow1 = sb("srow1", [1, 4], st=st11)
        scl11 = sb("scl11", [1, 1], st=st11)
        bx = sb("bx", [1, 4], st=st11)
        ps_e = st11.enter_context(nc.psum_tensor("ps_e", [128, 4], F32))

        k.sec("sync", lambda y: (
            y.dma_start(out=srow1[:, :], in_=maskw_row),
            y.dma_start(out=bx[:, :], in_=exps_row),
            y.dma_start(out=scl11[:, :], in_=scale1)), waits=("v", "s"))
        k.sec("scalar", lambda s: s.activation(enm[:, :], srow1[:, :], ACT.Sigmoid),
              waits=("yd",))
        k.sec("vector", lambda v: (
            v.tensor_tensor(enm[:, :], enm[:, :], bx[:, :], op=OP.mult),
            v.tensor_scalar_mul(enm[:, :], enm[:, :], scl11[0:1, 0:1])),
            waits=("s",))
        k.sec("tensor", lambda t: t.matmul(ps_e[:, :], ones_f[0:1, 0:128],
                                           enm[0:1, :], start=True, stop=True),
              waits=("v",))
        k.sec("vector", lambda v: (
            v.tensor_copy(enmb[:, :], ps_e[:, :]),
            v.tensor_scalar_mul(enmbn[:, :], ps_e[:, :], -1.0)), waits=("t",))

        prep_state = {"sig": {}, "mac": {}, "wdma": {}, "ld": {}}

        def prep_step_fns():
            for ic in range(2):
                for half in range(2):
                    def step(ic=ic, half=half):
                        prep_state["ld"][(ic, half)] = dsec("gpsimd", lambda g: (
                            g.dma_start(out=pbuf[:, :],
                                        in_=pT[ic * 128:(ic + 1) * 128,
                                               half * 4096:(half + 1) * 4096]),
                            g.dma_start(out=nbuf[:, :],
                                        in_=nT[ic * 128:(ic + 1) * 128,
                                               half * 4096:(half + 1) * 4096])),
                            waits=[("s", prep_state["sig"].get(
                                (ic, half - 1),
                                prep_state["sig"].get((ic - 1, 1))))])
                        prep_state["sig"][(ic, half)] = k.sec(
                            "scalar", lambda s: (
                                s.activation(sps[:, :], pbuf[:, :], ACT.Sigmoid),
                                s.activation(sns[:, :], nbuf[:, :], ACT.Sigmoid)),
                            waits=[prep_state["ld"][(ic, half)],
                                   ("v", prep_state["mac"].get(
                                       (ic, half - 1),
                                       prep_state["mac"].get((ic - 1, 1))))])

                        def macs(v, half=half):
                            o = half * 4
                            for nn2 in range(2):
                                n_g = half * 2 + nn2
                                if o == 0:
                                    v.tensor_scalar_mul(acc[0][:, :],
                                                        sps[:, nn2 * D:(nn2 + 1) * D],
                                                        enmb[:, n_g:n_g + 1])
                                else:
                                    v.scalar_tensor_tensor(
                                        acc[o % 2][:, :],
                                        sps[:, nn2 * D:(nn2 + 1) * D],
                                        enmb[:, n_g:n_g + 1], acc[(o + 1) % 2][:, :],
                                        op0=OP.mult, op1=OP.add)
                                o += 1
                                v.scalar_tensor_tensor(
                                    acc[o % 2][:, :],
                                    sns[:, nn2 * D:(nn2 + 1) * D],
                                    enmbn[:, n_g:n_g + 1], acc[(o + 1) % 2][:, :],
                                    op0=OP.mult, op1=OP.add)
                                o += 1
                            if half == 1:
                                v.tensor_copy(wtst[ic % 2][:, :], acc[1][:, :])
                        prep_state["mac"][(ic, half)] = k.sec(
                            "vector", macs,
                            waits=[("s", prep_state["sig"][(ic, half)])] +
                                  ([prep_state["wdma"].get(ic - 2)]
                                   if half == 1 else []))
                    yield step

                def store(ic=ic):
                    prep_state["wdma"][ic] = dsec("gpsimd", lambda g: g.dma_start(
                        out=cin_wt[ic * 128:(ic + 1) * 128, :],
                        in_=wtst[ic % 2][:, :]),
                        waits=[("v", prep_state["mac"][(ic, 1)])])
                yield store

        prep_iter = prep_step_fns()

        fence()
        with ExitStack() as st9:
            src128 = [sb("src128%d" % i, [128, D], BF16, st=st9) for i in range(2)]
            eo = [sb("eo9%d" % i, [128, 1024], BF16, st=st9) for i in range(2)]
            pos_sb = [sb("pos_sb%d" % i, [128, 32], st=st9) for i in range(2)]
            wc_sb9 = [sb("wc_sb9%d" % i, [128, 32 * 64], BF16, st=st9)
                      for i in range(2)]
            ckown = sb("ckown", [32, D], BF16, st=st9)
            cvown = sb("cvown", [32, D], BF16, st=st9)
            cbk = sb("cbk", [1, H2 * 64], BF16, st=st9)
            cbv = sb("cbv", [1, H2 * 64], BF16, st=st9)
            ps_ck = [psum("ps_ck9%d" % i, [32, 64], st=st9) for i in range(2)]

            cbdma = dsec("gpsimd", lambda g: (
                g.dma_start(out=cbk[:, :], in_=kcb_row),
                g.dma_start(out=cbv[:, :], in_=vcb_row)))
            items = [(h, kind) for h in range(H2) for kind in range(2)]

            def ld9(i):
                h, kind = items[i]
                src_h = KT_h if kind == 0 else VTF_h
                pos_in = kpos_eo if kind == 0 else vpos_eo
                wc_in = kc_h if kind == 0 else vc_h
                p = i % 2
                return dsec("gpsimd", lambda g: (
                    g.dma_start(out=src128[p][0:64, :],
                                in_=src_h[h // 2][(h % 2) * 64:(h % 2) * 64 + 64, :]),
                    g.dma_start(out=src128[p][64:128, :],
                                in_=src_h[h // 2][(h % 2) * 64:(h % 2) * 64 + 64, :]),
                    g.dma_start(out=pos_sb[p][:, :], in_=pos_in[h]),
                    g.dma_start(out=wc_sb9[p][:, :].rearrange(
                        "p (a o) -> p a o", a=32), in_=wc_in[h].transpose([1, 0, 2]))),
                    waits=[("v", veo.get(i - 2)), ("t", tck.get(i - 2))])

            veo = {}
            tck = {}
            vck = {}
            gld9 = {0: None}
            gld9[0] = ld9(0)
            for i, (h, kind) in enumerate(items):
                nxt = next(prep_iter, None)
                if nxt is not None:
                    nxt()
                if i + 1 < len(items):
                    gld9[i + 1] = ld9(i + 1)
                p = i % 2
                own = ckown if kind == 0 else cvown
                brow_sb = cbk if kind == 0 else cbv
                veo[i] = k.sec("vector", lambda v, p=p: [
                    v.scalar_tensor_tensor(
                        eo[p][64 * dd:64 * (dd + 1), :],
                        src128[p][64 * dd:64 * (dd + 1), dd::2], 1.0,
                        pos_sb[p][64 * dd:64 * (dd + 1), :].rearrange(
                            "p (one m) -> p one m", one=1)
                        .broadcast_to([64, 32, 32]),
                        op0=OP.mult, op1=OP.add) for dd in range(2)],
                    waits=[gld9[i], ("t", tck.get(i - 2))])
                tck[i] = k.sec("tensor", lambda t, h=h, brow_sb=brow_sb, p=p: (
                    [t.matmul(ps_ck[p][:, :], eo[p][:, tau::32],
                              wc_sb9[p][:, tau * 64:(tau + 1) * 64],
                              start=(tau == 0), stop=False) for tau in range(32)],
                    t.matmul(ps_ck[p][:, :], ones_bf[0:1, 0:32],
                             brow_sb[0:1, h * 64:(h + 1) * 64],
                             start=False, stop=True)),
                    waits=[("v", veo[i]), ("v", vck.get(i - 2)), cbdma])
                vck[i] = k.sec("vector", lambda v, h=h, own=own, p=p: v.tensor_copy(
                    own[:, h * 64:(h + 1) * 64], ps_ck[p][:, :]),
                    waits=[("t", tck[i])])
            k.sec("gpsimd", lambda g: (
                g.dma_start(out=cin_ck[:, 0:2048], in_=ckown[:, :]),
                g.dma_start(out=cin_ck[:, 2048:4096], in_=cvown[:, :])),
                waits=("v",))
            k.sec("gpsimd", lambda g: g.collective_compute(
                "AllGather", OP.bypass, replica_groups=PAIRS,
                ins=[cin_ck.opt()], outs=[cout_ck.opt()]), waits=("gd",))
        for nxt in prep_iter:
            nxt()
        k.sec("gpsimd", lambda g: g.collective_compute(
            "AllGather", OP.bypass, replica_groups=[list(range(8))],
            ins=[cin_wt.opt()], outs=[cout_wt.opt()]),
            waits=[prep_state["wdma"][0], prep_state["wdma"][1]])
        st11.close()
        tp = tapout("ckg", [64, 4096], BF16)
        if tp is not None:
            tap_dram(tp, cout_ck, 64, 4096, BF16)

        k.t_drain_mode = "full"
        fence()
        with ExitStack() as st10:
            wo1_sb = sb("wo1_sb", [128, IC * D], BF16, st=st10)
            ckg = sb("ckg", [128, D], BF16, st=st10)
            cva = sb("cva", [68, D], BF16, st=st10)
            ckT = sb("ckT", [128, 16 * 128], BF16, st=st10)
            mk_sb = sb("mk_sb", [128, 64], BF16, st=st10)
            qtile = [sb("qtile%d" % i, [128, IC * 128], BF16, st=st10)
                     for i in range(2)]
            ltile = [sb("ltile%d" % i, [128, D], BF16, st=st10) for i in range(2)]
            e_c = [sb("e_c10%d" % i, [68, 512], BF16, st=st10) for i in range(2)]

            mcp = sb("mcp", [68, NT * 512], BF16, st=st10)
            s1c = sb("s1c", [128, 32], st=st10)
            s2c = sb("s2c", [128, 32], st=st10)
            rc32 = sb("rc32", [128, 32], st=st10)
            tmps = sb("tmps", [128, 512], st=st10)
            combf = sb("combf", [128, D], BF16, st=st10)
            comb_bf = sb("comb_bf", [128, D], BF16, st=st10)
            combT = sb("combT", [128, D], BF16, st=st10)
            ao10 = sb("ao10", [128, D], st=st10)
            rn110 = [sb("rn110%d" % i, [128, D], st=st10) for i in range(2)]
            rn010 = [sb("rn010%d" % i, [128, D], st=st10) for i in range(2)]
            yb10 = sb("yb10", [128, D], st=st10)
            xh10 = [sb("xh10%d" % i, [128, D], BF16, st=st10) for i in range(2)]
            xtst10 = [sb("xtst10%d" % i, [128, D], BF16, st=st10) for i in range(2)]
            ps_cs = [psum("ps_cs%d" % i, [128, 512], st=st10) for i in range(2)]
            ps_co = [psum("ps_co%d" % i, [128, 512], st=st10) for i in range(4)]
            ps_dn = psum("ps_dn", [128, 256], st=st10)
            pt10 = psum("ptx10", [128, 1024], BF16, st=st10)

            gw10 = dsec("gpsimd", lambda g: (
                g.dma_start(out=wo1_sb[:, :].rearrange("p (ec o) -> p ec o", ec=IC),
                            in_=wout1.rearrange("(ec p) o -> p ec o", p=128)),
                g.dma_start(out=ckg[0:64, :], in_=cout_ck[:, 0:2048]),
                g.dma_start(out=ckg[64:128, :], in_=cout_ck[:, 0:2048]),
                g.dma_start(out=cva[4:68, :], in_=cout_ck[:, 2048:4096]),
                g.dma_start(out=cva[0:4, :], in_=memv_row),
                g.dma_start(out=mk_sb[:, :], in_=memkT),
                g.dma_start(out=mcp[:, :].rearrange("p (a q) -> p a q", a=NT),
                            in_=mcomp_in.transpose([1, 0, 2]))), waits=("c", "v"))
            tck10 = {}
            vck10 = {}
            zck = k.sec("vector", lambda v: v.memset(ckT[:, :], 0.0), waits=("t", "s"))
            for hh in range(16):
                tck10[hh] = k.sec("tensor", lambda t, hh=hh: [
                    t.transpose(pt10[par * 64:(par + 1) * 64, 0:64],
                                ckg[par * 64:par * 64 + 64,
                                    (2 * hh + par) * 64:(2 * hh + par + 1) * 64],
                                ident[par * 64:par * 64 + 64,
                                      par * 64:par * 64 + 64])
                    for par in range(2)],
                    waits=[gw10, ("v", vck10.get(hh - 1))])
                vck10[hh] = k.sec("vector", lambda v, hh=hh: (
                    v.tensor_copy(ckT[0:64, hh * 128 + 4: hh * 128 + 68],
                                  pt10[0:64, 0:64]),
                    v.tensor_copy(ckT[64:128, hh * 128 + 4: hh * 128 + 68],
                                  pt10[64:128, 0:64]),
                    v.tensor_copy(ckT[0:64, hh * 128: hh * 128 + 4],
                                  mk_sb[0:64, hh * 4: hh * 4 + 4]),
                    v.tensor_copy(ckT[64:128, hh * 128: hh * 128 + 4],
                                  mk_sb[64:128, hh * 4: hh * 4 + 4])),
                    waits=[("t", tck10[hh])])

            yq = {}
            tcs = {}
            vmask = {}
            sexp10 = {}
            tout = {}
            vrc = {}
            vcomb = {}
            trc10 = {}
            trev10 = {}
            wout_t = {}
            ev10 = {}
            ap210 = {}
            addv10 = {}
            rd10 = {}
            gdres10 = {}
            xtev10 = {}
            xtgd10 = {}

            yq[0] = dsec("sync", lambda y: (
                y.dma_start(out=qtile[0][:, :].rearrange("p (a t) -> p a t", a=IC),
                            in_=Q_h[:, :, 0:128].transpose([1, 0, 2])),
                y.dma_start(out=ltile[0][:, :], in_=LOC_h[0])),
                waits=[("v", vck10[15])])

            def emit_xt10(tcx):
                if tcx < 0:
                    return
                for half in range(2):
                    tx = k.sec("tensor", lambda t, tcx=tcx, half=half: [
                        t.transpose(pt10[:, j * 128:(j + 1) * 128],
                                    xh10[tcx % 2][:, half * 1024 + j * 128:
                                                  half * 1024 + (j + 1) * 128],
                                    ident[:, :]) for j in range(8)],
                        waits=[("s", ap210[tcx])] +
                              ([("v", xtev10[(tcx, 0)])] if half else
                               [("v", xtev10.get((tcx - 1, 1)))]))
                    xtev10[(tcx, half)] = k.sec(
                        "vector", lambda v, tcx=tcx, half=half: v.tensor_copy(
                            xtst10[tcx % 2][:, half * 1024:(half + 1) * 1024],
                            pt10[:, :]),
                        waits=[("t", tx), xtgd10.get(tcx - 2)])
                xtgd10[tcx] = dsec("gpsimd", lambda g, tcx=tcx: g.dma_start(
                    out=XT3_h[tcx], in_=xtst10[tcx % 2][:, :]),
                    waits=[("v", xtev10[(tcx, 1)])])

            for tc in range(NT):
                if tc + 1 < NT:
                    yq[tc + 1] = dsec("sync", lambda y, tc=tc: (
                        y.dma_start(out=qtile[(tc + 1) % 2][:, :].rearrange(
                            "p (a t) -> p a t", a=IC),
                            in_=Q_h[:, :, (tc + 1) * 128:(tc + 2) * 128]
                            .transpose([1, 0, 2])),
                        y.dma_start(out=ltile[(tc + 1) % 2][:, :],
                                    in_=LOC_h[tc + 1])),
                        waits=[("t", tcs.get((tc - 1, 7))),
                               ("v", vrc.get(tc - 1))])
                for hb4 in range(8):
                    p = hb4 % 2
                    tcs[(tc, hb4)] = k.sec("tensor", lambda t, tc=tc, hb4=hb4, p=p: [
                        t.matmul(ps_cs[p][:, i4 * 128:(i4 + 1) * 128],
                                 ckT[((hb4 * 4 + i4) % 2) * 64:
                                     ((hb4 * 4 + i4) % 2) * 64 + 64,
                                     ((hb4 * 4 + i4) // 2) * 128:
                                     ((hb4 * 4 + i4) // 2) * 128 + 128],
                                 qtile[tc % 2][((hb4 * 4 + i4) % 2) * 64:
                                               ((hb4 * 4 + i4) % 2) * 64 + 64,
                                               ((hb4 * 4 + i4) // 2) * 128:
                                               ((hb4 * 4 + i4) // 2) * 128 + 128],
                                 start=True, stop=True) for i4 in range(4)],
                        waits=[yq[tc], ("s", sexp10.get((tc, hb4 - 2)))])
                    sexp10[(tc, hb4)] = k.sec("scalar", lambda s, p=p: s.activation(
                        e_c[p][:, :], ps_cs[p][0:68, :], ACT.Exp, scale=SCALE2),
                        waits=[("t", tcs[(tc, hb4)]),
                               ("v", vmask.get((tc, hb4 - 2))),
                               ("t", tout.get((tc, hb4 - 2)))])
                    vmask[(tc, hb4)] = k.sec("vector", lambda v, tc=tc, p=p:
                                             v.tensor_tensor(
                        e_c[p][:, :], e_c[p][:, :],
                        mcp[:, tc * 512:(tc + 1) * 512], op=OP.mult),
                        waits=[("s", sexp10[(tc, hb4)])])
                    tout[(tc, hb4)] = k.sec("tensor", lambda t, hb4=hb4, p=p: [
                        (t.matmul(ps_co[(hb4 * 4 + i4) // 8][
                            :, ((hb4 * 4 + i4) % 8) * 64:
                            ((hb4 * 4 + i4) % 8 + 1) * 64],
                            e_c[p][:, i4 * 128:(i4 + 1) * 128],
                            cva[:, (hb4 * 4 + i4) * 64:(hb4 * 4 + i4 + 1) * 64],
                            start=True, stop=True),
                         t.matmul(ps_dn[:, (hb4 * 4 + i4) * 8:
                                        (hb4 * 4 + i4) * 8 + 8],
                                  e_c[p][:, i4 * 128:(i4 + 1) * 128],
                                  onescol8[0:68, :], start=True, stop=True))
                        for i4 in range(4)],
                        waits=[("v", vmask[(tc, hb4)]),
                               ("v", ev10.get((tc - 1, (hb4 * 4) // 8))),
                               ("v", vrc.get(tc - 1))])
                vrc[tc] = k.sec("vector", lambda v, tc=tc: (
                    v.reciprocal(rc32[:, :], ps_dn[:, 0:256:8]),
                    v.tensor_tensor(s1c[:, :],
                                    cw_sb[:, tc * 96 + 0: (tc + 1) * 96: 3],
                                    cw_sb[:, tc * 96 + 1: (tc + 1) * 96: 3],
                                    op=OP.add),
                    v.tensor_tensor(s2c[:, :],
                                    cw_sb[:, tc * 96 + 2: (tc + 1) * 96: 3],
                                    rc32[:, :], op=OP.mult),
                    v.tensor_tensor(combf[:, :], ltile[tc % 2][:, :],
                                    s1c[:, :].rearrange("p (m one) -> p m one",
                                                        one=1)
                                    .broadcast_to([128, 32, 64]), op=OP.mult)),
                    waits=[("t", tout[(tc, 7)])])
                for oc in range(4):
                    vcomb[(tc, oc)] = k.sec("vector", lambda v, oc=oc: (
                        v.tensor_tensor(tmps[:, :], ps_co[oc][:, :],
                                        s2c[:, oc * 8:(oc + 1) * 8].rearrange(
                                            "p (m one) -> p m one", one=1)
                                        .broadcast_to([128, 8, 64]), op=OP.mult),
                        v.tensor_tensor(comb_bf[:, oc * 512:(oc + 1) * 512],
                                        combf[:, oc * 512:(oc + 1) * 512],
                                        tmps[:, :], op=OP.add)))
                for half in range(2):
                    trc10[(tc, half)] = k.sec("tensor", lambda t, half=half: [
                        t.transpose(pt10[:, j * 128:(j + 1) * 128],
                                    comb_bf[:, half * 1024 + j * 128:
                                            half * 1024 + (j + 1) * 128],
                                    ident[:, :]) for j in range(8)],
                        waits=[("v", vcomb[(tc, 3)] if half == 0
                                else trev10[(tc, 0)])])
                    trev10[(tc, half)] = k.sec("vector", lambda v, half=half:
                                               v.tensor_copy(
                        combT[:, half * 1024:(half + 1) * 1024], pt10[:, :]),
                        waits=[("t", trc10[(tc, half)])])
                for oc in range(4):
                    wout_t[(tc, oc)] = k.sec("tensor", lambda t, oc=oc: [
                        t.matmul(ps_co[oc][:, :], combT[:, ec * 128:(ec + 1) * 128],
                                 wo1_sb[:, ec * D + oc * 512: ec * D + (oc + 1) * 512],
                                 start=(ec == 0), stop=(ec == IC - 1))
                        for ec in range(IC)],
                        waits=[("v", trev10[(tc, 1)])])
                rd10[tc] = dsec("sync", lambda y, tc=tc: y.dma_start(
                    out=rn110[tc % 2][:, :], in_=R_b[tc * 128:(tc + 1) * 128, :]),
                    waits=[("v", addv10.get(tc - 2))])
                for oc in range(4):
                    ev10[(tc, oc)] = k.sec("vector", lambda v, oc=oc: v.tensor_copy(
                        ao10[:, oc * 512:(oc + 1) * 512], ps_co[oc][:, :]),
                        waits=[("t", wout_t[(tc, oc)])])
                k.sec("vector", lambda v: emit_stats(v, ao10, EPSLN, True),
                      waits=("s",))
                sq110 = k.sec("scalar", lambda s: s.activation(
                    c_tmp[:, :], c_tmp[:, :], ACT.Sqrt), waits=("v",))
                iv110 = k.sec("vector", lambda v: emit_inv(v, True),
                              waits=[("s", sq110)])
                apl10 = k.sec("scalar", lambda s: s.activation(
                    yb10[:, :], ao10[:, :], ACT.Identity, bias=c_nmi[:, 0:1],
                    scale=c_inv[:, 0:1]), waits=[("v", iv110)])
                addv10[tc] = k.sec("vector", lambda v, tc=tc: v.tensor_tensor(
                    rn010[tc % 2][:, :], yb10[:, :], rn110[tc % 2][:, :], op=OP.add),
                    waits=[("s", apl10), rd10[tc],
                           gdres10.get(tc - 2)])
                gdres10[tc] = dsec("gpsimd", lambda g, tc=tc: g.dma_start(
                    out=R_c[tc * 128:(tc + 1) * 128, :], in_=rn010[tc % 2][:, :]),
                    waits=[("v", addv10[tc])])
                k.sec("vector", lambda v, tc=tc: emit_stats(
                    v, rn010[tc % 2], EPSLN, True), waits=("s",))
                sq210 = k.sec("scalar", lambda s: s.activation(
                    c_tmp[:, :], c_tmp[:, :], ACT.Sqrt), waits=("v",))
                iv210 = k.sec("vector", lambda v: emit_inv(v, True),
                              waits=[("s", sq210)])
                ap210[tc] = k.sec("scalar", lambda s, tc=tc: s.activation(
                    xh10[tc % 2][:, :], rn010[tc % 2][:, :], ACT.Identity,
                    bias=c_nmi[:, 0:1], scale=c_inv[:, 0:1]),
                    waits=[("v", iv210)])
                emit_xt10(tc - 1)
            emit_xt10(NT - 1)
        tp = tapout("r_c", [T, D])
        if tp is not None:
            tap_dram(tp, R_c, T, D, F32)

        k.t_drain_mode = "none"
        fence()
        with ExitStack() as st12:
            wt12 = sb("wt12", [128, IC * D], BF16, st=st12)
            xt3t = [sb("xt3t%d" % i, [128, D], BF16, st=st12) for i in range(2)]
            gwt = dsec("gpsimd", lambda g: g.dma_start(
                out=wt12[:, :].rearrange("p (a o) -> p a o", a=IC),
                in_=cout_wt.rearrange("(a p) o -> p a o", p=128)),
                waits=("v", "t", "c"))
            xt3dma = {}
            xt3dma[0] = dsec("sync", lambda y: y.dma_start(
                out=xt3t[0][:, :], in_=XT3_h[0]), waits=("gd", "v"))

            def pre12(tc, tcnt):
                if tc + 1 < NT:
                    xt3dma[tc + 1] = dsec("sync", lambda y, tc=tc: y.dma_start(
                        out=xt3t[(tc + 1) % 2][:, :], in_=XT3_h[tc + 1]),
                        waits=[("t", tcnt.get((tc - 1, 3)))])

            def mm12(t, tc, oc, bank):
                for ic in range(IC):
                    t.matmul(bank[:, :],
                             xt3t[tc % 2][:, ic * 128:(ic + 1) * 128],
                             wt12[:, ic * D + oc * 512: ic * D + (oc + 1) * 512],
                             start=(ic == 0), stop=False)
                t.matmul(bank[:, :], ones_bf[0:1, 0:128],
                         brow1[0:1, oc * 512:(oc + 1) * 512],
                         start=False, stop=True)

            out_phase(
                st12, "p12", mm12,
                lambda tc: R_c[tc * 128:(tc + 1) * 128, :],
                lambda tc: out_res[tc * 128:(tc + 1) * 128, :],
                prefetch_emit=pre12,
                mm_extra_waits=lambda tc: [xt3dma[tc]] +
                    ([gwt] if tc == 0 else []))

        fins = [(nm2, k.n[nm2]) for nm2 in (k.GD + k.YD)]
        k.sec("gpsimd", lambda g: [g.wait_ge(k.sem[nm2], c * 16)
                                   for nm2, c in fins if c > 0], inc=False)
        s_wt.close()

    return nc, tap_aps


def _prep_shared(i):
    f32 = np.float32
    sh = {}
    sh["wqT"] = np.ascontiguousarray(np.asarray(i["L0_attn_Wq"], f32).T)
    sh["wkT"] = np.ascontiguousarray(np.asarray(i["L0_attn_Wk"], f32).T)
    sh["wvT"] = np.ascontiguousarray(np.asarray(i["L0_attn_Wv"], f32).T)
    sh["woT"] = np.ascontiguousarray(np.asarray(i["L0_attn_Wo"], f32).T)
    sh["bqc"] = np.ascontiguousarray(np.asarray(i["L0_attn_bq"], f32).reshape(IC, 128).T)
    sh["bk_row"] = np.asarray(i["L0_attn_bk"], f32).reshape(1, 512)
    sh["bv_row"] = np.asarray(i["L0_attn_bv"], f32).reshape(1, 512)
    sh["f0wT"] = np.ascontiguousarray(np.asarray(i["L0_ffn_weight"], f32).T)
    sh["f0scale"] = np.asarray(i["L0_ffn_scale_weight"], f32).reshape(1, D)
    sh["f0bias"] = np.asarray(i["L0_ffn_bias"], f32).reshape(1, D)
    sh["wqkv"] = np.ascontiguousarray(np.asarray(i["L1_attn_w_qkv"], f32))
    sh["wcomb"] = np.ascontiguousarray(np.asarray(i["L1_attn_w_comb"], f32))
    sh["bcomb_row"] = np.asarray(i["L1_attn_b_comb"], f32).reshape(1, 96)
    sh["wout1"] = np.ascontiguousarray(np.asarray(i["L1_attn_w_out"], f32))
    sh["rmswc"] = np.ascontiguousarray(np.asarray(i["L1_attn_rms_w"], f32).reshape(IC, 128).T)
    kc = np.asarray(i["L1_attn_kc_w"], f32).transpose(0, 3, 2, 1)
    sh["kc_h"] = np.ascontiguousarray(kc.reshape(H2, 32, 128, 64))
    vc = np.asarray(i["L1_attn_vc_w"], f32).transpose(0, 3, 2, 1)
    sh["vc_h"] = np.ascontiguousarray(vc.reshape(H2, 32, 128, 64))
    sh["kcb_row"] = np.ascontiguousarray(np.asarray(i["L1_attn_kc_b"], f32).reshape(1, H2 * 64))
    sh["vcb_row"] = np.ascontiguousarray(np.asarray(i["L1_attn_vc_b"], f32).reshape(1, H2 * 64))
    kp = np.asarray(i["L1_attn_k_pos"], f32).reshape(H2, 32, 2, 64)
    sh["kpos_eo"] = np.ascontiguousarray(kp.transpose(0, 2, 3, 1).reshape(H2, 128, 32))
    vp = np.asarray(i["L1_attn_v_pos"], f32).reshape(H2, 32, 2, 64)
    sh["vpos_eo"] = np.ascontiguousarray(vp.transpose(0, 2, 3, 1).reshape(H2, 128, 32))
    mk = np.asarray(i["L1_attn_mem_kv"], f32)[0]
    mkT = np.zeros((128, 64), f32)
    for h in range(H2):
        mkT[(h % 2) * 64:(h % 2) * 64 + 64,
            (h // 2) * 4:(h // 2) * 4 + 4] = mk[h].T
    sh["memkT"] = mkT
    sh["memv_row"] = np.ascontiguousarray(
        np.asarray(i["L1_attn_mem_kv"], f32)[1].transpose(1, 0, 2).reshape(4, H2 * 64))
    sh["_pT_full"] = np.asarray(i["L1_ffn_pweight"], f32).transpose(1, 2, 0).reshape(D, 4 * D)
    sh["_nT_full"] = np.asarray(i["L1_ffn_nweight"], f32).transpose(1, 2, 0).reshape(D, 4 * D)
    sh["pb_row"] = np.ascontiguousarray(np.asarray(i["L1_ffn_pbias"], f32).T.reshape(1, 4 * D))
    sh["nb_row"] = np.ascontiguousarray(np.asarray(i["L1_ffn_nbias"], f32).T.reshape(1, 4 * D))
    sh["exps_row"] = np.asarray(i["L1_ffn_exps"], f32).reshape(1, 4)
    sh["bexps_row"] = np.asarray(i["L1_ffn_bexps"], f32).reshape(1, 4)
    sh["maskw_row"] = np.asarray(i["L1_ffn_mask_weight"], f32).reshape(1, 4)
    sh["scale1"] = np.asarray(i["L1_ffn_scale"], f32).reshape(1, 1)
    sh["bscale1"] = np.asarray(i["L1_ffn_biasscale"], f32).reshape(1, 1)
    sh["ident_c"] = np.eye(128, dtype=f32)
    sh["ones_row"] = np.ones((1, 512), f32)
    sh["ones_col"] = np.ones((128, 1), f32)
    sh["ones_col8"] = np.ones((128, 8), f32)
    jj = np.arange(128)[:, None]
    qq = np.arange(128)[None, :]
    cur = np.where(jj <= qq, 1.0, 0.0).astype(f32)
    prv = np.where(qq <= jj, 1.0, 0.0).astype(f32)
    sh["mloc_cur"] = np.ascontiguousarray(np.tile(cur, (1, 4)))
    sh["mloc_prev"] = np.ascontiguousarray(np.tile(prv, (1, 4)))
    return sh


def _prep_core(i, sh, b, h):
    d = dict(sh)
    x = np.asarray(i["L0_attn_hidden_states"], np.float32)[b, h * T:(h + 1) * T, :]
    d["x_in"] = np.ascontiguousarray(x)
    c = b * 2 + h
    d["pT"] = np.ascontiguousarray(d.pop("_pT_full")[c * 256:(c + 1) * 256, :])
    d["nT"] = np.ascontiguousarray(d.pop("_nT_full")[c * 256:(c + 1) * 256, :])
    prv = sh["mloc_prev"][:, 0:128]
    first = np.zeros((128, 128), np.float32) if h == 0 else prv
    d["mloc_prev0"] = np.ascontiguousarray(
        np.concatenate([first, np.tile(prv, (1, 3))], axis=1))
    mc = np.zeros((NT, 68, 128), np.float32)
    for tc in range(NT):
        gq = h * T + tc * 128 + np.arange(128)
        gc = gq // CBS
        jcol = np.arange(68)
        allowed = (jcol[:, None] >= 4) & ((jcol[:, None] - 4) >= gc[None, :])
        mc[tc] = np.where(allowed, 1.0, 0.0)
    d["mcomp"] = np.ascontiguousarray(np.tile(mc, (1, 1, 4)))
    return d


_CACHED = {}


def kernel(**inputs):
    taps = tuple(TAPS)
    key = ("prog", taps)
    if key not in _CACHED:
        _CACHED[key] = build_program(taps)
    nc, tap_aps = _CACHED[key]
    sh = _prep_shared(inputs)
    in_maps = [_prep_core(inputs, sh, c // 2, c % 2) for c in range(8)]
    res = run_bass_kernel_spmd(nc, in_maps, core_ids=list(range(8)))
    out = np.zeros((4, 2 * T, D), np.float32)
    for c in range(8):
        out[c // 2, (c % 2) * T:(c % 2 + 1) * T, :] = res.results[c]["out_res"]
    if taps:
        kernel.tap_results = [{n: r["tap_" + n] for n in taps}
                              for r in res.results]
    return out


# version 23 (final)
# speedup vs baseline: 2.0254x; 1.0008x over previous
import os
import numpy as np
import ml_dtypes
import concourse.bass as bass
import concourse.mybir as mybir
from concourse.bass_utils import run_bass_kernel_spmd
from contextlib import ExitStack

F32 = mybir.dt.float32
BF16 = mybir.dt.bfloat16
OP = mybir.AluOpType
ACT = mybir.ActivationFunctionType
bfdt = ml_dtypes.bfloat16

D = 2048
T = 2048
NT = 16
IC = 16
H1, KVH1 = 16, 4
H2, DH, CBS = 32, 64, 64
SCALE2 = float(DH ** -0.5)
NEG = -1.0e30
EPS0 = 1e-6
EPSLN = 1e-5
EPSRMS = float(np.finfo(np.float32).eps)
PAIRS = [[0, 1], [2, 3], [4, 5], [6, 7]]

TAPS = [t for t in os.environ.get("SS_TAPS", "").split(",") if t]


def _is_ap(x):
    return hasattr(x, "tensor") and hasattr(x, "ap") and hasattr(x, "dtype")


class EW:

    def __init__(self, K, name, eng):
        self.K, self.name, self.e = K, name, eng
        self.last = None
        if name == "gpsimd":
            self.dma_sem = K.GD[K.dma_rr["gpsimd"] % len(K.GD)]
        elif name == "sync":
            self.dma_sem = K.YD[K.dma_rr["sync"] % len(K.YD)]
        else:
            self.dma_sem = None
        self.entry_count = K.n.get(self.dma_sem, 0)
        self.gated = False
        self.t_drained = False

    def wait_ge(self, sem, val):
        return self.e.wait_ge(sem, val)

    def nop(self):
        return self.e.nop()

    def drain(self):
        self.K.dirty_w[self.name].clear()
        self.K.dirty_r[self.name].clear()
        return self.e.drain()

    def __getattr__(self, op):
        fn = getattr(self.e, op)

        def wrap(*a, **kw):
            K, nm = self.K, self.name
            if op == "dma_start":
                key = self.dma_sem
                if not self.gated:
                    if self.entry_count > 0:
                        self.e.wait_ge(K.sem[key], self.entry_count * 16)
                    self.gated = True
                    K.dma_rr[nm] += 1
                r = fn(*a, **kw)
                r.then_inc(K.sem[key], 16)
                K.n[key] += 1
                return r
            if op == "collective_compute":
                r = fn(*a, **kw)
                r.then_inc(K.sem["c"], 1)
                K.n["c"] += 1
                return r
            outs, ins = [], []
            pos = list(a)
            if pos and _is_ap(pos[0]):
                outs.append(pos[0])
                pos = pos[1:]
            for x in pos:
                if _is_ap(x):
                    ins.append(x)
            for k2, v2 in kw.items():
                if _is_ap(v2):
                    (outs if k2 in ("out", "accum_out") else ins).append(v2)
            in_names = {x.tensor.name for x in ins}
            out_names = {x.tensor.name for x in outs}
            dw, dr = K.dirty_w[nm], K.dirty_r[nm]
            if (in_names & dw) or (out_names & (dw | dr)):
                dw.clear()
                dr.clear()
                mode = self.K.t_drain_mode if nm == "tensor" else "full"
                if (mode == "full" or (mode == "section" and not self.t_drained)):
                    self.e.drain()
                    self.t_drained = True
                elif nm == "tensor" and mode == "none":
                    pass
            dr |= in_names
            dw |= out_names
            r = fn(*a, **kw)
            self.last = r
            return r

        return wrap


class K:
    ENGS = ("tensor", "vector", "scalar", "gpsimd", "sync")

    def __init__(self, nc, block, stack):
        self.nc, self.block = nc, block
        self.sem = {}
        self.n = {}
        for s in ("t", "v", "s", "g", "c",
                  "gd0", "gd1", "gd2", "gd3", "yd0", "yd1"):
            self.sem[s] = stack.enter_context(nc.semaphore("sem_" + s))
            self.n[s] = 0
        self.GD = ("gd0", "gd1", "gd2", "gd3")
        self.YD = ("yd0", "yd1")
        self.dma_rr = {"gpsimd": 0, "sync": 0}
        self.t_drain_mode = "full"
        self.dirty_w = {e: set() for e in self.ENGS}
        self.dirty_r = {e: set() for e in self.ENGS}
        self.esem = {"tensor": "t", "vector": "v", "scalar": "s",
                     "gpsimd": "g", "sync": None}

    def sec(self, engname, body, waits=(), inc=True):
        cap = []
        for w in waits:
            if isinstance(w, str):
                if w == "gd":
                    cap += [(x, self.n[x]) for x in self.GD]
                elif w == "yd":
                    cap += [(x, self.n[x]) for x in self.YD]
                else:
                    cap.append((w, self.n[w]))
            elif w is None:
                pass
            else:
                nm2, c = w
                if c is not None and c > 0:
                    cap.append((nm2, min(c, self.n[nm2])))
        sem = self.sem
        esem = self.esem[engname]

        def f(e):
            w = EW(self, engname, e)
            for nm2, c in cap:
                tgt = c * (16 if nm2[0] in "gy" else 1)
                if tgt > 0:
                    e.wait_ge(sem[nm2], tgt)
            body(w)
            if inc and esem is not None:
                if w.last is not None:
                    w.last.then_inc(sem[esem], 1)
                else:
                    e.nop().then_inc(sem[esem], 1)
                self.n[esem] += 1

        getattr(self.block, engname)(f)
        return self.n[esem] if esem is not None else None


def build_program(taps=()):
    nc = bass.Bass("TRN2", target_bir_lowering=False, debug=False)

    def inp(name, shape, dt=F32):
        return nc.dram_tensor(name, shape, dt, kind="ExternalInput").ap()

    x_in = inp("x_in", [T, D])
    wqT = inp("wqT", [D, D])
    wkT = inp("wkT", [D, 512])
    wvT = inp("wvT", [D, 512])
    woT = inp("woT", [D, D])
    bqc = inp("bqc", [128, IC])
    bk_row = inp("bk_row", [1, 512])
    bv_row = inp("bv_row", [1, 512])
    f0wT = inp("f0wT", [D, D])
    f0scale = inp("f0scale", [1, D])
    f0bias = inp("f0bias", [1, D])
    wqkv = inp("wqkv", [D, 3 * D])
    wcomb = inp("wcomb", [D, 96])
    bcomb_row = inp("bcomb_row", [1, 96])
    wout1 = inp("wout1", [D, D])
    rmswc = inp("rmswc", [128, IC])
    kc_h = inp("kc_h", [H2, 32, 128, 64])
    vc_h = inp("vc_h", [H2, 32, 128, 64])
    kcb_row = inp("kcb_row", [1, H2 * 64])
    vcb_row = inp("vcb_row", [1, H2 * 64])
    kpos_eo = inp("kpos_eo", [H2, 128, 32])
    vpos_eo = inp("vpos_eo", [H2, 128, 32])
    memkT = inp("memkT", [128, 64])
    memv_row = inp("memv_row", [4, H2 * 64])
    pT = inp("pT", [256, 4 * D])
    nT = inp("nT", [256, 4 * D])
    pb_row = inp("pb_row", [1, 4 * D])
    nb_row = inp("nb_row", [1, 4 * D])
    exps_row = inp("exps_row", [1, 4])
    bexps_row = inp("bexps_row", [1, 4])
    maskw_row = inp("maskw_row", [1, 4])
    scale1 = inp("scale1", [1, 1])
    bscale1 = inp("bscale1", [1, 1])
    ident_in = inp("ident_c", [128, 128])
    ones_row = inp("ones_row", [1, 512])
    ones_col = inp("ones_col", [128, 1])
    ones_col8 = inp("ones_col8", [128, 8])
    mloc_cur_in = inp("mloc_cur", [128, 512])
    mloc_prev_in = inp("mloc_prev", [128, 512])
    mloc_prev0_in = inp("mloc_prev0", [128, 512])
    mcomp_in = inp("mcomp", [NT, 68, 512])

    out_res = nc.dram_tensor("out_res", [T, D], F32, kind="ExternalOutput").ap()

    def scr(name, shape, dt):
        return nc.dram_tensor(name, shape, dt).ap()

    R_a = scr("R_a", [T, D], F32)
    R_b = scr("R_b", [T, D], F32)
    R_c = scr("R_c", [T, D], F32)
    XT1_h = scr("XT1_h", [NT, 128, D], BF16)
    XT3_h = scr("XT3_h", [NT, 128, D], BF16)
    Q_h = scr("Q_h", [IC, 128, T], BF16)
    KT_h = scr("KT_h", [IC, 128, T], BF16)
    VTF_h = scr("VTF_h", [IC, 128, T], BF16)
    V_h = scr("V_h", [NT, 128, D], BF16)
    LOC_h = scr("LOC_h", [NT, 128, D], BF16)
    WT_h = scr("WT_h", [IC, 128, D], BF16)
    cin_wt = scr("cin_wt", [256, D], BF16)
    cout_wt = scr("cout_wt", [2048, D], BF16)
    cin_kv = scr("cin_kv", [128, 516], F32)
    cout_kv = scr("cout_kv", [128, 516], F32)
    cin_bd = scr("cin_bd", [128, 4096], BF16)
    cout_bd = scr("cout_bd", [256, 4096], BF16)
    cin_ck = scr("cin_ck", [32, 4096], BF16)
    cout_ck = scr("cout_ck", [64, 4096], BF16)

    tap_aps = {}

    def tapout(name, shape, dt=F32):
        if name in taps:
            tap_aps[name] = nc.dram_tensor(
                "tap_" + name, shape, dt, kind="ExternalOutput").ap()
            return tap_aps[name]
        return None

    with ExitStack() as ST:
        k = K(nc, ST.enter_context(nc.Block()), ST)

        def sb(name, shape, dt=F32, st=None, side=None):
            return (st or ST).enter_context(nc.sbuf_tensor(name, shape, dt, side=side))

        def psum(name, shape, dt=F32, st=None):
            return (st or ST).enter_context(nc.psum_tensor(name, shape, dt))

        ident = sb("ident", [128, 128], BF16)
        ones_bf = sb("ones_bf", [1, 512], BF16)
        onescol_bf = sb("onescol_bf", [128, 1], BF16)
        onescol8 = sb("onescol8", [128, 8], BF16)
        ones_f = sb("ones_f", [1, 512], F32)
        bn6 = sb("bn6", [128, 24])
        mv = sb("mv", [128, 2])
        c_inv = sb("c_inv", [128, 1])
        c_tmp = sb("c_tmp", [128, 1])
        c_nmi = sb("c_nmi", [128, 1])
        kv_bf = sb("kv_bf", [128, 512], BF16)
        ksum_bf = sb("ksum_bf", [128, 4], BF16)

        k.sec("gpsimd", lambda g: (
            g.dma_start(out=ident[:, :], in_=ident_in),
            g.dma_start(out=ones_bf[:, :], in_=ones_row),
            g.dma_start(out=onescol_bf[:, :], in_=ones_col),
            g.dma_start(out=onescol8[:, :], in_=ones_col8)))
        k.sec("sync", lambda y: y.dma_start(out=ones_f[:, :], in_=ones_row))

        def emit_stats(v, src, eps, mean):
            for i in range(4):
                v.bn_stats(bn6[:, 6 * i:6 * (i + 1)], src[:, 512 * i:512 * (i + 1)])
            v.bn_aggr(mv[:, :], bn6[:, :])
            if mean:
                v.tensor_scalar(c_tmp[:, :], mv[:, 1:2], 1.0, eps,
                                op0=OP.mult, op1=OP.add)
            else:
                v.tensor_tensor(c_tmp[:, :], mv[:, 0:1], mv[:, 0:1], op=OP.mult)
                v.tensor_tensor(c_tmp[:, :], c_tmp[:, :], mv[:, 1:2], op=OP.add)
                v.tensor_scalar(c_tmp[:, :], c_tmp[:, :], 1.0, eps,
                                op0=OP.mult, op1=OP.add)

        def emit_inv(v, mean):
            v.reciprocal(c_inv[:, :], c_tmp[:, :])
            if mean:
                v.tensor_scalar(c_nmi[:, :], mv[:, 0:1], c_inv[:, 0:1], -1.0,
                                op0=OP.mult, op1=OP.mult)
            else:
                v.memset(c_nmi[:, :], 0.0)

        def norm_cols(waits1, src_sb, eps, mean):
            k.sec("vector", lambda v: emit_stats(v, src_sb, eps, mean), waits=waits1)
            k.sec("scalar", lambda s: s.activation(c_tmp[:, :], c_tmp[:, :], ACT.Sqrt),
                  waits=("v",))
            k.sec("vector", lambda v: emit_inv(v, mean), waits=("s",))

        def transpose_tile(bufs, src_bf, dst_fn, twaits, vwaits=("t",)):
            k.sec("tensor", lambda t: [
                t.transpose(bufs[ic // 8][:, (ic % 8) * 128:(ic % 8 + 1) * 128],
                            src_bf[:, ic * 128:(ic + 1) * 128], ident[:, :])
                for ic in range(IC)], waits=twaits)
            k.sec("vector", lambda v: [dst_fn(v, hb, bufs[hb][:, :]) for hb in range(2)],
                  waits=vwaits)



        def fence():
            cnt = dict(k.n)
            for e in ("tensor", "vector", "scalar", "gpsimd", "sync"):
                own = k.esem[e]
                ws = tuple(x for x in ("t", "v", "s", "gd", "yd", "c")
                           if x != own)
                k.sec(e, lambda w: w.drain() if w.name in
                      ("tensor", "vector", "scalar") else w.nop(), waits=ws)

        def dsec(engname, body, waits=()):
            rr = k.dma_rr[engname]
            names = k.GD if engname == "gpsimd" else k.YD
            nm2 = names[rr % len(names)]
            k.sec(engname, body, waits=waits)
            return (nm2, k.n[nm2])

        def tap_dram(tpap, dram_ap, nrows, ncols, dt):
            fence()
            with ExitStack() as stt:
                stage = sb("tapstage_%d" % k.n["gd"], [128, ncols], dt, st=stt)
                nt2 = (nrows + 127) // 128
                for r in range(nt2):
                    h2 = min(128, nrows - r * 128)
                    k.sec("gpsimd", lambda g, r=r, h2=h2: g.dma_start(
                        out=stage[0:h2, :], in_=dram_ap[r * 128:r * 128 + h2, :]),
                        waits=("gd",))
                    k.sec("gpsimd", lambda g, r=r, h2=h2: g.dma_start(
                        out=tpap[r * 128:r * 128 + h2, :], in_=stage[0:h2, :]),
                        waits=("gd",))
        s_xt0 = ExitStack()
        s_att = ExitStack()
        s_xt2 = ExitStack()
        s_cw = ExitStack()
        s_wt = ExitStack()

        fence()
        xt0 = sb("xt0", [128, NT * D], BF16, st=s_xt0)
        with ExitStack() as st1:
            xin_b = [sb("xin%d" % i, [128, D], st=st1) for i in range(2)]
            xhat1b = sb("xhat_p1", [128, D], BF16, st=st1)
            ptb1 = [psum("pt1%d" % i, [128, 1024], BF16, st=st1) for i in range(2)]
            for tc in range(NT):
                bi = tc % 2
                k.sec("sync", lambda y, tc=tc, bi=bi: y.dma_start(
                    out=xin_b[bi][:, :], in_=x_in[tc * 128:(tc + 1) * 128, :]),
                    waits=("s",))
                norm_cols(("yd",), xin_b[bi], EPSLN, True)
                k.sec("scalar", lambda s, bi=bi: s.activation(
                    xhat1b[:, :], xin_b[bi][:, :], ACT.Identity,
                    bias=c_nmi[:, 0:1], scale=c_inv[:, 0:1]), waits=("v",))
                transpose_tile(ptb1, xhat1b,
                               lambda v, hb, ap, tc=tc: v.tensor_copy(
                                   xt0[:, tc * D + hb * 1024: tc * D + (hb + 1) * 1024],
                                   ap), ("s",))
        tp = tapout("xt0", [128, NT * D], BF16)
        if tp is not None:
            k.sec("gpsimd", lambda g: g.dma_start(out=tp, in_=xt0[:, :]), waits=("v",))

        fence()
        with ExitStack() as st2:
            wk_sb = sb("wk_sb", [128, IC * 512], BF16, st=st2)
            wv_sb = sb("wv_sb", [128, IC * 512], BF16, st=st2)
            bkr = sb("bkr", [1, 512], BF16, st=st2)
            bvr = sb("bvr", [1, 512], BF16, st=st2)
            sk_sb = sb("sk_sb", [128, NT * 512], BF16, st=st2)
            v_sb = sb("v_sb", [128, NT * 512], BF16, st=st2)
            r_bf = sb("r_bf2", [128, 512], BF16, st=st2)
            mel = sb("mel2", [128, 512], st=st2)
            ee = sb("ee2", [128, 512], st=st2)
            ps_k = psum("ps_k", [128, 512], st=st2)
            ps_v = psum("ps_v", [128, 512], st=st2)

            k.sec("gpsimd", lambda g: (
                g.dma_start(out=wk_sb[:, :].rearrange("p (ic o) -> p ic o", ic=IC),
                            in_=wkT.rearrange("(ic p) o -> p ic o", p=128)),
                g.dma_start(out=wv_sb[:, :].rearrange("p (ic o) -> p ic o", ic=IC),
                            in_=wvT.rearrange("(ic p) o -> p ic o", p=128)),
                g.dma_start(out=bkr[:, :], in_=bk_row),
                g.dma_start(out=bvr[:, :], in_=bv_row)))
            for tc in range(NT):
                k.sec("tensor", lambda t, tc=tc: (
                    [t.matmul(ps_k[:, :],
                              xt0[:, tc * D + ic * 128: tc * D + (ic + 1) * 128],
                              wk_sb[:, ic * 512:(ic + 1) * 512],
                              start=(ic == 0), stop=False) for ic in range(IC)],
                    t.matmul(ps_k[:, :], ones_bf[0:1, 0:128], bkr[0:1, :],
                             start=False, stop=True),
                    [t.matmul(ps_v[:, :],
                              xt0[:, tc * D + ic * 128: tc * D + (ic + 1) * 128],
                              wv_sb[:, ic * 512:(ic + 1) * 512],
                              start=(ic == 0), stop=False) for ic in range(IC)],
                    t.matmul(ps_v[:, :], ones_bf[0:1, 0:128], bvr[0:1, :],
                             start=False, stop=True)),
                    waits=("gd", "v", "s"))
                k.sec("vector", lambda v, tc=tc: (
                    v.tensor_scalar(r_bf[:, :], ps_k[:, :], 0.0, None, op0=OP.max),
                    v.scalar_tensor_tensor(mel[:, :], ps_k[:, :], 1.0, r_bf[:, :],
                                           op0=OP.mult, op1=OP.subtract),
                    v.tensor_copy(v_sb[:, tc * 512:(tc + 1) * 512], ps_v[:, :])),
                    waits=("t",))
                k.sec("scalar", lambda s: s.activation(ee[:, :], mel[:, :], ACT.Exp),
                      waits=("v",))
                k.sec("vector", lambda v, tc=tc: v.tensor_tensor(
                    sk_sb[:, tc * 512:(tc + 1) * 512], ee[:, :], r_bf[:, :], op=OP.add),
                    waits=("s",))

            ps_kv = psum("ps_kv", [128, 512], st=st2)
            ps_ks = psum("ps_ks", [128, 4], st=st2)
            kvred = sb("kvred", [128, 516], st=st2)

            def p2_kv(t):
                for g in range(KVH1):
                    for tc in range(NT):
                        t.matmul(ps_kv[:, g * 128:(g + 1) * 128],
                                 sk_sb[:, tc * 512 + g * 128: tc * 512 + (g + 1) * 128],
                                 v_sb[:, tc * 512 + g * 128: tc * 512 + (g + 1) * 128],
                                 start=(tc == 0), stop=(tc == NT - 1))
                for g in range(KVH1):
                    for tc in range(NT):
                        t.matmul(ps_ks[:, g:g + 1],
                                 sk_sb[:, tc * 512 + g * 128: tc * 512 + (g + 1) * 128],
                                 onescol_bf[:, :], start=(tc == 0), stop=(tc == NT - 1))
            k.sec("tensor", p2_kv, waits=("v",))
            k.sec("vector", lambda v: (
                v.tensor_copy(kvred[:, 0:512], ps_kv[:, :]),
                v.tensor_copy(kvred[:, 512:516], ps_ks[:, :])), waits=("t",))
            k.sec("gpsimd", lambda g: g.dma_start(out=cin_kv, in_=kvred[:, :]),
                  waits=("v",))
            k.sec("gpsimd", lambda g: g.collective_compute(
                "AllReduce", OP.add, replica_groups=PAIRS,
                ins=[cin_kv.opt()], outs=[cout_kv.opt()]), waits=("gd",))
            k.sec("gpsimd", lambda g: g.dma_start(out=kvred[:, :], in_=cout_kv),
                  waits=("c", "v"))
            k.sec("vector", lambda v: (
                v.tensor_copy(kv_bf[:, :], kvred[:, 0:512]),
                v.tensor_copy(ksum_bf[:, :], kvred[:, 512:516])), waits=("gd",))
            tp = tapout("kvred", [128, 516])
            if tp is not None:
                k.sec("gpsimd", lambda g: g.dma_start(out=tp, in_=kvred[:, :]),
                      waits=("v",))

        k.t_drain_mode = "none"
        fence()
        attnT = sb("attnT", [128, NT * D], BF16, st=s_att, side="right")
        with ExitStack() as st3:
            wq_sb = [sb("wq_sb%d" % i, [128, IC * 512], BF16, st=st3) for i in range(2)]
            bqc_sb = sb("bqc_sb", [128, IC], st=st3)
            sq_sb = [sb("sq_sb%d" % i, [128, D], BF16, st=st3) for i in range(2)]
            sqs_sb = [sb("sqs_sb%d" % i, [128, 512], BF16, st=st3) for i in range(2)]
            r_bf3 = [sb("r_bf3%d" % i, [128, 512], BF16, st=st3) for i in range(2)]
            mel3 = [sb("mel3%d" % i, [128, 512], st=st3) for i in range(2)]
            ee3 = [sb("ee3%d" % i, [128, 512], st=st3) for i in range(2)]
            nrow_tmp = sb("nrow_tmp", [1, 512], st=st3)
            nrow_bf = [sb("nrow_bf%d" % i, [1, D], BF16, st=st3) for i in range(2)]
            ps_q = [psum("ps_q%d" % i, [128, 512], st=st3) for i in range(2)]
            ps_n = [psum("ps_n%d" % i, [1, 512], st=st3) for i in range(2)]
            ps_b = [psum("ps_b%d" % i, [128, 512], st=st3) for i in range(2)]
            ps_a = [psum("ps_a%d" % i, [128, 512], st=st3) for i in range(2)]

            bqdma = dsec("sync", lambda y: y.dma_start(out=bqc_sb[:, :], in_=bqc))
            gq = {}
            tq = {}
            ve1 = {}
            sexp = {}
            ve2 = {}
            tn = {}
            vn = {}
            vr = {}
            tb_ = {}
            vsc = {}
            ta = {}
            vev = {}
            gq[0] = dsec("gpsimd", lambda g: g.dma_start(
                out=wq_sb[0][:, :].rearrange("p (ic o) -> p ic o", ic=IC),
                in_=wqT.rearrange("(ic p) o -> p ic o", p=128)[:, :, 0:512]),
                waits=("t", "v"))
            for h in range(H1):
                g2 = h // 4
                hh = h % 4
                hp = h % 2
                wbuf = wq_sb[g2 % 2]
                if hh == 0 and g2 + 1 < 4:
                    gq[g2 + 1] = dsec("gpsimd", lambda g, g2=g2: g.dma_start(
                        out=wq_sb[(g2 + 1) % 2][:, :].rearrange(
                            "p (ic o) -> p ic o", ic=IC),
                        in_=wqT.rearrange("(ic p) o -> p ic o", p=128)[
                            :, :, (g2 + 1) * 512:(g2 + 2) * 512]),
                        waits=[("t", tq.get((h - 1, 3)))])
                for tb in range(4):
                    st_i = 4 * h + tb
                    p = st_i % 2
                    tq[(h, tb)] = k.sec("tensor", lambda t, hh=hh, tb=tb, p=p,
                                        wbuf=wbuf: [
                        t.matmul(ps_q[p][:, :],
                                 wbuf[:, ic * 512 + hh * 128:
                                      ic * 512 + (hh + 1) * 128],
                                 xt0[:, :].rearrange("p (tc c) -> p tc c", tc=NT)[
                                     :, 4 * tb:4 * tb + 4, ic * 128:(ic + 1) * 128],
                                 start=(ic == 0), stop=(ic == IC - 1))
                        for ic in range(IC)],
                        waits=[gq[g2], ("v", ve1.get(st_i - 2))])
                    ve1[st_i] = k.sec("vector", lambda v, h=h, p=p: (
                        v.tensor_scalar(r_bf3[p][:, :], ps_q[p][:, :],
                                        bqc_sb[:, h:h + 1], 0.0,
                                        op0=OP.add, op1=OP.max),
                        v.scalar_tensor_tensor(mel3[p][:, :], ps_q[p][:, :],
                                               bqc_sb[:, h:h + 1], r_bf3[p][:, :],
                                               op0=OP.add, op1=OP.subtract)),
                        waits=[("t", tq[(h, tb)]), ("s", sexp.get(st_i - 2)),
                               bqdma])
                    sexp[st_i] = k.sec("scalar", lambda s, p=p: s.activation(
                        ee3[p][:, :], mel3[p][:, :], ACT.Exp),
                        waits=[("v", ve1[st_i])])
                    ve2[st_i] = k.sec("vector", lambda v, h=h, tb=tb, p=p:
                                      v.tensor_tensor(
                        sq_sb[h % 2][:, tb * 512:(tb + 1) * 512], ee3[p][:, :],
                        r_bf3[p][:, :], op=OP.add),
                        waits=[("s", sexp[st_i])])
                    tn[(h, tb)] = k.sec("tensor", lambda t, g2=g2, h=h, tb=tb, p=p:
                                        t.matmul(
                        ps_n[p][:, :], ksum_bf[:, g2:g2 + 1],
                        sq_sb[h % 2][:, tb * 512:(tb + 1) * 512],
                        start=True, stop=True),
                        waits=[("v", ve2[st_i]), ("v", vn.get(st_i - 2))])
                    def _vn(v, h=h, tb=tb, p=p):
                        v.tensor_scalar(nrow_tmp[0:1, :], ps_n[p][0:1, :],
                                        1.0, EPS0, op0=OP.mult, op1=OP.add)
                        with nc.allow_low_precision(
                                reason="bf16 recip feeds bf16 matmul operand"):
                            v.reciprocal(
                                nrow_bf[h % 2][0:1, tb * 512:(tb + 1) * 512],
                                nrow_tmp[0:1, :])
                    vn[st_i] = k.sec("vector", _vn,
                                     waits=[("t", tn[(h, tb)]),
                                            ("t", tb_.get((h - 1, tb)))])

                for tb in range(4):
                    st_i = 4 * h + tb
                    p = st_i % 2
                    tb_[(h, tb)] = k.sec("tensor", lambda t, h=h, tb=tb, p=p:
                                         t.matmul(
                        ps_b[p][:, :], ones_bf[0:1, 0:128],
                        nrow_bf[h % 2][0:1, tb * 512:(tb + 1) * 512],
                        start=True, stop=True),
                        waits=[("v", vn[st_i]), ("v", vsc.get(st_i - 2))])
                    vsc[st_i] = k.sec("vector", lambda v, h=h, tb=tb, p=p:
                                      v.tensor_tensor(
                        sqs_sb[p][:, :], sq_sb[h % 2][:, tb * 512:(tb + 1) * 512],
                        ps_b[p][:, :], op=OP.mult),
                        waits=[("t", tb_[(h, tb)])])
                    ta[(h, tb)] = k.sec("tensor", lambda t, g2=g2, p=p: t.matmul(
                        ps_a[p][:, :], kv_bf[:, g2 * 128:(g2 + 1) * 128],
                        sqs_sb[p][:, :], start=True, stop=True),
                        waits=[("v", vsc[st_i]), ("v", vev.get(st_i - 2))])
                    vev[st_i] = k.sec("vector", lambda v, h=h, tb=tb, p=p:
                                      v.tensor_copy(
                        attnT[:, h * D + tb * 512: h * D + (tb + 1) * 512],
                        ps_a[p][:, :]), waits=[("t", ta[(h, tb)])])
        s_xt0.close()
        tp = tapout("attnT", [128, NT * D], BF16)
        if tp is not None:
            k.sec("gpsimd", lambda g: g.dma_start(out=tp, in_=attnT[:, :]), waits=("v",))

        def boundary_chain(bufs, src_psums, res_src, res_dst,
                           extra_scale=None, extra_bias=None):
            ao, rn, yb = bufs["ao"], bufs["rn"], bufs["yb"]
            k.sec("sync", lambda y: y.dma_start(out=rn[1][:, :], in_=res_src),
                  waits=("v",))
            if extra_scale is None:
                k.sec("vector", lambda v: [
                    v.tensor_copy(ao[:, oc * 512:(oc + 1) * 512], src_psums[oc])
                    for oc in range(4)], waits=("t",))
            else:
                k.sec("vector", lambda v: [
                    (v.tensor_tensor(ao[:, oc * 512:(oc + 1) * 512], src_psums[oc],
                                     extra_scale[:, oc * 512:(oc + 1) * 512],
                                     op=OP.mult),
                     v.tensor_tensor(ao[:, oc * 512:(oc + 1) * 512],
                                     ao[:, oc * 512:(oc + 1) * 512],
                                     extra_bias[:, oc * 512:(oc + 1) * 512],
                                     op=OP.add))
                    for oc in range(4)], waits=("t",))
            norm_cols((), ao, EPSLN, True)
            k.sec("scalar", lambda s: s.activation(
                yb[:, :], ao[:, :], ACT.Identity, bias=c_nmi[:, 0:1], scale=c_inv[:, 0:1]),
                waits=("v",))
            k.sec("vector", lambda v: v.tensor_tensor(
                rn[0][:, :], yb[:, :], rn[1][:, :], op=OP.add), waits=("s", "yd"))
            k.sec("gpsimd", lambda g: g.dma_start(out=res_dst, in_=rn[0][:, :]),
                  waits=("v",))

        def next_norm_apply(bufs, eps2, mean2):
            rn, xh = bufs["rn"], bufs["xh"]
            norm_cols((), rn[0], eps2, mean2)
            k.sec("scalar", lambda s: s.activation(
                xh[:, :], rn[0][:, :], ACT.Identity, bias=c_nmi[:, 0:1],
                scale=c_inv[:, 0:1]), waits=("v",))

        def mk_bufs(st, tag):
            return {
                "ao": sb("ao_" + tag, [128, D], st=st),
                "rn": (sb("rn0_" + tag, [128, D], st=st),
                       sb("rn1_" + tag, [128, D], st=st)),
                "yb": sb("yb_" + tag, [128, D], st=st),
                "xh": sb("xh_" + tag, [128, D], BF16, st=st),
                "pt": [psum("pt_" + tag + str(i), [128, 1024], BF16, st=st)
                       for i in range(2)],
            }

        def out_phase(st, tag, mm_emit, res_src_fn, res_dst_fn,
                      extra_scale=None, extra_bias=None,
                      xt_write=None, xt_dma_fn=None, eps2=EPSLN, mean2=True,
                      prefetch_emit=None, mm_extra_waits=lambda tc: []):
            psb = [psum(tag + "_w%d" % i2, [128, 512], st=st) for i2 in range(4)]
            ptb = [psum(tag + "_pt%d" % i2, [128, 1024], BF16, st=st) for i2 in range(2)]
            ao = sb(tag + "_ao", [128, D], st=st)
            rn1 = [sb(tag + "_rn1%d" % i2, [128, D], st=st) for i2 in range(2)]
            rn0 = [sb(tag + "_rn0%d" % i2, [128, D], st=st) for i2 in range(2)]
            yb = sb(tag + "_yb", [128, D], st=st)
            xh = [sb(tag + "_xh%d" % i2, [128, D], BF16, st=st) for i2 in range(2)]
            ev = {}
            tcnt = {}
            ap2 = {}
            addv = {}
            rd = {}
            gdres = {}
            trc = {}
            xtev = {}

            xtgd = {}

            def emit_transpose(tcx):
                if xt_write is None or tcx < 0:
                    return
                trc[tcx] = k.sec("tensor", lambda t, tcx=tcx: [
                    t.transpose(ptb[ic // 8][:, (ic % 8) * 128:(ic % 8 + 1) * 128],
                                xh[tcx % 2][:, ic * 128:(ic + 1) * 128], ident[:, :])
                    for ic in range(IC)],
                    waits=[("s", ap2[tcx]), ("v", xtev.get(tcx - 1))])
                xtev[tcx] = k.sec("vector", lambda v, tcx=tcx: [
                    xt_write(v, tcx, hb, ptb[hb][:, :]) for hb in range(2)],
                    waits=[("t", trc[tcx]), xtgd.get(tcx - 2)])
                if xt_dma_fn is not None:
                    xtgd[tcx] = dsec("gpsimd", lambda g, tcx=tcx: xt_dma_fn(g, tcx),
                                     waits=[("v", xtev[tcx])])

            for tc in range(NT):
                if prefetch_emit is not None:
                    prefetch_emit(tc, tcnt)
                rd[tc] = dsec("sync", lambda y, tc=tc: y.dma_start(
                    out=rn1[tc % 2][:, :], in_=res_src_fn(tc)),
                    waits=[("v", addv.get(tc - 2))])
                for oc in range(4):
                    tcnt[(tc, oc)] = k.sec(
                        "tensor", lambda t, tc=tc, oc=oc: mm_emit(t, tc, oc, psb[oc]),
                        waits=[("v", ev.get((tc - 1, oc)))] + mm_extra_waits(tc))
                for oc in range(4):
                    if extra_scale is None:
                        ev[(tc, oc)] = k.sec("vector", lambda v, oc=oc: v.tensor_copy(
                            ao[:, oc * 512:(oc + 1) * 512], psb[oc][:, :]),
                            waits=[("t", tcnt[(tc, oc)])])
                    else:
                        ev[(tc, oc)] = k.sec("vector", lambda v, oc=oc: (
                            v.tensor_tensor(ao[:, oc * 512:(oc + 1) * 512],
                                            psb[oc][:, :],
                                            extra_scale[:, oc * 512:(oc + 1) * 512],
                                            op=OP.mult),
                            v.tensor_tensor(ao[:, oc * 512:(oc + 1) * 512],
                                            ao[:, oc * 512:(oc + 1) * 512],
                                            extra_bias[:, oc * 512:(oc + 1) * 512],
                                            op=OP.add)),
                            waits=[("t", tcnt[(tc, oc)])])
                k.sec("vector", lambda v: emit_stats(v, ao, EPSLN, True),
                      waits=("s",))
                sq1 = k.sec("scalar", lambda s: s.activation(c_tmp[:, :], c_tmp[:, :],
                                                             ACT.Sqrt), waits=("v",))
                iv1 = k.sec("vector", lambda v: emit_inv(v, True),
                            waits=[("s", sq1)])
                apl = k.sec("scalar", lambda s: s.activation(
                    yb[:, :], ao[:, :], ACT.Identity, bias=c_nmi[:, 0:1],
                    scale=c_inv[:, 0:1]), waits=[("v", iv1)])
                addv[tc] = k.sec("vector", lambda v, tc=tc: v.tensor_tensor(
                    rn0[tc % 2][:, :], yb[:, :], rn1[tc % 2][:, :], op=OP.add),
                    waits=[("s", apl), rd[tc], gdres.get(tc - 2)])
                gdres[tc] = dsec("gpsimd", lambda g, tc=tc: g.dma_start(
                    out=res_dst_fn(tc), in_=rn0[tc % 2][:, :]),
                    waits=[("v", addv[tc])])
                if xt_write is not None:
                    k.sec("vector", lambda v, tc=tc: emit_stats(
                        v, rn0[tc % 2], eps2, mean2), waits=("s",))
                    sq2 = k.sec("scalar", lambda s: s.activation(
                        c_tmp[:, :], c_tmp[:, :], ACT.Sqrt), waits=("v",))
                    iv2 = k.sec("vector", lambda v: emit_inv(v, mean2),
                                waits=[("s", sq2)])
                    ap2[tc] = k.sec("scalar", lambda s, tc=tc: s.activation(
                        xh[tc % 2][:, :], rn0[tc % 2][:, :], ACT.Identity,
                        bias=c_nmi[:, 0:1], scale=c_inv[:, 0:1]),
                        waits=[("v", iv2)])
                emit_transpose(tc - 1)
            emit_transpose(NT - 1)
            return gdres, xtev

        k.t_drain_mode = "none"
        fence()
        with ExitStack() as st4:
            wo_sb = sb("wo_sb", [128, IC * D], BF16, st=st4)
            xtst = [sb("xtst4%d" % i, [128, D], BF16, st=st4) for i in range(2)]
            gwo = dsec("gpsimd", lambda g: g.dma_start(
                out=wo_sb[:, :].rearrange("p (ec o) -> p ec o", ec=IC),
                in_=woT.rearrange("(ec p) o -> p ec o", p=128)), waits=("v",))
            xt1dma = {}

            def mm4(t, tc, oc, bank):
                for ec in range(IC):
                    t.matmul(bank[:, :],
                             attnT[:, ec * D + tc * 128: ec * D + (tc + 1) * 128],
                             wo_sb[:, ec * D + oc * 512: ec * D + (oc + 1) * 512],
                             start=(ec == 0), stop=(ec == IC - 1))

            def xtw4(v, tcx, hb, ap):
                v.tensor_copy(xtst[tcx % 2][:, hb * 1024:(hb + 1) * 1024], ap)

            out_phase(
                st4, "p4", mm4,
                lambda tc: x_in[tc * 128:(tc + 1) * 128, :],
                lambda tc: R_a[tc * 128:(tc + 1) * 128, :],
                xt_write=xtw4,
                xt_dma_fn=lambda g, tcx: g.dma_start(out=XT1_h[tcx],
                                                     in_=xtst[tcx % 2][:, :]),
                mm_extra_waits=lambda tc: [gwo] if tc == 0 else [])
        s_att.close()
        tp = tapout("r_a", [T, D])
        if tp is not None:
            tap_dram(tp, R_a, T, D, F32)

        fence()
        xt2 = sb("xt2", [128, NT * D], BF16, st=s_xt2)
        with ExitStack() as st5:
            fw_sb = sb("fw_sb", [128, IC * D], BF16, st=st5)
            scl_sb = sb("scl_sb", [128, D], BF16, st=st5)
            bias_sb = sb("bias_sb", [128, D], BF16, st=st5)
            xt1t = [sb("xt1t%d" % i, [128, D], BF16, st=st5) for i in range(2)]
            ps_bc = psum("ps_bc", [128, 512], st=st5)

            gfw = dsec("gpsimd", lambda g: g.dma_start(
                out=fw_sb[:, :].rearrange("p (ic o) -> p ic o", ic=IC),
                in_=f0wT.rearrange("(ic p) o -> p ic o", p=128)), waits=("v", "t"))
            with ExitStack() as st5b:
                srow = sb("srow", [1, D], st=st5b)
                brow = sb("brow", [1, D], st=st5b)
                k.sec("sync", lambda y: (y.dma_start(out=srow[:, :], in_=f0scale),
                                         y.dma_start(out=brow[:, :], in_=f0bias)))
                for row, dst in [(srow, scl_sb), (brow, bias_sb)]:
                    for tb in range(4):
                        k.sec("tensor", lambda t, row=row, tb=tb: t.matmul(
                            ps_bc[:, :], ones_f[0:1, 0:128],
                            row[0:1, tb * 512:(tb + 1) * 512],
                            start=True, stop=True), waits=("yd", "v"))
                        k.sec("vector", lambda v, dst=dst, tb=tb: v.tensor_copy(
                            dst[:, tb * 512:(tb + 1) * 512], ps_bc[:, :]),
                            waits=("t",))
            fence()

            xt1dma = {}
            xt1dma[0] = dsec("sync", lambda y: y.dma_start(
                out=xt1t[0][:, :], in_=XT1_h[0]), waits=("gd",))

            def pre5(tc, tcnt):
                if tc + 1 < NT:
                    xt1dma[tc + 1] = dsec("sync", lambda y, tc=tc: y.dma_start(
                        out=xt1t[(tc + 1) % 2][:, :], in_=XT1_h[tc + 1]),
                        waits=[("t", tcnt.get((tc - 1, 3)))])

            def mm5(t, tc, oc, bank):
                for ic in range(IC):
                    t.matmul(bank[:, :],
                             xt1t[tc % 2][:, ic * 128:(ic + 1) * 128],
                             fw_sb[:, ic * D + oc * 512: ic * D + (oc + 1) * 512],
                             start=(ic == 0), stop=(ic == IC - 1))

            def xtw5(v, tcx, hb, ap):
                v.tensor_copy(xt2[:, tcx * D + hb * 1024: tcx * D + (hb + 1) * 1024],
                              ap)

            out_phase(
                st5, "p5", mm5,
                lambda tc: R_a[tc * 128:(tc + 1) * 128, :],
                lambda tc: R_b[tc * 128:(tc + 1) * 128, :],
                extra_scale=scl_sb, extra_bias=bias_sb,
                xt_write=xtw5, eps2=EPSRMS, mean2=False,
                prefetch_emit=pre5,
                mm_extra_waits=lambda tc: [xt1dma[tc]] +
                    ([gfw] if tc == 0 else []))
        tp = tapout("r_b", [T, D])
        if tp is not None:
            tap_dram(tp, R_b, T, D, F32)
        tp = tapout("xt2", [128, NT * D], BF16)
        if tp is not None:
            k.sec("gpsimd", lambda g: g.dma_start(out=tp, in_=xt2[:, :]), waits=("v",))

        fence()
        cw_sb = sb("cw_sb", [128, NT * 96], st=s_cw, side="right")
        with ExitStack() as st6:
            wsec_ = [sb("wsec%d" % i, [128, IC * 1024], BF16, st=st6)
                     for i in range(2)]
            rms_sb = sb("rms_sb", [128, IC], st=st6)
            qstage = [sb("qstage%d" % i, [128, 512], BF16, st=st6) for i in range(2)]
            vstage = [sb("vstage%d" % i, [128, 1024], BF16, st=st6) for i in range(2)]
            vtst = [sb("vtst%d" % i, [128, 1024], BF16, st=st6) for i in range(2)]
            ps_p = [psum("ps_p6%d" % i, [128, 512], st=st6) for i in range(2)]
            ps_v4 = [psum("ps_v4%d" % i, [128, 512], st=st6) for i in range(4)]
            ptv = [psum("ptv%d" % i, [128, 1024], BF16, st=st6) for i in range(2)]

            k.sec("sync", lambda y: y.dma_start(out=rms_sb[:, :], in_=rmswc))
            ev6 = {}
            gv6 = {}
            gqk = {}
            tv6 = {}
            vtev = {}
            vtgd = {}
            step = 0
            vstep = 0
            lastt = {}
            folds = {}

            def emit_wload(osec):
                wb = wsec_[osec % 2]
                gw2 = dsec("gpsimd", lambda g, osec=osec, wb=wb: g.dma_start(
                    out=wb[:, :].rearrange("p (ic o) -> p ic o", ic=IC),
                    in_=wqkv.rearrange("(ic p) o -> p ic o", p=128)[
                        :, :, osec * 1024:(osec + 1) * 1024]),
                    waits=[("t", lastt.get(osec - 2)), "yd" if osec == 0 else None])
                folds[osec] = k.sec("vector", lambda v, wb=wb: [
                    v.tensor_scalar_mul(wb[:, ic * 1024:(ic + 1) * 1024],
                                        wb[:, ic * 1024:(ic + 1) * 1024],
                                        rms_sb[:, ic:ic + 1]) for ic in range(IC)],
                    waits=[gw2, "yd" if osec == 0 else None])

            emit_wload(0)
            for osec in range(6):
                wsec = wsec_[osec % 2]
                if osec + 1 < 6:
                    emit_wload(osec + 1)
                fold = folds[osec]
                if osec < 4:
                    for oc8 in range(8):
                        oc = osec * 8 + oc8
                        dst_h = Q_h if oc < 16 else KT_h
                        oc_l = oc % 16
                        for tb in range(4):
                            par = step % 2
                            tcnt6 = k.sec("tensor", lambda t, oc8=oc8, tb=tb,
                                          par=par: [
                                t.matmul(ps_p[par][:, :],
                                         wsec[:, ic * 1024 + oc8 * 128:
                                              ic * 1024 + (oc8 + 1) * 128],
                                         xt2[:, :].rearrange(
                                             "p (tc c) -> p tc c", tc=NT)[
                                             :, 4 * tb:4 * tb + 4,
                                             ic * 128:(ic + 1) * 128],
                                         start=(ic == 0), stop=(ic == IC - 1))
                                for ic in range(IC)],
                                waits=[("v", fold), ("v", ev6.get(step - 2))])
                            ev6[step] = k.sec(
                                "vector", lambda v, par=par: v.tensor_copy(
                                    qstage[par][:, :], ps_p[par][:, :]),
                                waits=[("t", tcnt6), gqk.get(step - 2)])
                            gqk[step] = dsec(
                                "gpsimd", lambda g, dst_h=dst_h, oc_l=oc_l,
                                tb=tb, par=par: g.dma_start(
                                    out=dst_h[oc_l][:, tb * 512:(tb + 1) * 512],
                                    in_=qstage[par][:, :]),
                                waits=[("v", ev6[step])])
                            lastt[osec] = tcnt6
                            step += 1
                else:
                    for tc in range(NT):
                        par = vstep % 2
                        tvc = k.sec("tensor", lambda t, tc=tc, par=par: [
                            t.matmul(ps_v4[par * 2 + oh][:, :],
                                     xt2[:, tc * D + ic * 128:
                                         tc * D + (ic + 1) * 128],
                                     wsec[:, ic * 1024 + oh * 512:
                                          ic * 1024 + (oh + 1) * 512],
                                     start=(ic == 0), stop=(ic == IC - 1))
                            for oh in range(2) for ic in range(IC)],
                            waits=[("v", fold), ("v", tv6.get(vstep - 2))])
                        tv6[vstep] = k.sec("vector", lambda v, par=par: [
                            v.tensor_copy(vstage[par][:, oh * 512:(oh + 1) * 512],
                                          ps_v4[par * 2 + oh][:, :])
                            for oh in range(2)],
                            waits=[("t", tvc), vtgd.get(vstep - 2),
                                   gv6.get(vstep - 2)])
                        gv6[vstep] = dsec("gpsimd", lambda g, tc=tc, osec=osec,
                                          par=par: g.dma_start(
                            out=V_h[tc][:, (osec - 4) * 1024:(osec - 3) * 1024],
                            in_=vstage[par][:, :]), waits=[("v", tv6[vstep])])
                        trv = k.sec("tensor", lambda t, par=par: [
                            t.transpose(ptv[par][:, j * 128:(j + 1) * 128],
                                        vstage[par][:, j * 128:(j + 1) * 128],
                                        ident[:, :])
                            for j in range(8)],
                            waits=[("v", tv6[vstep]), ("v", vtev.get(vstep - 2))])
                        vtev[vstep] = k.sec(
                            "vector", lambda v, par=par: v.tensor_copy(
                                vtst[par][:, :], ptv[par][:, :]),
                            waits=[("t", trv), vtgd.get(vstep - 1)])
                        vtgd[vstep] = dsec(
                            "gpsimd", lambda g, tc=tc, osec=osec, par=par:
                            g.dma_start(
                                out=VTF_h[(osec - 4) * 8:(osec - 4) * 8 + 8, :,
                                          tc * 128:(tc + 1) * 128]
                                .transpose([1, 0, 2]),
                                in_=vtst[par][:, :].rearrange(
                                    "p (j t) -> p j t", j=8)),
                            waits=[("v", vtev[vstep])])
                        lastt[osec] = trv
                        vstep += 1
            wcb = sb("wcb", [128, IC * 96], BF16, st=st6)
            bcb = sb("bcb", [1, 96], BF16, st=st6)
            k.sec("gpsimd", lambda g: (
                g.dma_start(out=wcb[:, :].rearrange("p (ic o) -> p ic o", ic=IC),
                            in_=wcomb.rearrange("(ic p) o -> p ic o", p=128)),
                g.dma_start(out=bcb[:, :], in_=bcomb_row)), waits=("t",))
            fw6 = k.sec("vector", lambda v: [
                v.tensor_scalar_mul(wcb[:, ic * 96:(ic + 1) * 96],
                                    wcb[:, ic * 96:(ic + 1) * 96],
                                    rms_sb[:, ic:ic + 1]) for ic in range(IC)],
                waits=("gd", "v", "s"))
            scw = {}
            for tc in range(NT):
                pc = ps_p[tc % 2][:, 0:96]
                tcw = k.sec("tensor", lambda t, tc=tc, pc=pc: (
                    [t.matmul(pc,
                              xt2[:, tc * D + ic * 128: tc * D + (ic + 1) * 128],
                              wcb[:, ic * 96:(ic + 1) * 96],
                              start=(ic == 0), stop=False) for ic in range(IC)],
                    t.matmul(pc, ones_bf[0:1, 0:128], bcb[0:1, :],
                             start=False, stop=True)),
                    waits=[("v", fw6), ("s", scw.get(tc - 2))])
                scw[tc] = k.sec("scalar", lambda s, tc=tc, pc=pc: s.activation(
                    cw_sb[:, tc * 96:(tc + 1) * 96], pc, ACT.Sigmoid),
                    waits=[("t", tcw)])
        s_xt2.close()
        tp = tapout("cw", [128, NT * 96])
        if tp is not None:
            k.sec("gpsimd", lambda g: g.dma_start(out=tp, in_=cw_sb[:, :]),
                  waits=("s",))

        fence()
        with ExitStack() as st7:
            bstg = sb("bstg", [128, D], BF16, st=st7)
            bstg2 = sb("bstg2", [128, D], BF16, st=st7)
            k.sec("gpsimd", lambda g: (
                g.dma_start(out=bstg[:, :].rearrange("p (a t) -> p a t", a=IC),
                            in_=KT_h[:, :, T - 128:T].transpose([1, 0, 2])),
                g.dma_start(out=bstg2[:, :], in_=V_h[NT - 1])),
                waits=("gd", "s"))
            k.sec("gpsimd", lambda g: (
                g.dma_start(out=cin_bd[:, 0:2048], in_=bstg2[:, :]),
                g.dma_start(out=cin_bd[:, 2048:4096], in_=bstg[:, :])),
                waits=("gd",))
            k.sec("gpsimd", lambda g: g.collective_compute(
                "AllGather", OP.bypass, replica_groups=PAIRS,
                ins=[cin_bd.opt()], outs=[cout_bd.opt()]), waits=("gd",))

        fence()
        with ExitStack() as st8:
            qsb_ = [sb("qsb%d" % i, [128, IC * 512], BF16, st=st8) for i in range(2)]
            ksb_ = [sb("ksb%d" % i, [128, IC * 512], BF16, st=st8) for i in range(2)]
            kpb_ = [sb("kpb%d" % i, [128, IC * 128], BF16, st=st8) for i in range(2)]
            vwin_ = [sb("vwin%d" % i, [128, 5 * D], BF16, st=st8) for i in range(2)]
            ec_sb = [sb("ec_sb%d" % i, [128, 512], BF16, st=st8) for i in range(2)]
            ep_sb = [sb("ep_sb%d" % i, [128, 512], BF16, st=st8) for i in range(2)]

            loc4 = [sb("loc4%d" % i, [128, 4 * D], BF16, st=st8) for i in range(2)]
            rc4 = [sb("rc4%d" % i, [128, 4], st=st8) for i in range(2)]
            mlc = sb("mlc", [128, 512], BF16, st=st8)
            mlp = sb("mlp", [128, 512], BF16, st=st8)
            mlp0 = sb("mlp0", [128, 512], BF16, st=st8)
            ps_sc = [psum("ps_sc%d" % i, [128, 512], st=st8) for i in range(2)]
            ps_sp = [psum("ps_sp%d" % i, [128, 512], st=st8) for i in range(2)]
            ps_od = [psum("ps_od%d" % i, [128, 260], st=st8) for i in range(2)]

            mldma = dsec("gpsimd", lambda g: (
                g.dma_start(out=mlc[:, :], in_=mloc_cur_in),
                g.dma_start(out=mlp[:, :], in_=mloc_prev_in),
                g.dma_start(out=mlp0[:, :], in_=mloc_prev0_in)))

            ts8 = {}
            vm1 = {}
            se1 = {}
            vm2 = {}
            se2 = {}
            to8 = {}
            vfin = {}
            gld = {}
            locdma = {}
            def emit_gld(sbk):
                sp = sbk % 2
                gld[sbk] = dsec("gpsimd", lambda g, sbk=sbk, sp=sp: (
                    g.dma_start(out=qsb_[sp][:, :].rearrange("p (a t) -> p a t", a=IC),
                                in_=Q_h[:, :, sbk * 512:(sbk + 1) * 512]
                                .transpose([1, 0, 2])),
                    g.dma_start(out=ksb_[sp][:, :].rearrange("p (a t) -> p a t", a=IC),
                                in_=KT_h[:, :, sbk * 512:(sbk + 1) * 512]
                                .transpose([1, 0, 2])),
                    (g.dma_start(out=kpb_[sp][:, :].rearrange("p (a t) -> p a t", a=IC),
                                 in_=KT_h[:, :, sbk * 512 - 128: sbk * 512]
                                 .transpose([1, 0, 2]))
                     if sbk > 0 else
                     g.dma_start(out=kpb_[sp][:, :], in_=cout_bd[0:128, 2048:4096])),
                    [g.dma_start(out=vwin_[sp][:, (w + 1) * D:(w + 2) * D],
                                 in_=V_h[sbk * 4 + w]) for w in range(4)],
                    (g.dma_start(out=vwin_[sp][:, 0:D], in_=V_h[sbk * 4 - 1])
                     if sbk > 0 else
                     g.dma_start(out=vwin_[sp][:, 0:D], in_=cout_bd[0:128, 0:2048]))),
                    waits=[("t", ts8.get((sbk - 2) * H2 + H2 - 1) if sbk >= 2 else None),
                           ("c", 2),
                           ("v", vfin.get((sbk - 2) * H2 + H2 - 1) if sbk >= 2 else None)])
            emit_gld(0)
            for sbk in range(4):
                if sbk + 1 < 4:
                    emit_gld(sbk + 1)
                qsb = qsb_[sbk % 2]
                ksb = ksb_[sbk % 2]
                kpb = kpb_[sbk % 2]
                vwin = vwin_[sbk % 2]
                for h in range(H2):
                    hp, oc = h % 2, h // 2
                    p = h % 2
                    gs = sbk * H2 + h
                    ts8[gs] = k.sec("tensor", lambda t, hp=hp, oc=oc, p=p, qsb=qsb, ksb=ksb, kpb=kpb: [
                        (t.matmul(ps_sc[p][:, b * 128:(b + 1) * 128],
                                  ksb[hp * 64:hp * 64 + 64,
                                      oc * 512 + b * 128: oc * 512 + (b + 1) * 128],
                                  qsb[hp * 64:hp * 64 + 64,
                                      oc * 512 + b * 128: oc * 512 + (b + 1) * 128],
                                  start=True, stop=True),
                         t.matmul(ps_sp[p][:, b * 128:(b + 1) * 128],
                                  (kpb[hp * 64:hp * 64 + 64, oc * 128:(oc + 1) * 128]
                                   if b == 0 else
                                   ksb[hp * 64:hp * 64 + 64,
                                       oc * 512 + (b - 1) * 128: oc * 512 + b * 128]),
                                  qsb[hp * 64:hp * 64 + 64,
                                      oc * 512 + b * 128: oc * 512 + (b + 1) * 128],
                                  start=True, stop=True)) for b in range(4)],
                        waits=[gld[sbk], ("s", se1.get(gs - 2)),
                               ("s", se2.get(gs - 2))])
                    se1[gs] = k.sec("scalar", lambda s, p=p: s.activation(
                        ec_sb[p][:, :], ps_sc[p][:, :], ACT.Exp, scale=SCALE2),
                        waits=[("t", ts8[gs]), ("v", vm1.get(gs - 2)),
                               ("t", to8.get(gs - 2))])
                    se2[gs] = k.sec("scalar", lambda s, p=p: s.activation(
                        ep_sb[p][:, :], ps_sp[p][:, :], ACT.Exp, scale=SCALE2),
                        waits=[("v", vm1.get(gs - 2))])
                    vm1[gs] = k.sec("vector", lambda v, sbk=sbk, p=p: (
                        v.tensor_tensor(ec_sb[p][:, :], ec_sb[p][:, :], mlc[:, :],
                                        op=OP.mult),
                        v.tensor_tensor(ep_sb[p][:, :], ep_sb[p][:, :],
                                        mlp0[:, :] if sbk == 0 else mlp[:, :],
                                        op=OP.mult)),
                        waits=[("s", se2[gs]), mldma])
                    to8[gs] = k.sec("tensor", lambda t, h=h, p=p, vwin=vwin: [
                        (t.matmul(ps_od[p][:, b * 64:(b + 1) * 64],
                                  ec_sb[p][:, b * 128:(b + 1) * 128],
                                  vwin[:, (b + 1) * D + h * 64:
                                       (b + 1) * D + (h + 1) * 64],
                                  start=True, stop=False),
                         t.matmul(ps_od[p][:, b * 64:(b + 1) * 64],
                                  ep_sb[p][:, b * 128:(b + 1) * 128],
                                  vwin[:, b * D + h * 64: b * D + (h + 1) * 64],
                                  start=False, stop=True),
                         t.matmul(ps_od[p][:, 256 + b: 256 + b + 1],
                                  ec_sb[p][:, b * 128:(b + 1) * 128],
                                  onescol_bf[:, :], start=True, stop=False),
                         t.matmul(ps_od[p][:, 256 + b: 256 + b + 1],
                                  ep_sb[p][:, b * 128:(b + 1) * 128],
                                  onescol_bf[:, :], start=False, stop=True))
                        for b in range(4)],
                        waits=[("v", vm1[gs]), ("v", vfin.get(gs - 2))])
                    vfin[gs] = k.sec("vector", lambda v, sbk=sbk, h=h, p=p: (
                        v.reciprocal(rc4[p][:, :], ps_od[p][:, 256:260]),
                        [v.tensor_scalar_mul(
                            loc4[sbk % 2][:, b * D + h * 64: b * D + (h + 1) * 64],
                            ps_od[p][:, b * 64:(b + 1) * 64], rc4[p][:, b:b + 1])
                         for b in range(4)]),
                        waits=[("t", to8[gs]),
                               locdma.get(sbk - 2)])
                locdma[sbk] = dsec("gpsimd", lambda g, sbk=sbk: [
                    g.dma_start(out=LOC_h[sbk * 4 + b],
                                in_=loc4[sbk % 2][:, b * D:(b + 1) * D])
                    for b in range(4)], waits=[("v", vfin[sbk * H2 + H2 - 1])])

        tp = tapout("loc", [NT * 128, D], BF16)
        if tp is not None:
            tap_dram(tp, LOC_h.rearrange("a p d -> (a p) d"), NT * 128, D, BF16)

        fence()
        brow1 = sb("brow1", [1, D], BF16, st=s_wt)
        st11 = ExitStack()
        with ExitStack() as st11a:
            rwa = sb("rwa", [1, 4 * D], st=st11a)
            rwb = sb("rwb", [1, 4 * D], st=st11a)
            bacc = [sb("bacc%d" % i, [1, D], st=st11a) for i in range(2)]
            bscl11 = sb("bscl11", [1, 1], st=st11a)
            bx0 = sb("bx0", [1, 4], st=st11a)
            k.sec("sync", lambda y: (y.dma_start(out=rwa[:, :], in_=pb_row),
                                     y.dma_start(out=rwb[:, :], in_=nb_row),
                                     y.dma_start(out=bx0[:, :], in_=bexps_row),
                                     y.dma_start(out=bscl11[:, :], in_=bscale1)),
                  waits=("v", "s"))

            def bias_chain(v):
                v.tensor_tensor(rwa[:, :], rwa[:, :], rwb[:, :], op=OP.subtract)
                for n in range(4):
                    if n == 0:
                        v.tensor_scalar_mul(bacc[0][:, :], rwa[0:1, 0:D],
                                            bx0[0:1, 0:1])
                    else:
                        v.scalar_tensor_tensor(bacc[n % 2][:, :],
                                               rwa[0:1, n * D:(n + 1) * D],
                                               bx0[0:1, n:n + 1],
                                               bacc[(n + 1) % 2][:, :],
                                               op0=OP.mult, op1=OP.add)
                v.tensor_scalar(bacc[0][:, :], bacc[1][:, :], -1.0, 1.0,
                                op0=OP.max, op1=OP.min)
                v.tensor_scalar_mul(bacc[0][:, :], bacc[0][:, :], 15.0)
                MAGIC = 8388608.0
                v.tensor_scalar(bacc[1][:, :], bacc[0][:, :], 0.0, MAGIC,
                                op0=OP.max, op1=OP.add)
                v.tensor_scalar(bacc[1][:, :], bacc[1][:, :], 1.0, -MAGIC,
                                op0=OP.mult, op1=OP.add)
                v.tensor_scalar(rwa[0:1, 0:D], bacc[0][:, :], -1.0, 0.0,
                                op0=OP.mult, op1=OP.max)
                v.tensor_scalar(rwa[0:1, 0:D], rwa[0:1, 0:D], 1.0, MAGIC,
                                op0=OP.mult, op1=OP.add)
                v.tensor_scalar(rwa[0:1, 0:D], rwa[0:1, 0:D], 1.0, -MAGIC,
                                op0=OP.mult, op1=OP.add)
                v.tensor_tensor(bacc[1][:, :], bacc[1][:, :], rwa[0:1, 0:D],
                                op=OP.subtract)
                v.tensor_scalar_mul(bacc[1][:, :], bacc[1][:, :], 1.0 / 15.0)
                v.tensor_scalar_mul(bacc[1][:, :], bacc[1][:, :], bscl11[0:1, 0:1])
                v.tensor_copy(brow1[:, :], bacc[1][:, :])
            k.sec("vector", bias_chain, waits=("yd",))
        fence()
        pbuf = sb("pbuf", [128, 4096], st=st11)
        nbuf = sb("nbuf", [128, 4096], st=st11)
        sps = sb("sps", [128, 4096], st=st11)
        sns = sb("sns", [128, 4096], st=st11)
        acc = [sb("acc%d" % i, [128, D], st=st11) for i in range(2)]
        wtst = [sb("wtst%d" % i, [128, D], BF16, st=st11) for i in range(2)]
        enm = sb("enm", [1, 4], st=st11)
        enmb = sb("enmb", [128, 4], st=st11)
        enmbn = sb("enmbn", [128, 4], st=st11)
        srow1 = sb("srow1", [1, 4], st=st11)
        scl11 = sb("scl11", [1, 1], st=st11)
        bx = sb("bx", [1, 4], st=st11)
        ps_e = st11.enter_context(nc.psum_tensor("ps_e", [128, 4], F32))

        k.sec("sync", lambda y: (
            y.dma_start(out=srow1[:, :], in_=maskw_row),
            y.dma_start(out=bx[:, :], in_=exps_row),
            y.dma_start(out=scl11[:, :], in_=scale1)), waits=("v", "s"))
        k.sec("scalar", lambda s: s.activation(enm[:, :], srow1[:, :], ACT.Sigmoid),
              waits=("yd",))
        k.sec("vector", lambda v: (
            v.tensor_tensor(enm[:, :], enm[:, :], bx[:, :], op=OP.mult),
            v.tensor_scalar_mul(enm[:, :], enm[:, :], scl11[0:1, 0:1])),
            waits=("s",))
        k.sec("tensor", lambda t: t.matmul(ps_e[:, :], ones_f[0:1, 0:128],
                                           enm[0:1, :], start=True, stop=True),
              waits=("v",))
        k.sec("vector", lambda v: (
            v.tensor_copy(enmb[:, :], ps_e[:, :]),
            v.tensor_scalar_mul(enmbn[:, :], ps_e[:, :], -1.0)), waits=("t",))

        prep_state = {"sig": {}, "mac": {}, "wdma": {}, "ld": {}}

        def prep_step_fns():
            for ic in range(2):
                for half in range(2):
                    def step(ic=ic, half=half):
                        prep_state["ld"][(ic, half)] = dsec("gpsimd", lambda g: (
                            g.dma_start(out=pbuf[:, :],
                                        in_=pT[ic * 128:(ic + 1) * 128,
                                               half * 4096:(half + 1) * 4096]),
                            g.dma_start(out=nbuf[:, :],
                                        in_=nT[ic * 128:(ic + 1) * 128,
                                               half * 4096:(half + 1) * 4096])),
                            waits=[("s", prep_state["sig"].get(
                                (ic, half - 1),
                                prep_state["sig"].get((ic - 1, 1))))])
                        prep_state["sig"][(ic, half)] = k.sec(
                            "scalar", lambda s: (
                                s.activation(sps[:, :], pbuf[:, :], ACT.Sigmoid),
                                s.activation(sns[:, :], nbuf[:, :], ACT.Sigmoid)),
                            waits=[prep_state["ld"][(ic, half)],
                                   ("v", prep_state["mac"].get(
                                       (ic, half - 1),
                                       prep_state["mac"].get((ic - 1, 1))))])

                        def macs(v, half=half):
                            o = half * 4
                            for nn2 in range(2):
                                n_g = half * 2 + nn2
                                if o == 0:
                                    v.tensor_scalar_mul(acc[0][:, :],
                                                        sps[:, nn2 * D:(nn2 + 1) * D],
                                                        enmb[:, n_g:n_g + 1])
                                else:
                                    v.scalar_tensor_tensor(
                                        acc[o % 2][:, :],
                                        sps[:, nn2 * D:(nn2 + 1) * D],
                                        enmb[:, n_g:n_g + 1], acc[(o + 1) % 2][:, :],
                                        op0=OP.mult, op1=OP.add)
                                o += 1
                                v.scalar_tensor_tensor(
                                    acc[o % 2][:, :],
                                    sns[:, nn2 * D:(nn2 + 1) * D],
                                    enmbn[:, n_g:n_g + 1], acc[(o + 1) % 2][:, :],
                                    op0=OP.mult, op1=OP.add)
                                o += 1
                            if half == 1:
                                v.tensor_copy(wtst[ic % 2][:, :], acc[1][:, :])
                        prep_state["mac"][(ic, half)] = k.sec(
                            "vector", macs,
                            waits=[("s", prep_state["sig"][(ic, half)])] +
                                  ([prep_state["wdma"].get(ic - 2)]
                                   if half == 1 else []))
                    yield step

                def store(ic=ic):
                    prep_state["wdma"][ic] = dsec("gpsimd", lambda g: g.dma_start(
                        out=cin_wt[ic * 128:(ic + 1) * 128, :],
                        in_=wtst[ic % 2][:, :]),
                        waits=[("v", prep_state["mac"][(ic, 1)])])
                yield store

        prep_iter = prep_step_fns()

        fence()
        with ExitStack() as st9:
            src128 = [sb("src128%d" % i, [128, D], BF16, st=st9) for i in range(2)]
            eo = [sb("eo9%d" % i, [128, 1024], BF16, st=st9) for i in range(2)]
            pos_sb = [sb("pos_sb%d" % i, [128, 32], st=st9) for i in range(2)]
            wc_sb9 = [sb("wc_sb9%d" % i, [128, 32 * 64], BF16, st=st9)
                      for i in range(2)]
            ckown = sb("ckown", [32, D], BF16, st=st9)
            cvown = sb("cvown", [32, D], BF16, st=st9)
            cbk = sb("cbk", [1, H2 * 64], BF16, st=st9)
            cbv = sb("cbv", [1, H2 * 64], BF16, st=st9)
            ps_ck = [psum("ps_ck9%d" % i, [32, 64], st=st9) for i in range(2)]

            cbdma = dsec("gpsimd", lambda g: (
                g.dma_start(out=cbk[:, :], in_=kcb_row),
                g.dma_start(out=cbv[:, :], in_=vcb_row)))
            items = [(h, kind) for h in range(H2) for kind in range(2)]

            def ld9(i):
                h, kind = items[i]
                src_h = KT_h if kind == 0 else VTF_h
                pos_in = kpos_eo if kind == 0 else vpos_eo
                wc_in = kc_h if kind == 0 else vc_h
                p = i % 2
                return dsec("gpsimd", lambda g: (
                    g.dma_start(out=src128[p][0:64, :],
                                in_=src_h[h // 2][(h % 2) * 64:(h % 2) * 64 + 64, :]),
                    g.dma_start(out=src128[p][64:128, :],
                                in_=src_h[h // 2][(h % 2) * 64:(h % 2) * 64 + 64, :]),
                    g.dma_start(out=pos_sb[p][:, :], in_=pos_in[h]),
                    g.dma_start(out=wc_sb9[p][:, :].rearrange(
                        "p (a o) -> p a o", a=32), in_=wc_in[h].transpose([1, 0, 2]))),
                    waits=[("v", veo.get(i - 2)), ("t", tck.get(i - 2))])

            veo = {}
            tck = {}
            vck = {}
            gld9 = {0: None}
            gld9[0] = ld9(0)
            for i, (h, kind) in enumerate(items):
                nxt = next(prep_iter, None)
                if nxt is not None:
                    nxt()
                if i + 1 < len(items):
                    gld9[i + 1] = ld9(i + 1)
                p = i % 2
                own = ckown if kind == 0 else cvown
                brow_sb = cbk if kind == 0 else cbv
                veo[i] = k.sec("vector", lambda v, p=p: [
                    v.scalar_tensor_tensor(
                        eo[p][64 * dd:64 * (dd + 1), :],
                        src128[p][64 * dd:64 * (dd + 1), dd::2], 1.0,
                        pos_sb[p][64 * dd:64 * (dd + 1), :].rearrange(
                            "p (one m) -> p one m", one=1)
                        .broadcast_to([64, 32, 32]),
                        op0=OP.mult, op1=OP.add) for dd in range(2)],
                    waits=[gld9[i], ("t", tck.get(i - 2))])
                tck[i] = k.sec("tensor", lambda t, h=h, brow_sb=brow_sb, p=p: (
                    [t.matmul(ps_ck[p][:, :], eo[p][:, tau::32],
                              wc_sb9[p][:, tau * 64:(tau + 1) * 64],
                              start=(tau == 0), stop=False) for tau in range(32)],
                    t.matmul(ps_ck[p][:, :], ones_bf[0:1, 0:32],
                             brow_sb[0:1, h * 64:(h + 1) * 64],
                             start=False, stop=True)),
                    waits=[("v", veo[i]), ("v", vck.get(i - 2)), cbdma])
                vck[i] = k.sec("vector", lambda v, h=h, own=own, p=p: v.tensor_copy(
                    own[:, h * 64:(h + 1) * 64], ps_ck[p][:, :]),
                    waits=[("t", tck[i])])
            k.sec("gpsimd", lambda g: (
                g.dma_start(out=cin_ck[:, 0:2048], in_=ckown[:, :]),
                g.dma_start(out=cin_ck[:, 2048:4096], in_=cvown[:, :])),
                waits=("v",))
            k.sec("gpsimd", lambda g: g.collective_compute(
                "AllGather", OP.bypass, replica_groups=PAIRS,
                ins=[cin_ck.opt()], outs=[cout_ck.opt()]), waits=("gd",))
        for nxt in prep_iter:
            nxt()
        k.sec("gpsimd", lambda g: g.collective_compute(
            "AllGather", OP.bypass, replica_groups=[list(range(8))],
            ins=[cin_wt.opt()], outs=[cout_wt.opt()]),
            waits=[prep_state["wdma"][0], prep_state["wdma"][1]])
        st11.close()
        tp = tapout("ckg", [64, 4096], BF16)
        if tp is not None:
            tap_dram(tp, cout_ck, 64, 4096, BF16)

        k.t_drain_mode = "full"
        fence()
        with ExitStack() as st10:
            wo1_sb = sb("wo1_sb", [128, IC * D], BF16, st=st10)
            ckg = sb("ckg", [128, D], BF16, st=st10)
            cva = sb("cva", [68, D], BF16, st=st10)
            ckT = sb("ckT", [128, 16 * 128], BF16, st=st10)
            mk_sb = sb("mk_sb", [128, 64], BF16, st=st10)
            qtile = [sb("qtile%d" % i, [128, IC * 128], BF16, st=st10)
                     for i in range(2)]
            ltile = [sb("ltile%d" % i, [128, D], BF16, st=st10) for i in range(2)]
            e_c = [sb("e_c10%d" % i, [128, 512], BF16, st=st10) for i in range(2)]

            mcp = sb("mcp", [68, NT * 512], BF16, st=st10)
            s1c = sb("s1c", [128, 32], st=st10)
            s2c = sb("s2c", [128, 32], st=st10)
            rc32 = sb("rc32", [128, 32], st=st10)
            tmps = sb("tmps", [128, 512], st=st10)
            combf = sb("combf", [128, D], BF16, st=st10)
            comb_bf = sb("comb_bf", [128, D], BF16, st=st10)
            combT = sb("combT", [128, D], BF16, st=st10)
            ao10 = sb("ao10", [128, D], st=st10)
            rn110 = [sb("rn110%d" % i, [128, D], st=st10) for i in range(2)]
            rn010 = [sb("rn010%d" % i, [128, D], st=st10) for i in range(2)]
            yb10 = sb("yb10", [128, D], st=st10)
            xh10 = [sb("xh10%d" % i, [128, D], BF16, st=st10) for i in range(2)]
            xtst10 = [sb("xtst10%d" % i, [128, D], BF16, st=st10) for i in range(2)]
            ps_cs = [psum("ps_cs%d" % i, [128, 512], st=st10) for i in range(2)]
            ps_co = [psum("ps_co%d" % i, [128, 512], st=st10) for i in range(4)]
            ps_dn = psum("ps_dn", [128, 256], st=st10)
            pt10 = psum("ptx10", [128, 1024], BF16, st=st10)

            gw10 = dsec("gpsimd", lambda g: (
                g.dma_start(out=wo1_sb[:, :].rearrange("p (ec o) -> p ec o", ec=IC),
                            in_=wout1.rearrange("(ec p) o -> p ec o", p=128)),
                g.dma_start(out=ckg[0:64, :], in_=cout_ck[:, 0:2048]),
                g.dma_start(out=ckg[64:128, :], in_=cout_ck[:, 0:2048]),
                g.dma_start(out=cva[4:68, :], in_=cout_ck[:, 2048:4096]),
                g.dma_start(out=cva[0:4, :], in_=memv_row),
                g.dma_start(out=mk_sb[:, :], in_=memkT),
                g.dma_start(out=mcp[:, :].rearrange("p (a q) -> p a q", a=NT),
                            in_=mcomp_in.transpose([1, 0, 2]))), waits=("c", "v"))
            tck10 = {}
            vck10 = {}
            zck = k.sec("vector", lambda v: v.memset(ckT[:, :], 0.0), waits=("t", "s"))
            for hh in range(16):
                tck10[hh] = k.sec("tensor", lambda t, hh=hh: [
                    t.transpose(pt10[par * 64:(par + 1) * 64, 0:64],
                                ckg[par * 64:par * 64 + 64,
                                    (2 * hh + par) * 64:(2 * hh + par + 1) * 64],
                                ident[par * 64:par * 64 + 64,
                                      par * 64:par * 64 + 64])
                    for par in range(2)],
                    waits=[gw10, ("v", vck10.get(hh - 1))])
                vck10[hh] = k.sec("vector", lambda v, hh=hh: (
                    v.tensor_copy(ckT[0:64, hh * 128 + 4: hh * 128 + 68],
                                  pt10[0:64, 0:64]),
                    v.tensor_copy(ckT[64:128, hh * 128 + 4: hh * 128 + 68],
                                  pt10[64:128, 0:64]),
                    v.tensor_copy(ckT[0:64, hh * 128: hh * 128 + 4],
                                  mk_sb[0:64, hh * 4: hh * 4 + 4]),
                    v.tensor_copy(ckT[64:128, hh * 128: hh * 128 + 4],
                                  mk_sb[64:128, hh * 4: hh * 4 + 4])),
                    waits=[("t", tck10[hh])])

            yq = {}
            tcs = {}
            vmask = {}
            sexp10 = {}
            tout = {}
            vrc = {}
            vcomb = {}
            trc10 = {}
            trev10 = {}
            wout_t = {}
            ev10 = {}
            ap210 = {}
            addv10 = {}
            rd10 = {}
            gdres10 = {}
            xtev10 = {}
            xtgd10 = {}

            yq[0] = dsec("sync", lambda y: (
                y.dma_start(out=qtile[0][:, :].rearrange("p (a t) -> p a t", a=IC),
                            in_=Q_h[:, :, 0:128].transpose([1, 0, 2])),
                y.dma_start(out=ltile[0][:, :], in_=LOC_h[0])),
                waits=[("v", vck10[15])])

            def emit_xt10(tcx):
                if tcx < 0:
                    return
                for half in range(2):
                    tx = k.sec("tensor", lambda t, tcx=tcx, half=half: [
                        t.transpose(pt10[:, j * 128:(j + 1) * 128],
                                    xh10[tcx % 2][:, half * 1024 + j * 128:
                                                  half * 1024 + (j + 1) * 128],
                                    ident[:, :]) for j in range(8)],
                        waits=[("s", ap210[tcx])] +
                              ([("v", xtev10[(tcx, 0)])] if half else
                               [("v", xtev10.get((tcx - 1, 1)))]))
                    xtev10[(tcx, half)] = k.sec(
                        "vector", lambda v, tcx=tcx, half=half: v.tensor_copy(
                            xtst10[tcx % 2][:, half * 1024:(half + 1) * 1024],
                            pt10[:, :]),
                        waits=[("t", tx), xtgd10.get(tcx - 2)])
                xtgd10[tcx] = dsec("gpsimd", lambda g, tcx=tcx: g.dma_start(
                    out=XT3_h[tcx], in_=xtst10[tcx % 2][:, :]),
                    waits=[("v", xtev10[(tcx, 1)])])

            for tc in range(NT):
                if tc + 1 < NT:
                    yq[tc + 1] = dsec("sync", lambda y, tc=tc: (
                        y.dma_start(out=qtile[(tc + 1) % 2][:, :].rearrange(
                            "p (a t) -> p a t", a=IC),
                            in_=Q_h[:, :, (tc + 1) * 128:(tc + 2) * 128]
                            .transpose([1, 0, 2])),
                        y.dma_start(out=ltile[(tc + 1) % 2][:, :],
                                    in_=LOC_h[tc + 1])),
                        waits=[("t", tcs.get((tc - 1, 7))),
                               ("v", vrc.get(tc - 1))])
                for hb4 in range(8):
                    p = hb4 % 2
                    g8 = tc * 8 + hb4
                    tcs[(tc, hb4)] = k.sec("tensor", lambda t, tc=tc, hb4=hb4, p=p: [
                        t.matmul(ps_cs[p][:, i4 * 128:(i4 + 1) * 128],
                                 ckT[((hb4 * 4 + i4) % 2) * 64:
                                     ((hb4 * 4 + i4) % 2) * 64 + 64,
                                     ((hb4 * 4 + i4) // 2) * 128:
                                     ((hb4 * 4 + i4) // 2) * 128 + 128],
                                 qtile[tc % 2][((hb4 * 4 + i4) % 2) * 64:
                                               ((hb4 * 4 + i4) % 2) * 64 + 64,
                                               ((hb4 * 4 + i4) // 2) * 128:
                                               ((hb4 * 4 + i4) // 2) * 128 + 128],
                                 start=True, stop=True) for i4 in range(4)],
                        waits=[yq[tc], ("s", sexp10.get(g8 - 2))])
                    sexp10[g8] = k.sec("scalar", lambda s, p=p: s.activation(
                        e_c[p][:, :], ps_cs[p][:, :], ACT.Exp, scale=SCALE2),
                        waits=[("t", tcs[(tc, hb4)]),
                               ("v", vmask.get(g8 - 2)),
                               ("t", tout.get(g8 - 2))])
                    vmask[g8] = k.sec("vector", lambda v, tc=tc, p=p:
                                             v.tensor_tensor(
                        e_c[p][0:68, :], e_c[p][0:68, :],
                        mcp[:, tc * 512:(tc + 1) * 512], op=OP.mult),
                        waits=[("s", sexp10[g8])])
                    tout[g8] = k.sec("tensor", lambda t, hb4=hb4, p=p: [
                        (t.matmul(ps_co[(hb4 * 4 + i4) // 8][
                            :, ((hb4 * 4 + i4) % 8) * 64:
                            ((hb4 * 4 + i4) % 8 + 1) * 64],
                            e_c[p][0:68, i4 * 128:(i4 + 1) * 128],
                            cva[:, (hb4 * 4 + i4) * 64:(hb4 * 4 + i4 + 1) * 64],
                            start=True, stop=True),
                         t.matmul(ps_dn[:, (hb4 * 4 + i4) * 8:
                                        (hb4 * 4 + i4) * 8 + 8],
                                  e_c[p][0:68, i4 * 128:(i4 + 1) * 128],
                                  onescol8[0:68, :], start=True, stop=True))
                        for i4 in range(4)],
                        waits=[("v", vmask[g8]),
                               ("v", ev10.get((tc - 1, (hb4 * 4) // 8))),
                               ("v", vrc.get(tc - 1))])
                vrc[tc] = k.sec("vector", lambda v, tc=tc: (
                    v.reciprocal(rc32[:, :], ps_dn[:, 0:256:8]),
                    v.tensor_tensor(s1c[:, :],
                                    cw_sb[:, tc * 96 + 0: (tc + 1) * 96: 3],
                                    cw_sb[:, tc * 96 + 1: (tc + 1) * 96: 3],
                                    op=OP.add),
                    v.tensor_tensor(s2c[:, :],
                                    cw_sb[:, tc * 96 + 2: (tc + 1) * 96: 3],
                                    rc32[:, :], op=OP.mult),
                    v.tensor_tensor(combf[:, :], ltile[tc % 2][:, :],
                                    s1c[:, :].rearrange("p (m one) -> p m one",
                                                        one=1)
                                    .broadcast_to([128, 32, 64]), op=OP.mult)),
                    waits=[("t", tout[tc * 8 + 7])])
                for oc in range(4):
                    vcomb[(tc, oc)] = k.sec("vector", lambda v, oc=oc: (
                        v.tensor_tensor(tmps[:, :], ps_co[oc][:, :],
                                        s2c[:, oc * 8:(oc + 1) * 8].rearrange(
                                            "p (m one) -> p m one", one=1)
                                        .broadcast_to([128, 8, 64]), op=OP.mult),
                        v.tensor_tensor(comb_bf[:, oc * 512:(oc + 1) * 512],
                                        combf[:, oc * 512:(oc + 1) * 512],
                                        tmps[:, :], op=OP.add)))
                for half in range(2):
                    trc10[(tc, half)] = k.sec("tensor", lambda t, half=half: [
                        t.transpose(pt10[:, j * 128:(j + 1) * 128],
                                    comb_bf[:, half * 1024 + j * 128:
                                            half * 1024 + (j + 1) * 128],
                                    ident[:, :]) for j in range(8)],
                        waits=[("v", vcomb[(tc, 3)] if half == 0
                                else trev10[(tc, 0)]),
                               ("v", xtev10.get((tc - 2, 1)))])
                    trev10[(tc, half)] = k.sec("vector", lambda v, half=half:
                                               v.tensor_copy(
                        combT[:, half * 1024:(half + 1) * 1024], pt10[:, :]),
                        waits=[("t", trc10[(tc, half)])])
                for oc in range(4):
                    wout_t[(tc, oc)] = k.sec("tensor", lambda t, oc=oc: [
                        t.matmul(ps_co[oc][:, :], combT[:, ec * 128:(ec + 1) * 128],
                                 wo1_sb[:, ec * D + oc * 512: ec * D + (oc + 1) * 512],
                                 start=(ec == 0), stop=(ec == IC - 1))
                        for ec in range(IC)],
                        waits=[("v", trev10[(tc, 1)])])
                rd10[tc] = dsec("sync", lambda y, tc=tc: y.dma_start(
                    out=rn110[tc % 2][:, :], in_=R_b[tc * 128:(tc + 1) * 128, :]),
                    waits=[("v", addv10.get(tc - 2))])
                for oc in range(4):
                    ev10[(tc, oc)] = k.sec("vector", lambda v, oc=oc: v.tensor_copy(
                        ao10[:, oc * 512:(oc + 1) * 512], ps_co[oc][:, :]),
                        waits=[("t", wout_t[(tc, oc)])])
                k.sec("vector", lambda v: emit_stats(v, ao10, EPSLN, True),
                      waits=("s",))
                sq110 = k.sec("scalar", lambda s: s.activation(
                    c_tmp[:, :], c_tmp[:, :], ACT.Sqrt), waits=("v",))
                iv110 = k.sec("vector", lambda v: emit_inv(v, True),
                              waits=[("s", sq110)])
                apl10 = k.sec("scalar", lambda s: s.activation(
                    yb10[:, :], ao10[:, :], ACT.Identity, bias=c_nmi[:, 0:1],
                    scale=c_inv[:, 0:1]), waits=[("v", iv110)])
                addv10[tc] = k.sec("vector", lambda v, tc=tc: v.tensor_tensor(
                    rn010[tc % 2][:, :], yb10[:, :], rn110[tc % 2][:, :], op=OP.add),
                    waits=[("s", apl10), rd10[tc],
                           gdres10.get(tc - 2)])
                gdres10[tc] = dsec("gpsimd", lambda g, tc=tc: g.dma_start(
                    out=R_c[tc * 128:(tc + 1) * 128, :], in_=rn010[tc % 2][:, :]),
                    waits=[("v", addv10[tc])])
                k.sec("vector", lambda v, tc=tc: emit_stats(
                    v, rn010[tc % 2], EPSLN, True), waits=("s",))
                sq210 = k.sec("scalar", lambda s: s.activation(
                    c_tmp[:, :], c_tmp[:, :], ACT.Sqrt), waits=("v",))
                iv210 = k.sec("vector", lambda v: emit_inv(v, True),
                              waits=[("s", sq210)])
                ap210[tc] = k.sec("scalar", lambda s, tc=tc: s.activation(
                    xh10[tc % 2][:, :], rn010[tc % 2][:, :], ACT.Identity,
                    bias=c_nmi[:, 0:1], scale=c_inv[:, 0:1]),
                    waits=[("v", iv210)])
                emit_xt10(tc - 1)
            emit_xt10(NT - 1)
        tp = tapout("r_c", [T, D])
        if tp is not None:
            tap_dram(tp, R_c, T, D, F32)

        k.t_drain_mode = "none"
        fence()
        with ExitStack() as st12:
            wt12 = sb("wt12", [128, IC * D], BF16, st=st12)
            xt3t = [sb("xt3t%d" % i, [128, D], BF16, st=st12) for i in range(2)]
            gwt = dsec("gpsimd", lambda g: g.dma_start(
                out=wt12[:, :].rearrange("p (a o) -> p a o", a=IC),
                in_=cout_wt.rearrange("(a p) o -> p a o", p=128)),
                waits=("v", "t", "c"))
            xt3dma = {}
            xt3dma[0] = dsec("sync", lambda y: y.dma_start(
                out=xt3t[0][:, :], in_=XT3_h[0]), waits=("gd", "v"))

            def pre12(tc, tcnt):
                if tc + 1 < NT:
                    xt3dma[tc + 1] = dsec("sync", lambda y, tc=tc: y.dma_start(
                        out=xt3t[(tc + 1) % 2][:, :], in_=XT3_h[tc + 1]),
                        waits=[("t", tcnt.get((tc - 1, 3)))])

            def mm12(t, tc, oc, bank):
                for ic in range(IC):
                    t.matmul(bank[:, :],
                             xt3t[tc % 2][:, ic * 128:(ic + 1) * 128],
                             wt12[:, ic * D + oc * 512: ic * D + (oc + 1) * 512],
                             start=(ic == 0), stop=False)
                t.matmul(bank[:, :], ones_bf[0:1, 0:128],
                         brow1[0:1, oc * 512:(oc + 1) * 512],
                         start=False, stop=True)

            out_phase(
                st12, "p12", mm12,
                lambda tc: R_c[tc * 128:(tc + 1) * 128, :],
                lambda tc: out_res[tc * 128:(tc + 1) * 128, :],
                prefetch_emit=pre12,
                mm_extra_waits=lambda tc: [xt3dma[tc]] +
                    ([gwt] if tc == 0 else []))

        fins = [(nm2, k.n[nm2]) for nm2 in (k.GD + k.YD)]
        k.sec("gpsimd", lambda g: [g.wait_ge(k.sem[nm2], c * 16)
                                   for nm2, c in fins if c > 0], inc=False)
        s_wt.close()

    return nc, tap_aps


def _prep_shared(i):
    f32 = np.float32
    sh = {}
    sh["wqT"] = np.ascontiguousarray(np.asarray(i["L0_attn_Wq"], f32).T)
    sh["wkT"] = np.ascontiguousarray(np.asarray(i["L0_attn_Wk"], f32).T)
    sh["wvT"] = np.ascontiguousarray(np.asarray(i["L0_attn_Wv"], f32).T)
    sh["woT"] = np.ascontiguousarray(np.asarray(i["L0_attn_Wo"], f32).T)
    sh["bqc"] = np.ascontiguousarray(np.asarray(i["L0_attn_bq"], f32).reshape(IC, 128).T)
    sh["bk_row"] = np.asarray(i["L0_attn_bk"], f32).reshape(1, 512)
    sh["bv_row"] = np.asarray(i["L0_attn_bv"], f32).reshape(1, 512)
    sh["f0wT"] = np.ascontiguousarray(np.asarray(i["L0_ffn_weight"], f32).T)
    sh["f0scale"] = np.asarray(i["L0_ffn_scale_weight"], f32).reshape(1, D)
    sh["f0bias"] = np.asarray(i["L0_ffn_bias"], f32).reshape(1, D)
    sh["wqkv"] = np.ascontiguousarray(np.asarray(i["L1_attn_w_qkv"], f32))
    sh["wcomb"] = np.ascontiguousarray(np.asarray(i["L1_attn_w_comb"], f32))
    sh["bcomb_row"] = np.asarray(i["L1_attn_b_comb"], f32).reshape(1, 96)
    sh["wout1"] = np.ascontiguousarray(np.asarray(i["L1_attn_w_out"], f32))
    sh["rmswc"] = np.ascontiguousarray(np.asarray(i["L1_attn_rms_w"], f32).reshape(IC, 128).T)
    kc = np.asarray(i["L1_attn_kc_w"], f32).transpose(0, 3, 2, 1)
    sh["kc_h"] = np.ascontiguousarray(kc.reshape(H2, 32, 128, 64))
    vc = np.asarray(i["L1_attn_vc_w"], f32).transpose(0, 3, 2, 1)
    sh["vc_h"] = np.ascontiguousarray(vc.reshape(H2, 32, 128, 64))
    sh["kcb_row"] = np.ascontiguousarray(np.asarray(i["L1_attn_kc_b"], f32).reshape(1, H2 * 64))
    sh["vcb_row"] = np.ascontiguousarray(np.asarray(i["L1_attn_vc_b"], f32).reshape(1, H2 * 64))
    kp = np.asarray(i["L1_attn_k_pos"], f32).reshape(H2, 32, 2, 64)
    sh["kpos_eo"] = np.ascontiguousarray(kp.transpose(0, 2, 3, 1).reshape(H2, 128, 32))
    vp = np.asarray(i["L1_attn_v_pos"], f32).reshape(H2, 32, 2, 64)
    sh["vpos_eo"] = np.ascontiguousarray(vp.transpose(0, 2, 3, 1).reshape(H2, 128, 32))
    mk = np.asarray(i["L1_attn_mem_kv"], f32)[0]
    mkT = np.zeros((128, 64), f32)
    for h in range(H2):
        mkT[(h % 2) * 64:(h % 2) * 64 + 64,
            (h // 2) * 4:(h // 2) * 4 + 4] = mk[h].T
    sh["memkT"] = mkT
    sh["memv_row"] = np.ascontiguousarray(
        np.asarray(i["L1_attn_mem_kv"], f32)[1].transpose(1, 0, 2).reshape(4, H2 * 64))
    sh["_pT_full"] = np.asarray(i["L1_ffn_pweight"], f32).transpose(1, 2, 0).reshape(D, 4 * D)
    sh["_nT_full"] = np.asarray(i["L1_ffn_nweight"], f32).transpose(1, 2, 0).reshape(D, 4 * D)
    sh["pb_row"] = np.ascontiguousarray(np.asarray(i["L1_ffn_pbias"], f32).T.reshape(1, 4 * D))
    sh["nb_row"] = np.ascontiguousarray(np.asarray(i["L1_ffn_nbias"], f32).T.reshape(1, 4 * D))
    sh["exps_row"] = np.asarray(i["L1_ffn_exps"], f32).reshape(1, 4)
    sh["bexps_row"] = np.asarray(i["L1_ffn_bexps"], f32).reshape(1, 4)
    sh["maskw_row"] = np.asarray(i["L1_ffn_mask_weight"], f32).reshape(1, 4)
    sh["scale1"] = np.asarray(i["L1_ffn_scale"], f32).reshape(1, 1)
    sh["bscale1"] = np.asarray(i["L1_ffn_biasscale"], f32).reshape(1, 1)
    sh["ident_c"] = np.eye(128, dtype=f32)
    sh["ones_row"] = np.ones((1, 512), f32)
    sh["ones_col"] = np.ones((128, 1), f32)
    sh["ones_col8"] = np.ones((128, 8), f32)
    jj = np.arange(128)[:, None]
    qq = np.arange(128)[None, :]
    cur = np.where(jj <= qq, 1.0, 0.0).astype(f32)
    prv = np.where(qq <= jj, 1.0, 0.0).astype(f32)
    sh["mloc_cur"] = np.ascontiguousarray(np.tile(cur, (1, 4)))
    sh["mloc_prev"] = np.ascontiguousarray(np.tile(prv, (1, 4)))
    return sh


def _prep_core(i, sh, b, h):
    d = dict(sh)
    x = np.asarray(i["L0_attn_hidden_states"], np.float32)[b, h * T:(h + 1) * T, :]
    d["x_in"] = np.ascontiguousarray(x)
    c = b * 2 + h
    d["pT"] = np.ascontiguousarray(d.pop("_pT_full")[c * 256:(c + 1) * 256, :])
    d["nT"] = np.ascontiguousarray(d.pop("_nT_full")[c * 256:(c + 1) * 256, :])
    prv = sh["mloc_prev"][:, 0:128]
    first = np.zeros((128, 128), np.float32) if h == 0 else prv
    d["mloc_prev0"] = np.ascontiguousarray(
        np.concatenate([first, np.tile(prv, (1, 3))], axis=1))
    mc = np.zeros((NT, 68, 128), np.float32)
    for tc in range(NT):
        gq = h * T + tc * 128 + np.arange(128)
        gc = gq // CBS
        jcol = np.arange(68)
        allowed = (jcol[:, None] >= 4) & ((jcol[:, None] - 4) >= gc[None, :])
        mc[tc] = np.where(allowed, 1.0, 0.0)
    d["mcomp"] = np.ascontiguousarray(np.tile(mc, (1, 1, 4)))
    return d


_CACHED = {}


def kernel(**inputs):
    taps = tuple(TAPS)
    key = ("prog", taps)
    if key not in _CACHED:
        _CACHED[key] = build_program(taps)
    nc, tap_aps = _CACHED[key]
    sh = _prep_shared(inputs)
    in_maps = [_prep_core(inputs, sh, c // 2, c % 2) for c in range(8)]
    res = run_bass_kernel_spmd(nc, in_maps, core_ids=list(range(8)))
    out = np.zeros((4, 2 * T, D), np.float32)
    for c in range(8):
        out[c // 2, (c % 2) * T:(c % 2 + 1) * T, :] = res.results[c]["out_res"]
    if taps:
        kernel.tap_results = [{n: r["tap_" + n] for n in taps}
                              for r in res.results]
    return out
